# version 02
# baseline (speedup 1.0000x reference)
import numpy as np
import ml_dtypes
import concourse.bass as bass
import concourse.mybir as mybir
from concourse.bass_utils import run_bass_kernel_spmd
from concourse.aot_env import allow_ici_all_to_all

F32 = mybir.dt.float32
BF16 = mybir.dt.bfloat16
AF = mybir.ActivationFunctionType
OP = mybir.AluOpType
AX = mybir.AxisListType
BF = ml_dtypes.bfloat16

NCORES = 8
NT = 2048
TC = 16
D = 1024
MAGIC = float(np.float32(1.5 * 2 ** 23))
LN_EPS = 1e-5

N_SUBLAYERS = 6
DO_WQ = True
DO_FFN = True
FFN_STAGES = 4
FFN_NGRP = 8
DBG = False



_V_OPS = {"tensor_tensor", "tensor_scalar", "tensor_reduce", "tensor_copy",
          "scalar_tensor_tensor", "reciprocal", "memset", "tensor_tensor_scan",
          "select", "tensor_add", "tensor_mul", "tensor_sub", "tensor_max",
          "iota", "tensor_tensor_reduce", "tensor_relu", "bn_stats", "bn_aggr"}


class VD:

    def __init__(self, v):
        self._v = v

    def __getattr__(self, name):
        f = getattr(self._v, name)
        if name in _V_OPS:
            def wrapped(*a, **k):
                r = f(*a, **k)
                self._v.drain()
                return r
            return wrapped
        return f



_S_OPS = {"activation", "copy", "add", "mul", "sign", "sqrt", "square", "memzero"}


class SD:

    def __init__(self, s):
        self._s = s

    def __getattr__(self, name):
        f = getattr(self._s, name)
        if name in _S_OPS:
            def wrapped(*a, **k):
                r = f(*a, **k)
                self._s.drain()
                return r
            return wrapped
        return f


class Sched:
    def __init__(self):
        self.c = {}

    def get(self, sem):
        return self.c.setdefault(sem.name, 0)

    def add(self, sem, n):
        self.c[sem.name] = self.get(sem) + n
        return self.c[sem.name]


def build_nc():
    nc = bass.Bass("TRN2", target_bir_lowering=False, detect_race_conditions=False)

    x_in = nc.dram_tensor("x", [NT, D], F32, kind="ExternalInput").ap()
    y_out = nc.dram_tensor("y", [NT, D], F32, kind="ExternalOutput").ap()
    ident_f = nc.dram_tensor("ident_f", [128, 128], F32, kind="ExternalInput").ap()

    wg0 = nc.dram_tensor("l0wg", [4096, 1024], F32, kind="ExternalInput").ap()
    wu0 = nc.dram_tensor("l0wu", [4096, 1024], F32, kind="ExternalInput").ap()
    wd0 = nc.dram_tensor("l0wd", [1024, 4096], F32, kind="ExternalInput").ap()

    wgi0 = nc.dram_tensor("wgi0", [1024, 4096], BF16)
    wui0 = nc.dram_tensor("wui0", [1024, 4096], BF16)
    wdi0 = nc.dram_tensor("wdi0", [4096, 1024], BF16)

    l0pw = nc.dram_tensor("l0pw", [1024, 392], F32, kind="ExternalInput").ap()
    l0wg_a = nc.dram_tensor("l0wg_a", [1024, 1024], F32, kind="ExternalInput").ap()
    l0wo_a = nc.dram_tensor("l0wo_a", [1024, 1024], F32, kind="ExternalInput").ap()
    lng_r = nc.dram_tensor("lng_r", [128, 1024], F32, kind="ExternalInput").ap()
    lnb_r = nc.dram_tensor("lnb_r", [128, 1024], F32, kind="ExternalInput").ap()
    esel_d = nc.dram_tensor("esel", [128, 128], BF16, kind="ExternalInput").ap()
    l0pw_b = nc.dram_tensor("l0pw_b", [1024, 392], BF16)
    l0wg_ab = nc.dram_tensor("l0wg_ab", [1024, 1024], BF16)
    l0wo_ab = nc.dram_tensor("l0wo_ab", [1024, 1024], BF16)
    agx_in = nc.dram_tensor("agx_in", [1024, 2048], BF16)
    agx_out = nc.dram_tensor("agx_out", [8192, 2048], BF16)
    scanK = nc.dram_tensor("scanK", [8 * 64, 4096], BF16)
    scanQ = nc.dram_tensor("scanQ", [8 * 64, 4096], BF16)
    scanBK = nc.dram_tensor("scanBK", [8 * 64, 4096], BF16)
    scanBV = nc.dram_tensor("scanBV", [8 * 64, 4096], BF16)
    a2a0_in = nc.dram_tensor("a2a0_in", [1024, 2048], BF16)
    a2a0_out = nc.dram_tensor("a2a0_out", [1024, 2048], BF16)

    wg1 = nc.dram_tensor("l1wg", [2816, 1024], F32, kind="ExternalInput").ap()
    wu1 = nc.dram_tensor("l1wu", [2816, 1024], F32, kind="ExternalInput").ap()
    wd1 = nc.dram_tensor("l1wd", [1024, 2816], F32, kind="ExternalInput").ap()
    wgq1 = nc.dram_tensor("wgq1", [1024, 2816], BF16)
    wuq1 = nc.dram_tensor("wuq1", [1024, 2816], BF16)
    wdq1 = nc.dram_tensor("wdq1", [2816, 1024], BF16)

    wdq_a = nc.dram_tensor("l1wdq", [1024, 4096], F32, kind="ExternalInput").ap()
    wuq_a = nc.dram_tensor("l1wuq", [4096, 1024], F32, kind="ExternalInput").ap()
    wqr_a = nc.dram_tensor("l1wqr", [4096, 512], F32, kind="ExternalInput").ap()
    wkv_a = nc.dram_tensor("l1wkv", [1024, 288], F32, kind="ExternalInput").ap()
    wukv_c = nc.dram_tensor("l1wukv", [256, 256], F32, kind="ExternalInput").ap()
    wo_a = nc.dram_tensor("l1wo", [1024, 1024], F32, kind="ExternalInput").ap()
    scal_t = nc.dram_tensor("scal_t", [128, 512], F32, kind="ExternalInput").ap()
    mask4_d = nc.dram_tensor("mask4", [128, 2048], BF16, kind="ExternalInput").ap()
    wdq_ab = nc.dram_tensor("wdq_ab", [1024, 4096], BF16)
    wuqr_ab = nc.dram_tensor("wuqr_ab", [4096, 1536], BF16)
    wkv_ab = nc.dram_tensor("wkv_ab", [1024, 288], BF16)
    wukv_cb = nc.dram_tensor("wukv_cb", [256, 256], BF16)
    wo_ab = nc.dram_tensor("wo_ab", [1024, 1024], BF16)
    agkv_in = nc.dram_tensor("agkv_in", [2048, 288], BF16)
    agkv_out = nc.dram_tensor("agkv_out", [16384, 288], BF16)
    cqT_scr = nc.dram_tensor("cqT_scr", [4096, 2048], BF16)
    a2aq_in = nc.dram_tensor("a2aq_in", [16384, 192], BF16)
    a2aq_out = nc.dram_tensor("a2aq_out", [16384, 192], BF16)
    a2ay_in = nc.dram_tensor("a2ay_in", [16384, 128], BF16)
    a2ay_out = nc.dram_tensor("a2ay_out", [16384, 128], BF16)

    from contextlib import ExitStack
    with ExitStack() as _es:
        x_res = _es.enter_context(nc.sbuf_tensor("x_res", [128, TC * D], F32))
        sbA = _es.enter_context(nc.sbuf_tensor("sbA", [128, 8192], BF16))
        sbB = _es.enter_context(nc.sbuf_tensor("sbB", [128, 8192], BF16))
        sbC = _es.enter_context(nc.sbuf_tensor("sbC", [128, 8192], BF16))
        sbD = _es.enter_context(nc.sbuf_tensor("sbD", [128, 4096], F32))
        sbE = _es.enter_context(nc.sbuf_tensor("sbE", [128, 2048], F32))
        wb0 = _es.enter_context(nc.sbuf_tensor("wb0", [128, 8192], BF16))
        wb1 = _es.enter_context(nc.sbuf_tensor("wb1", [128, 8192], BF16))
        cols = _es.enter_context(nc.sbuf_tensor("cols", [128, 64], F32))
        idf = _es.enter_context(nc.sbuf_tensor("idf", [128, 128], F32))
        esl = _es.enter_context(nc.sbuf_tensor("esl", [128, 128], BF16))
        lngb = _es.enter_context(nc.sbuf_tensor("lngb", [128, 2048], F32))
        sct = _es.enter_context(nc.sbuf_tensor("sct", [128, 512], F32))
        msk = _es.enter_context(nc.sbuf_tensor("msk", [128, 2048], BF16))
        psA = _es.enter_context(nc.psum_tensor("psA", [128, 3072], F32))
        psB = _es.enter_context(nc.psum_tensor("psB", [128, 1024], F32))
        ld = _es.enter_context(nc.semaphore("ld"))
        gd = _es.enter_context(nc.semaphore("gd"))
        td = _es.enter_context(nc.semaphore("td"))
        vd = _es.enter_context(nc.semaphore("vd"))
        sd = _es.enter_context(nc.semaphore("sd"))
        st = _es.enter_context(nc.semaphore("st"))
        cc = _es.enter_context(nc.semaphore("cc"))
        wl0 = _es.enter_context(nc.semaphore("wl0"))
        wl1 = _es.enter_context(nc.semaphore("wl1"))
        dl0 = _es.enter_context(nc.semaphore("dl0"))
        dl1 = _es.enter_context(nc.semaphore("dl1"))
        dl2 = _es.enter_context(nc.semaphore("dl2"))
        dl3 = _es.enter_context(nc.semaphore("dl3"))
        r0 = _es.enter_context(nc.semaphore("r0"))
        r1 = _es.enter_context(nc.semaphore("r1"))
        block = _es.enter_context(nc.Block())
        S = Sched()
        wbufs = [wb0, wb1]

        def bank(i):
            return psA[:, i * 512:(i + 1) * 512]

        def pb(slot):
            return psB[:, slot * 512:slot * 512 + 128]

        @block.sync
        def _(s):
            for tc in range(TC):
                s.dma_start(out=x_res[:, tc * D:(tc + 1) * D],
                            in_=x_in[tc * 128:(tc + 1) * 128, :]).then_inc(ld, 16)
            s.dma_start(out=idf[:], in_=ident_f[:]).then_inc(ld, 16)
            s.dma_start(out=esl[:], in_=esel_d[:]).then_inc(ld, 16)
            s.dma_start(out=lngb[:, 0:1024], in_=lng_r[:]).then_inc(ld, 16)
            s.dma_start(out=lngb[:, 1024:2048], in_=lnb_r[:]).then_inc(ld, 16)
            s.dma_start(out=sct[:], in_=scal_t[:]).then_inc(ld, 16)
            s.dma_start(out=msk[:], in_=mask4_d[:]).then_inc(ld, 16)
        ld_x = S.add(ld, 16 * (TC + 6))

        def emit_transposes(src_f32, dst_fn, nchunks, extra_tensor_wait=None):
            t_b = S.get(td)
            s_b = S.get(sd)

            @block.tensor
            def _(t):
                if extra_tensor_wait is not None:
                    sem, val = extra_tensor_wait
                    t.wait_ge(sem, val)
                t.wait_ge(sd, s_b)
                for f in range(nchunks):
                    if f >= 2:
                        t.wait_ge(sd, s_b + (f - 1))
                    t.transpose(out=pb(f % 2),
                                in_=src_f32[:, f * 128:(f + 1) * 128],
                                identity=idf[:]).then_inc(td, 1)

            @block.scalar
            def _(sc):
                sc = SD(sc)
                for f in range(nchunks):
                    sc.wait_ge(td, t_b + f + 1)
                    sc.copy(out=dst_fn(f), in_=pb(f % 2)).then_inc(sd, 1)

            S.add(td, nchunks)
            S.add(sd, nchunks)

        def l0_wq(w_nat, w_stash, R, Cin, sw_i, isw_i):
            ntile = R // 128
            nch = Cin // 128
            fb = [wb0[:].bitcast(F32), wb1[:].bitcast(F32)]
            ld_b = S.get(ld)
            vd_b = S.get(vd)

            @block.sync
            def _(s):
                s.wait_ge(vd, vd_b)
                for it in range(ntile):
                    if it >= 1:
                        s.wait_ge(vd, vd_b + it)
                    s.dma_start(out=fb[it % 2][:, 0:Cin],
                                in_=w_nat[it * 128:(it + 1) * 128, :]).then_inc(ld, 16)
            S.add(ld, 16 * ntile)

            @block.vector
            def _(v):
                v = VD(v)
                for it in range(ntile):
                    v.wait_ge(ld, ld_b + 16 * (it + 1))
                    v.tensor_reduce(out=cols[:, 17:18], in_=fb[it % 2][:, 0:Cin],
                                    axis=AX.X, op=OP.add, apply_absolute_value=True)
                    if it == 0:
                        v.tensor_copy(cols[:, 16:17], cols[:, 17:18]
                                      ).then_inc(vd, 1)
                    else:
                        v.tensor_tensor(out=cols[:, 16:17], in0=cols[:, 16:17],
                                        in1=cols[:, 17:18], op=OP.add
                                        ).then_inc(vd, 1)
            S.add(vd, ntile)
            vd_p1 = S.get(vd)

            @block.tensor
            def _(t):
                t.wait_ge(vd, vd_p1)
                t.transpose(out=pb(0)[0:1, :], in_=cols[:, 16:17],
                            identity=idf[:]).then_inc(td, 1)
            td_tr = S.add(td, 1)

            @block.vector
            def _(v):
                v = VD(v)
                v.wait_ge(td, td_tr)
                v.tensor_reduce(out=cols[0:1, 18:19], in_=pb(0)[0:1, :],
                                axis=AX.X, op=OP.add)
                v.tensor_scalar(out=cols[0:1, 18:19], in0=cols[0:1, 18:19],
                                scalar1=1.0 / (R * Cin), scalar2=1e-5,
                                op0=OP.mult, op1=OP.max)
                v.reciprocal(out=cols[0:1, 19:20], in_=cols[0:1, 18:19]
                             ).then_inc(vd, 1)
            vd_sc = S.add(vd, 1)

            @block.gpsimd
            def _(g):
                g.wait_ge(vd, vd_sc)
                g.dma_start(out=cols[:, sw_i:sw_i + 1],
                            in_=cols[0:1, 19:20].unsqueeze(1)
                            .broadcast_to([1, 128, 1])).then_inc(gd, 16)
                g.dma_start(out=cols[:, isw_i:isw_i + 1],
                            in_=cols[0:1, 18:19].unsqueeze(1)
                            .broadcast_to([1, 128, 1])).then_inc(gd, 16)
            gd_sc = S.add(gd, 32)

            ld_b2 = S.get(ld)
            vd_b2 = S.get(vd)

            @block.sync
            def _(s):
                s.wait_ge(vd, vd_p1)
                for it in range(ntile):
                    if it >= 1:
                        s.wait_ge(vd, vd_b2 + it)
                    s.dma_start(out=fb[it % 2][:, 0:Cin],
                                in_=w_nat[it * 128:(it + 1) * 128, :]).then_inc(ld, 16)
            S.add(ld, 16 * ntile)

            for it in range(ntile):
                sd_prev = S.get(sd)

                @block.vector
                def _(v, it=it, st_prev=S.get(st)):
                    v = VD(v)
                    v.wait_ge(st, st_prev)
                    if it == 0:
                        v.wait_ge(gd, gd_sc)
                    v.wait_ge(ld, ld_b2 + 16 * (it + 1))
                    if it >= 1:
                        v.wait_ge(td, S.get(td))
                    v.tensor_scalar(out=sbD[:, 0:Cin], in0=fb[it % 2][:, 0:Cin],
                                    scalar1=cols[:, sw_i:sw_i + 1], scalar2=MAGIC,
                                    op0=OP.mult, op1=OP.add)
                    v.tensor_scalar(out=sbD[:, 0:Cin], in0=sbD[:, 0:Cin],
                                    scalar1=MAGIC, scalar2=1.0,
                                    op0=OP.subtract, op1=OP.min)
                    v.tensor_scalar(out=sbD[:, 0:Cin], in0=sbD[:, 0:Cin],
                                    scalar1=-1.0, scalar2=0.0,
                                    op0=OP.max, op1=OP.bypass).then_inc(vd, 1)
                vq = S.add(vd, 1)

                emit_transposes(sbD[:, 0:Cin],
                                lambda f: sbB[:, (f % 8) * 128:(f % 8 + 1) * 128],
                                nch, extra_tensor_wait=(vd, vq))
                sd_now = S.get(sd)

                @block.gpsimd
                def _(g, it=it, sd_prev=sd_prev):
                    for f in range(nch):
                        g.wait_ge(sd, sd_prev + f + 1)
                        g.dma_start(
                            out=w_stash[f * 128:(f + 1) * 128,
                                        it * 128:(it + 1) * 128],
                            in_=sbB[:, (f % 8) * 128:(f % 8 + 1) * 128]
                            ).then_inc(st, 16)
                S.add(st, 16 * nch)
                if nch > 8:
                    pass

            return

        def l0_wq_big(w_nat, w_stash, R, Cin, sw_i, isw_i):
            ntile = R // 128
            nch = Cin // 128
            fb = [wb0[:].bitcast(F32), wb1[:].bitcast(F32)]
            ld_b = S.get(ld)
            vd_b = S.get(vd)

            @block.sync
            def _(s):
                s.wait_ge(vd, vd_b)
                for it in range(ntile):
                    if it >= 1:
                        s.wait_ge(vd, vd_b + it)
                    s.dma_start(out=fb[it % 2][:, 0:Cin],
                                in_=w_nat[it * 128:(it + 1) * 128, :]).then_inc(ld, 16)
            S.add(ld, 16 * ntile)

            @block.vector
            def _(v):
                v = VD(v)
                for it in range(ntile):
                    v.wait_ge(ld, ld_b + 16 * (it + 1))
                    v.tensor_reduce(out=cols[:, 17:18], in_=fb[it % 2][:, 0:Cin],
                                    axis=AX.X, op=OP.add, apply_absolute_value=True)
                    if it == 0:
                        v.tensor_copy(cols[:, 16:17], cols[:, 17:18]
                                      ).then_inc(vd, 1)
                    else:
                        v.tensor_tensor(out=cols[:, 16:17], in0=cols[:, 16:17],
                                        in1=cols[:, 17:18], op=OP.add
                                        ).then_inc(vd, 1)
            S.add(vd, ntile)
            vd_p1 = S.get(vd)

            @block.tensor
            def _(t):
                t.wait_ge(vd, vd_p1)
                t.transpose(out=pb(0)[0:1, :], in_=cols[:, 16:17],
                            identity=idf[:]).then_inc(td, 1)
            td_tr = S.add(td, 1)

            @block.vector
            def _(v):
                v = VD(v)
                v.wait_ge(td, td_tr)
                v.tensor_reduce(out=cols[0:1, 18:19], in_=pb(0)[0:1, :],
                                axis=AX.X, op=OP.add)
                v.tensor_scalar(out=cols[0:1, 18:19], in0=cols[0:1, 18:19],
                                scalar1=1.0 / (R * Cin), scalar2=1e-5,
                                op0=OP.mult, op1=OP.max)
                v.reciprocal(out=cols[0:1, 19:20], in_=cols[0:1, 18:19]
                             ).then_inc(vd, 1)
            vd_sc = S.add(vd, 1)

            @block.gpsimd
            def _(g):
                g.wait_ge(vd, vd_sc)
                g.dma_start(out=cols[:, sw_i:sw_i + 1],
                            in_=cols[0:1, 19:20].unsqueeze(1)
                            .broadcast_to([1, 128, 1])).then_inc(gd, 16)
                g.dma_start(out=cols[:, isw_i:isw_i + 1],
                            in_=cols[0:1, 18:19].unsqueeze(1)
                            .broadcast_to([1, 128, 1])).then_inc(gd, 16)
            gd_sc = S.add(gd, 32)

            ld_b2 = S.get(ld)
            vd_b2 = S.get(vd)

            @block.sync
            def _(s):
                s.wait_ge(vd, vd_p1)
                for it in range(ntile):
                    if it >= 1:
                        s.wait_ge(vd, vd_b2 + it)
                    s.dma_start(out=fb[it % 2][:, 0:Cin],
                                in_=w_nat[it * 128:(it + 1) * 128, :]).then_inc(ld, 16)
            S.add(ld, 16 * ntile)

            for it in range(ntile):
                @block.vector
                def _(v, it=it, st_prev=S.get(st)):
                    v = VD(v)
                    v.wait_ge(st, st_prev)
                    if it == 0:
                        v.wait_ge(gd, gd_sc)
                    v.wait_ge(ld, ld_b2 + 16 * (it + 1))
                    if it >= 1:
                        v.wait_ge(td, S.get(td))
                    v.tensor_scalar(out=sbD[:, 0:Cin], in0=fb[it % 2][:, 0:Cin],
                                    scalar1=cols[:, sw_i:sw_i + 1], scalar2=MAGIC,
                                    op0=OP.mult, op1=OP.add)
                    v.tensor_scalar(out=sbD[:, 0:Cin], in0=sbD[:, 0:Cin],
                                    scalar1=MAGIC, scalar2=1.0,
                                    op0=OP.subtract, op1=OP.min)
                    v.tensor_scalar(out=sbD[:, 0:Cin], in0=sbD[:, 0:Cin],
                                    scalar1=-1.0, scalar2=0.0,
                                    op0=OP.max, op1=OP.bypass).then_inc(vd, 1)
                vq = S.add(vd, 1)

                for g0 in range(0, nch, 8):
                    gn = min(8, nch - g0)
                    st_before = S.get(st)
                    sd_before = S.get(sd)
                    emit_transposes(sbD[:, g0 * 128:(g0 + gn) * 128],
                                    lambda f: sbB[:, f * 128:(f + 1) * 128],
                                    gn, extra_tensor_wait=(vd, vq) if g0 == 0
                                    else (st, st_before))

                    @block.gpsimd
                    def _(g, sd_before=sd_before, it=it, g0=g0, gn=gn):
                        for f in range(gn):
                            g.wait_ge(sd, sd_before + f + 1)
                            g.dma_start(
                                out=w_stash[(g0 + f) * 128:(g0 + f + 1) * 128,
                                            it * 128:(it + 1) * 128],
                                in_=sbB[:, f * 128:(f + 1) * 128]).then_inc(st, 16)
                    S.add(st, 16 * gn)

        if DO_WQ:
            l0_wq(wg0, wgi0, 4096, 1024, 20, 21)
            l0_wq(wu0, wui0, 4096, 1024, 22, 23)
            l0_wq_big(wd0, wdi0, 1024, 4096, 24, 25)
        wq_st = S.get(st)

        def emit_l0ffn():
            G = 2
            xiTb = sbE[:].bitcast(BF16)
            entry_td = S.get(td)
            for grp in range(min(FFN_NGRP, TC // G)):
                grp_td = S.get(td)
                grp_vd = S.get(vd)

                for j in range(G):
                    tc = grp * G + j
                    vpre = S.get(vd)
                    spre = S.get(sd)
                    hb = sbD[:, 0:4096] if j == 0 else sbC[:].bitcast(F32)[:, 0:4096]
                    xq = hb[:, 0:1024]

                    @block.vector
                    def _(v, tc=tc, j=j, xq=xq, vpre=vpre, spre=spre, grp=grp):
                        v = VD(v)
                        if grp == 0 and j == 0:
                            v.wait_ge(ld, ld_x)
                            v.wait_ge(td, entry_td)
                        xr = x_res[:, tc * D:(tc + 1) * D]
                        v.tensor_reduce(out=cols[:, 0:1], in_=xr, axis=AX.X,
                                        op=OP.add)
                        v.scalar_tensor_tensor(out=xq, in0=xr, scalar=1.0,
                                               in1=xr, op0=OP.mult, op1=OP.mult,
                                               accum_out=cols[:, 1:2])
                        v.tensor_scalar(out=cols[:, 2:3], in0=cols[:, 0:1],
                                        scalar1=1.0 / D, scalar2=0.0,
                                        op0=OP.mult, op1=OP.bypass)
                        v.tensor_scalar(out=cols[:, 3:4], in0=cols[:, 1:2],
                                        scalar1=1.0 / D, scalar2=0.0,
                                        op0=OP.mult, op1=OP.bypass)
                        v.tensor_tensor(out=cols[:, 4:5], in0=cols[:, 2:3],
                                        in1=cols[:, 2:3], op=OP.mult)
                        v.scalar_tensor_tensor(out=cols[:, 4:5], in0=cols[:, 3:4],
                                               scalar=LN_EPS, in1=cols[:, 4:5],
                                               op0=OP.add, op1=OP.subtract
                                               ).then_inc(vd, 1)
                        v.wait_ge(sd, spre + 1)
                        v.reciprocal(out=cols[:, 5:6], in_=cols[:, 4:5])
                        v.scalar_tensor_tensor(out=cols[:, 6:7], in0=cols[:, 2:3],
                                               scalar=-1.0, in1=cols[:, 5:6],
                                               op0=OP.mult, op1=OP.mult)
                        v.tensor_scalar(out=xq, in0=xr,
                                        scalar1=cols[:, 5:6], scalar2=cols[:, 6:7],
                                        op0=OP.mult, op1=OP.add)
                        v.tensor_reduce(out=cols[:, 8:9], in_=xq, axis=AX.X,
                                        op=OP.max, apply_absolute_value=True)
                        v.tensor_scalar(out=cols[:, 8:9], in0=cols[:, 8:9],
                                        scalar1=1e-5, scalar2=0.0,
                                        op0=OP.max, op1=OP.bypass)
                        v.reciprocal(out=cols[:, 9:10], in_=cols[:, 8:9])
                        v.tensor_scalar(out=cols[:, 10 + 2 * j:11 + 2 * j],
                                        in0=cols[:, 9:10], scalar1=127.0,
                                        scalar2=0.0, op0=OP.mult, op1=OP.bypass)
                        v.tensor_scalar(out=cols[:, 11 + 2 * j:12 + 2 * j],
                                        in0=cols[:, 8:9], scalar1=1.0 / 127.0,
                                        scalar2=0.0, op0=OP.mult, op1=OP.bypass)
                        v.tensor_scalar(out=xq, in0=xq,
                                        scalar1=cols[:, 10 + 2 * j:11 + 2 * j],
                                        scalar2=MAGIC, op0=OP.mult, op1=OP.add)
                        v.tensor_scalar(out=xq, in0=xq, scalar1=MAGIC,
                                        scalar2=127.0, op0=OP.subtract, op1=OP.min)
                        v.tensor_scalar(out=xq, in0=xq, scalar1=-128.0,
                                        scalar2=0.0, op0=OP.max, op1=OP.bypass
                                        ).then_inc(vd, 1)
                    vq = S.add(vd, 2)

                    @block.scalar
                    def _(sc, vpre=vpre):
                        sc = SD(sc)
                        sc.wait_ge(vd, vpre + 1)
                        sc.activation(out=cols[:, 4:5], in_=cols[:, 4:5],
                                      func=AF.Sqrt).then_inc(sd, 1)
                    S.add(sd, 1)

                    emit_transposes(
                        xq,
                        lambda f, j=j: xiTb[:, (j * 8 + f) * 128:
                                            (j * 8 + f + 1) * 128],
                        8, extra_tensor_wait=(vd, vq))

                prep_vd = S.get(vd)
                prep_sd = S.get(sd)
                prep_td = S.get(td)
                if FFN_STAGES < 2:
                    continue

                ld_mb = S.get(ld)

                @block.sync
                def _(s, grp_td=grp_td, grp=grp, st_e=S.get(st)):
                    if grp == 0:
                        s.wait_ge(st, max(wq_st, st_e))
                        s.wait_ge(vd, grp_vd)
                        s.wait_ge(td, grp_td)
                    else:
                        s.wait_ge(td, grp_td)
                    for m in range(8):
                        if m >= 2:
                            s.wait_ge(td, prep_td + 32 * (m - 1))
                        wsem = wl0 if m % 2 == 0 else wl1
                        for k in range(8):
                            s.dma_start(
                                out=wbufs[m % 2][:, k * 512:(k + 1) * 512],
                                in_=wgi0[k * 128:(k + 1) * 128,
                                         m * 512:(m + 1) * 512]).then_inc(wsem, 16)
                            s.dma_start(
                                out=wbufs[m % 2][:, 4096 + k * 512:
                                                 4096 + (k + 1) * 512],
                                in_=wui0[k * 128:(k + 1) * 128,
                                         m * 512:(m + 1) * 512]).then_inc(wsem, 16)
                for m in range(8):
                    S.add(wl0 if m % 2 == 0 else wl1, 256)

                @block.tensor
                def _(t, ld_mb=ld_mb, prep_vd=prep_vd, prep_sd=prep_sd,
                      prep_td=prep_td, wl0_b=S.get(wl0) - 4 * 256,
                      wl1_b=S.get(wl1) - 4 * 256):
                    t.wait_ge(vd, prep_vd)
                    t.wait_ge(sd, prep_sd)
                    for m in range(8):
                        if m % 2 == 0:
                            t.wait_ge(wl0, wl0_b + 256 * (m // 2 + 1))
                        else:
                            t.wait_ge(wl1, wl1_b + 256 * (m // 2 + 1))
                        if m >= 1:
                            t.wait_ge(vd, prep_vd + 4 + 3 * m)
                            t.wait_ge(sd, prep_sd + 2 * m)
                        for j in range(G):
                            for k in range(8):
                                t.matmul(bank(j),
                                         xiTb[:, (j * 8 + k) * 128:
                                              (j * 8 + k + 1) * 128],
                                         wbufs[m % 2][:, k * 512:(k + 1) * 512],
                                         start=(k == 0), stop=(k == 7))
                            for k in range(8):
                                mm = t.matmul(bank(2 + j),
                                              xiTb[:, (j * 8 + k) * 128:
                                                   (j * 8 + k + 1) * 128],
                                              wbufs[m % 2][:, 4096 + k * 512:
                                                           4096 + (k + 1) * 512],
                                              start=(k == 0), stop=(k == 7))
                                if j == G - 1 and k == 7:
                                    mm.then_inc(td, 32)
                S.add(td, 32 * 8)

                @block.scalar
                def _(sc, prep_td=prep_td, prep_vd=prep_vd):
                    sc = SD(sc)
                    sc.wait_ge(vd, prep_vd + 4)
                    for m in range(8):
                        sc.wait_ge(td, prep_td + 32 * (m + 1))
                        for j in range(G):
                            sc.activation(
                                out=sbB[:, (j * 8 + m) * 512:
                                        (j * 8 + m + 1) * 512],
                                in_=bank(j), func=AF.Silu,
                                scale=cols[:, 26 + j:27 + j]).then_inc(sd, 1)
                S.add(sd, 16)

                @block.vector
                def _(v, prep_sd=prep_sd):
                    v = VD(v)
                    v.tensor_tensor(out=cols[:, 26:27], in0=cols[:, 11:12],
                                    in1=cols[:, 21:22], op=OP.mult)
                    v.tensor_tensor(out=cols[:, 27:28], in0=cols[:, 13:14],
                                    in1=cols[:, 21:22], op=OP.mult)
                    v.tensor_tensor(out=cols[:, 28:29], in0=cols[:, 11:12],
                                    in1=cols[:, 23:24], op=OP.mult)
                    v.tensor_tensor(out=cols[:, 29:30], in0=cols[:, 13:14],
                                    in1=cols[:, 23:24], op=OP.mult)
                    v.memset(cols[:, 32:34], 0.0).then_inc(vd, 4)
                    for m in range(8):
                        for j in range(G):
                            v.wait_ge(sd, prep_sd + 2 * m + j + 1)
                            hb = (sbD[:, 0:4096] if j == 0
                                  else sbC[:].bitcast(F32)[:, 0:4096])
                            v.scalar_tensor_tensor(
                                out=hb[:, m * 512:(m + 1) * 512],
                                in0=bank(2 + j), scalar=cols[:, 28 + j:29 + j],
                                in1=sbB[:, (j * 8 + m) * 512:
                                        (j * 8 + m + 1) * 512],
                                op0=OP.mult, op1=OP.mult)
                            v.tensor_reduce(out=cols[:, 34:35],
                                            in_=hb[:, m * 512:(m + 1) * 512],
                                            axis=AX.X, op=OP.max,
                                            apply_absolute_value=True)
                            tt = v.tensor_tensor(out=cols[:, 32 + j:33 + j],
                                                 in0=cols[:, 32 + j:33 + j],
                                                 in1=cols[:, 34:35], op=OP.max)
                            if j == G - 1:
                                tt.then_inc(vd, 3)
                S.add(vd, 4 + 24)

                if FFN_STAGES < 3:
                    continue
                for j in range(G):
                    hb = sbD[:, 0:4096] if j == 0 else sbC[:].bitcast(F32)[:, 0:4096]

                    @block.vector
                    def _(v, j=j, hb=hb):
                        v = VD(v)
                        v.tensor_scalar(out=cols[:, 35:36],
                                        in0=cols[:, 32 + j:33 + j],
                                        scalar1=1e-5, scalar2=0.0,
                                        op0=OP.max, op1=OP.bypass)
                        v.reciprocal(out=cols[:, 36:37], in_=cols[:, 35:36])
                        v.tensor_scalar(out=cols[:, 37:38], in0=cols[:, 36:37],
                                        scalar1=127.0, scalar2=0.0,
                                        op0=OP.mult, op1=OP.bypass)
                        v.tensor_scalar(out=cols[:, 30 + j:31 + j],
                                        in0=cols[:, 35:36], scalar1=1.0 / 127.0,
                                        scalar2=0.0, op0=OP.mult, op1=OP.bypass)
                        v.tensor_scalar(out=hb, in0=hb, scalar1=cols[:, 37:38],
                                        scalar2=MAGIC, op0=OP.mult, op1=OP.add)
                        v.tensor_scalar(out=hb, in0=hb, scalar1=MAGIC,
                                        scalar2=127.0, op0=OP.subtract, op1=OP.min)
                        v.tensor_scalar(out=hb, in0=hb, scalar1=-128.0,
                                        scalar2=0.0, op0=OP.max, op1=OP.bypass
                                        ).then_inc(vd, 1)
                    vq = S.add(vd, 1)

                    emit_transposes(
                        hb,
                        lambda f, j=j: sbA[:, (f * G + j) * 128:
                                           (f * G + j + 1) * 128],
                        32, extra_tensor_wait=(vd, vq))

                hq_sd = S.get(sd)
                hq_td = S.get(td)

                if FFN_STAGES < 4:
                    continue
                ld_db = S.get(ld)

                @block.sync
                def _(s, ld_db=ld_db, hq_td=hq_td, prep_td=prep_td):
                    s.wait_ge(td, prep_td + 32 * 8)
                    dsems = [dl0, dl1, dl2, dl3]
                    for k in range(32):
                        if k >= 4:
                            s.wait_ge(td, hq_td + 4 * (k - 3))
                        s.dma_start(out=wbufs[(k % 4) // 2][:, (k % 2) * 1024:
                                                            (k % 2 + 1) * 1024],
                                    in_=wdi0[k * 128:(k + 1) * 128, :]
                                    ).then_inc(dsems[k % 4], 16)
                for k in range(32):
                    S.add([dl0, dl1, dl2, dl3][k % 4], 16)

                @block.tensor
                def _(t, ld_db=ld_db, hq_sd=hq_sd,
                      dbs=[S.get(x) - 8 * 16 for x in [dl0, dl1, dl2, dl3]]):
                    t.wait_ge(sd, hq_sd)
                    dsems = [dl0, dl1, dl2, dl3]
                    for k in range(32):
                        t.wait_ge(dsems[k % 4], dbs[k % 4] + 16 * (k // 4 + 1))
                        wv = wbufs[(k % 4) // 2][:, (k % 2) * 1024:
                                                 (k % 2 + 1) * 1024]
                        for j in range(G):
                            for mo in range(2):
                                mm = t.matmul(bank(j * 2 + mo),
                                              sbA[:, (k * G + j) * 128:
                                                  (k * G + j + 1) * 128],
                                              wv[:, mo * 512:(mo + 1) * 512],
                                              start=(k == 0), stop=(k == 31))
                                if j == G - 1 and mo == 1:
                                    mm.then_inc(td, 4)
                S.add(td, 4 * 32)
                dn_td = S.get(td)

                @block.vector
                def _(v, dn_td=dn_td, grp=grp):
                    v = VD(v)
                    v.wait_ge(td, dn_td)
                    for j in range(G):
                        tc = grp * G + j
                        v.tensor_tensor(out=cols[:, 38:39],
                                        in0=cols[:, 30 + j:31 + j],
                                        in1=cols[:, 25:26], op=OP.mult)
                        v.tensor_scalar(out=cols[:, 38:39], in0=cols[:, 38:39],
                                        scalar1=0.5, scalar2=0.0,
                                        op0=OP.mult, op1=OP.bypass)
                        for mo in range(2):
                            xr = x_res[:, tc * D + mo * 512:
                                       tc * D + (mo + 1) * 512]
                            stt = v.scalar_tensor_tensor(
                                out=xr, in0=bank(j * 2 + mo),
                                scalar=cols[:, 38:39], in1=xr,
                                op0=OP.mult, op1=OP.add)
                            if j == G - 1 and mo == 1:
                                stt.then_inc(vd, 1)
                S.add(vd, 1)



        def cast_stash(w_in, w_out, R, C, oc0=0):
            ntile = R // 128
            fview = wb0[:].bitcast(F32)
            stage = wb1
            ld_b = S.get(ld)
            vd_b = S.get(vd)
            st_b = S.get(st)

            @block.sync
            def _(s):
                s.wait_ge(vd, vd_b)
                s.wait_ge(td, S.get(td))
                for it in range(ntile):
                    if it >= 1:
                        s.wait_ge(vd, vd_b + it)
                    s.dma_start(out=fview[:, 0:C],
                                in_=w_in[it * 128:(it + 1) * 128, :]).then_inc(ld, 16)
            S.add(ld, 16 * ntile)

            @block.vector
            def _(v):
                v = VD(v)
                for it in range(ntile):
                    v.wait_ge(ld, ld_b + 16 * (it + 1))
                    if it >= 1:
                        v.wait_ge(st, st_b + 16 * it)
                    v.tensor_copy(stage[:, 0:C], fview[:, 0:C]).then_inc(vd, 1)
            S.add(vd, ntile)

            @block.gpsimd
            def _(g):
                for it in range(ntile):
                    g.wait_ge(vd, vd_b + it + 1)
                    g.dma_start(out=w_out[it * 128:(it + 1) * 128, oc0:oc0 + C],
                                in_=stage[:, 0:C]).then_inc(st, 16)
            S.add(st, 16 * ntile)

        def emit_l0attn():
            cast_stash(l0pw, l0pw_b, 1024, 392)
            cast_stash(l0wg_a, l0wg_ab, 1024, 1024)
            cast_stash(l0wo_a, l0wo_ab, 1024, 1024)

            entry_td = S.get(td)
            xhat = sbD[:, 0:1024]
            stg = sbA
            for tc in range(TC):
                vpre = S.get(vd)
                spre = S.get(sd)
                st_pre = S.get(st)

                @block.vector
                def _(v, tc=tc, vpre=vpre, spre=spre, st_pre=st_pre):
                    v = VD(v)
                    if tc == 0:
                        v.wait_ge(td, entry_td)
                    else:
                        v.wait_ge(st, st_pre)
                    xr = x_res[:, tc * D:(tc + 1) * D]
                    v.tensor_reduce(out=cols[:, 0:1], in_=xr, axis=AX.X, op=OP.add)
                    v.scalar_tensor_tensor(out=xhat, in0=xr, scalar=1.0,
                                           in1=xr, op0=OP.mult, op1=OP.mult,
                                           accum_out=cols[:, 1:2])
                    v.tensor_scalar(out=cols[:, 2:3], in0=cols[:, 0:1],
                                    scalar1=1.0 / D, scalar2=0.0,
                                    op0=OP.mult, op1=OP.bypass)
                    v.tensor_scalar(out=cols[:, 3:4], in0=cols[:, 1:2],
                                    scalar1=1.0 / D, scalar2=0.0,
                                    op0=OP.mult, op1=OP.bypass)
                    v.tensor_tensor(out=cols[:, 4:5], in0=cols[:, 2:3],
                                    in1=cols[:, 2:3], op=OP.mult)
                    v.scalar_tensor_tensor(out=cols[:, 4:5], in0=cols[:, 3:4],
                                           scalar=LN_EPS, in1=cols[:, 4:5],
                                           op0=OP.add, op1=OP.subtract
                                           ).then_inc(vd, 1)
                    v.wait_ge(sd, spre + 1)
                    v.reciprocal(out=cols[:, 5:6], in_=cols[:, 4:5])
                    v.scalar_tensor_tensor(out=cols[:, 6:7], in0=cols[:, 2:3],
                                           scalar=-1.0, in1=cols[:, 5:6],
                                           op0=OP.mult, op1=OP.mult)
                    v.tensor_scalar(out=xhat, in0=xr,
                                    scalar1=cols[:, 5:6], scalar2=cols[:, 6:7],
                                    op0=OP.mult, op1=OP.add).then_inc(vd, 1)

                @block.scalar
                def _(sc, vpre=vpre):
                    sc = SD(sc)
                    sc.wait_ge(vd, vpre + 1)
                    sc.activation(out=cols[:, 4:5], in_=cols[:, 4:5],
                                  func=AF.Sqrt).then_inc(sd, 1)
                S.add(vd, 2)
                S.add(sd, 1)
                vq = S.get(vd)

                emit_transposes(xhat,
                                lambda f: stg[:, f * 128:(f + 1) * 128],
                                8, extra_tensor_wait=(vd, vq))
                sd_now = S.get(sd)

                @block.gpsimd
                def _(g, tc=tc, sd_now=sd_now):
                    g.wait_ge(sd, sd_now)
                    for f in range(8):
                        g.dma_start(
                            out=agx_in[f * 128:(f + 1) * 128,
                                       tc * 128:(tc + 1) * 128],
                            in_=stg[:, f * 128:(f + 1) * 128]).then_inc(st, 16)
                S.add(st, 16 * 8)

            st_ag = S.get(st)

            @block.gpsimd
            def _(g):
                g.wait_ge(st, st_ag)
                g.collective_compute(
                    "AllGather", OP.bypass,
                    replica_groups=[list(range(NCORES))],
                    ins=[agx_in.ap().opt()],
                    outs=[agx_out.ap().opt()],
                ).then_inc(cc, 1)
            gd_ag = S.add(cc, 1)

            pw = sbC
            ld_pw = S.get(ld)

            @block.sync
            def _(s):
                s.wait_ge(vd, S.get(vd))
                for f in range(8):
                    s.dma_start(out=pw[:, f * 392:(f + 1) * 392],
                                in_=l0pw_b[f * 128:(f + 1) * 128, :]
                                ).then_inc(ld, 16)
            S.add(ld, 16 * 8)
            ld_pw_done = S.get(ld)

            nbt = 4 * 32
            td_proj0 = S.get(td)
            for ib in range(nbt):
                b = ib // 32
                tb = ib % 32
                r = 2 * b + (1 if tb >= 16 else 0)
                col0 = 128 * (tb % 16)
                xs = sbB[:, (ib % 2) * 1024:(ib % 2) * 1024 + 1024]
                ld_pre = S.get(ld)
                td_pre = S.get(td)
                vd_pre = S.get(vd)
                sd_pre = S.get(sd)
                st_pre = S.get(st)

                @block.sync
                def _(s, ib=ib, r=r, col0=col0, xs=xs):
                    if ib == 0:
                        s.wait_ge(cc, gd_ag)
                        s.wait_ge(ld, ld_pw_done)
                    if ib >= 2:
                        s.wait_ge(td, td_proj0 + 5 * (ib - 2) + 1)
                    xsem = dl0 if ib % 2 == 0 else dl1
                    for f in range(8):
                        s.dma_start(out=xs[:, f * 128:(f + 1) * 128],
                                    in_=agx_out[1024 * r + f * 128:
                                                1024 * r + (f + 1) * 128,
                                                col0:col0 + 128]).then_inc(xsem, 16)
                S.add(dl0 if ib % 2 == 0 else dl1, 16 * 8)

                @block.tensor
                def _(t, ib=ib, xs=xs, vd_pre=vd_pre,
                      sd_pre=sd_pre, xv=S.get(dl0 if ib % 2 == 0 else dl1)):
                    t.wait_ge(dl0 if ib % 2 == 0 else dl1, xv)
                    if ib >= 1:
                        t.wait_ge(vd, vd_pre)
                        t.wait_ge(sd, sd_pre)
                    for f in range(8):
                        mm = t.matmul(psA[:, 0:392], xs[:, f * 128:(f + 1) * 128],
                                      pw[:, f * 392:(f + 1) * 392],
                                      start=(f == 0), stop=(f == 7))
                    mm.then_inc(td, 1)
                S.add(td, 1)
                td_mm = S.get(td)

                @block.scalar
                def _(sc, td_mm=td_mm, vd_pre=vd_pre):
                    sc = SD(sc)
                    sc.wait_ge(td, td_mm)
                    for i in range(4):
                        sc.activation(out=sbD[:, 512:576], in_=psA[:, i * 64:(i + 1) * 64],
                                      func=AF.Square, accum_out=cols[:, 8 + i:9 + i])
                    sc.activation(out=cols[:, 12:14], in_=psA[:, 384:386],
                                  func=AF.Sigmoid).then_inc(sd, 1)
                S.add(sd, 1)
                sd_sq = S.get(sd)

                @block.vector
                def _(v, sd_sq=sd_sq):
                    v = VD(v)
                    v.wait_ge(sd, sd_sq)
                    v.tensor_copy(cols[:, 16:20], cols[:, 8:12]).then_inc(vd, 1)
                S.add(vd, 1)
                vd_cp = S.get(vd)

                @block.scalar
                def _(sc, vd_cp=vd_cp):
                    sc = SD(sc)
                    sc.wait_ge(vd, vd_cp)
                    sc.activation(out=cols[:, 16:20], in_=cols[:, 16:20],
                                  func=AF.Sqrt).then_inc(sd, 1)
                S.add(sd, 1)
                sd_sqrt = S.get(sd)

                @block.vector
                def _(v, sd_sqrt=sd_sqrt, ib=ib, st_pre=st_pre):
                    v = VD(v)
                    v.wait_ge(sd, sd_sqrt)
                    if ib >= 1:
                        v.wait_ge(st, st_pre)
                    v.tensor_scalar(out=cols[:, 16:20], in0=cols[:, 16:20],
                                    scalar1=1e-12, scalar2=0.0,
                                    op0=OP.max, op1=OP.bypass)
                    v.reciprocal(out=cols[:, 16:20], in_=cols[:, 16:20])
                    v.tensor_scalar(out=cols[:, 14:16], in0=cols[:, 12:14],
                                    scalar1=-1.0, scalar2=0.0,
                                    op0=OP.mult, op1=OP.bypass)
                    for i in range(4):
                        v.tensor_scalar(out=sbD[:, i * 64:(i + 1) * 64],
                                        in0=psA[:, i * 64:(i + 1) * 64],
                                        scalar1=cols[:, 16 + i:17 + i], scalar2=0.0,
                                        op0=OP.mult, op1=OP.bypass)
                    for hh in range(2):
                        v.tensor_scalar(out=sbD[:, (4 + hh) * 64:(5 + hh) * 64],
                                        in0=sbD[:, (2 + hh) * 64:(3 + hh) * 64],
                                        scalar1=cols[:, 14 + hh:15 + hh], scalar2=0.0,
                                        op0=OP.mult, op1=OP.bypass)
                    for hh in range(2):
                        tsv = v.tensor_scalar(out=sbD[:, (6 + hh) * 64:(7 + hh) * 64],
                                              in0=psA[:, (4 + hh) * 64:(5 + hh) * 64],
                                              scalar1=cols[:, 12 + hh:13 + hh],
                                              scalar2=0.0,
                                              op0=OP.mult, op1=OP.bypass)
                        if hh == 1:
                            tsv.then_inc(vd, 1)
                S.add(vd, 1)
                vq = S.get(vd)

                emit_transposes(sbD[:, 0:512],
                                lambda f: stg[:, 1024 + f * 128:1024 + (f + 1) * 128],
                                4, extra_tensor_wait=(vd, vq))
                sd_tr = S.get(sd)

                @block.gpsimd
                def _(g, b=b, tb=tb, sd_tr=sd_tr):
                    g.wait_ge(sd, sd_tr)
                    for f, dst in enumerate([scanQ, scanK, scanBK, scanBV]):
                        for hh in range(2):
                            p = b * 2 + hh
                            g.dma_start(
                                out=dst[p * 64:(p + 1) * 64,
                                        tb * 128:(tb + 1) * 128],
                                in_=stg[hh * 64:(hh + 1) * 64,
                                        1024 + f * 128:1024 + (f + 1) * 128]
                            ).then_inc(st, 16)
                S.add(st, 16 * 8)

            proj_st = S.get(st)
            proj_vd = S.get(vd)
            proj_sd = S.get(sd)
            proj_td = S.get(td)

            sbDb = sbD[:].bitcast(BF16)
            sbEb = sbE[:].bitcast(BF16)
            zcol = cols[:, 60:61]
            onec = cols[:, 61:62]

            @block.vector
            def _(v):
                v = VD(v)
                v.wait_ge(st, proj_st)
                v.wait_ge(td, proj_td)
                v.wait_ge(sd, proj_sd)
                v.memset(zcol, 0.0)
                v.memset(onec, 1.0).then_inc(vd, 1)
            S.add(vd, 1)
            scan_v0 = S.get(vd)

            for p in range(8):
                ld_p = S.get(ld)

                @block.sync
                def _(s, p=p, ld_p=ld_p):
                    if p == 0:
                        s.wait_ge(vd, scan_v0)
                    else:
                        s.wait_ge(vd, S.get(vd))
                    for half in range(2):
                        s.dma_start(out=sbA[half * 64:(half + 1) * 64, 0:4096],
                                    in_=scanK[p * 64:(p + 1) * 64, :]
                                    ).then_inc(ld, 16)
                        s.dma_start(out=sbA[half * 64:(half + 1) * 64, 4096:8192],
                                    in_=scanQ[p * 64:(p + 1) * 64, :]
                                    ).then_inc(ld, 16)
                    s.dma_start(out=sbB[0:64, 0:4096],
                                in_=scanBK[p * 64:(p + 1) * 64, :]).then_inc(ld, 16)
                    s.dma_start(out=sbB[64:128, 0:4096],
                                in_=scanBV[p * 64:(p + 1) * 64, :]).then_inc(ld, 16)
                S.add(ld, 16 * 6)
                ld_pair = S.get(ld)

                @block.sync
                def _(s, p=p, ld_pair=ld_pair, vd_base=S.get(vd)):
                    s.wait_ge(ld, ld_pair)
                    for tau in range(32):
                        if tau >= 2:
                            s.wait_ge(vd, vd_base + 4 * (tau - 2) + 2)
                        bkr = (sbB[:, 4096:8192] if tau % 2 == 0
                               else wb1[:, 0:4096])
                        bvr = (sbC[:, 0:4096] if tau % 2 == 0
                               else wb1[:, 4096:8192])
                        rsem = r0 if tau % 2 == 0 else r1
                        s.dma_start(out=bkr[0:64, :],
                                    in_=sbB[2 * tau:2 * tau + 1, 0:4096]
                                    .unsqueeze(1).broadcast_to([1, 64, 4096])
                                    ).then_inc(rsem, 16)
                        s.dma_start(out=bkr[64:128, :],
                                    in_=sbB[2 * tau + 1:2 * tau + 2, 0:4096]
                                    .unsqueeze(1).broadcast_to([1, 64, 4096])
                                    ).then_inc(rsem, 16)
                        s.dma_start(out=bvr[0:64, :],
                                    in_=sbB[64 + 2 * tau:65 + 2 * tau, 0:4096]
                                    .unsqueeze(1).broadcast_to([1, 64, 4096])
                                    ).then_inc(rsem, 16)
                        s.dma_start(out=bvr[64:128, :],
                                    in_=sbB[65 + 2 * tau:66 + 2 * tau, 0:4096]
                                    .unsqueeze(1).broadcast_to([1, 64, 4096])
                                    ).then_inc(rsem, 16)
                for tau in range(32):
                    S.add(r0 if tau % 2 == 0 else r1, 64)

                pe_sd = S.get(sd)
                for tau in range(32):
                    vd_pre = S.get(vd)
                    sd_pre = S.get(sd)
                    td_pre = S.get(td)

                    @block.vector
                    def _(v, p=p, tau=tau, vd_pre=vd_pre, sd_pre=sd_pre,
                          ld_pair=ld_pair, td_pre=td_pre,
                          rb0=S.get(r0) - 16 * 64, rb1=S.get(r1) - 16 * 64):
                        v = VD(v)
                        bkr = (sbB[:, 4096:8192] if tau % 2 == 0
                               else wb1[:, 0:4096])
                        bvr = (sbC[:, 0:4096] if tau % 2 == 0
                               else wb1[:, 4096:8192])
                        v.wait_ge(r0 if tau % 2 == 0 else r1,
                                  (rb0 if tau % 2 == 0 else rb1)
                                  + 64 * (tau // 2 + 1))
                        v.tensor_tensor(out=sbD[:, 0:4096], in0=sbA[:, 0:4096],
                                        in1=bkr[:], op=OP.mult).then_inc(vd, 1)
                        v.tensor_tensor(out=sbC[:, 4096:8192], in0=sbA[:, 0:4096],
                                        in1=bvr[:], op=OP.mult).then_inc(vd, 1)
                        v.wait_ge(sd, sd_pre + 1)
                        v.tensor_tensor_scan(out=sbEb[:, 0:4096],
                                             data0=sbD[:, 0:4096],
                                             data1=sbC[:, 4096:8192],
                                             initial=zcol,
                                             op0=OP.mult, op1=OP.add
                                             ).then_inc(vd, 1)
                        v.tensor_tensor(out=wb0[:, 4096:8192],
                                        in0=sbEb[:, 0:4096],
                                        in1=sbA[:, 4096:8192], op=OP.mult
                                        ).then_inc(vd, 1)
                    S.add(vd, 4)

                    @block.scalar
                    def _(sc, vd_pre=vd_pre, td_pre=td_pre):
                        sc = SD(sc)
                        sc.wait_ge(vd, vd_pre + 1)
                        sc.activation(out=sbD[:, 0:4096], in_=sbD[:, 0:4096],
                                      func=AF.Identity, bias=onec,
                                      scale=1.0).then_inc(sd, 1)
                    S.add(sd, 1)

                    @block.tensor
                    def _(t, tau=tau, vd_pre=vd_pre, pe_sd=pe_sd):
                        t.wait_ge(vd, vd_pre + 4)
                        if tau == 0:
                            t.wait_ge(sd, pe_sd)
                        for m in range(8):
                            pbank = (psA[0:64, m * 512:(m + 1) * 512] if m < 6
                                     else psB[0:64, (m - 6) * 512:(m - 5) * 512])
                            mm = t.matmul(pbank,
                                          esl[:, 62 - 2 * tau:126 - 2 * tau],
                                          wb0[:, 4096 + m * 512:4096 + (m + 1) * 512],
                                          start=(tau == 0), stop=(tau == 31))
                            if m == 7:
                                mm.then_inc(td, 1)
                    S.add(td, 1)

                td_pair = S.get(td)

                @block.scalar
                def _(sc, td_pair=td_pair):
                    sc = SD(sc)
                    sc.wait_ge(td, td_pair)
                    for m in range(8):
                        pbank = (psA[0:64, m * 512:(m + 1) * 512] if m < 6
                                 else psB[0:64, (m - 6) * 512:(m - 5) * 512])
                        sc.copy(out=wb0[0:64, m * 512:(m + 1) * 512],
                                in_=pbank).then_inc(sd, 1)
                S.add(sd, 8)

                sd_pair = S.get(sd)
                b = p // 2
                hh = p % 2

                @block.gpsimd
                def _(g, b=b, hh=hh, sd_pair=sd_pair):
                    g.wait_ge(sd, sd_pair)
                    for jh in range(2):
                        j = 2 * b + jh
                        g.dma_start(
                            out=a2a0_in[j * 128 + hh * 64:j * 128 + (hh + 1) * 64, :],
                            in_=wb0[0:64, jh * 2048:(jh + 1) * 2048]
                            ).then_inc(st, 16)
                S.add(st, 16 * 2)


            st_a2a = S.get(st)

            @block.gpsimd
            def _(g):
                g.wait_ge(st, st_a2a)
                with allow_ici_all_to_all():
                    g.collective_compute(
                        "AllToAll", OP.bypass,
                        replica_groups=[list(range(NCORES))],
                        ins=[a2a0_in.ap().opt()],
                        outs=[a2a0_out.ap().opt()],
                    ).then_inc(cc, 1)
            gd_a2a = S.add(cc, 1)

            sbEf = sbE
            for tc in range(TC):
                ld_pre = S.get(ld)
                vd_pre = S.get(vd)
                sd_pre = S.get(sd)
                td_pre = S.get(td)

                @block.sync
                def _(s, tc=tc, ld_pre=ld_pre, vd_pre=vd_pre, td_pre=td_pre):
                    if tc == 0:
                        s.wait_ge(cc, gd_a2a)
                    else:
                        s.wait_ge(vd, vd_pre)
                    for f in range(8):
                        s.dma_start(out=sbB[:, f * 128:(f + 1) * 128],
                                    in_=a2a0_out[f * 128:(f + 1) * 128,
                                                 tc * 128:(tc + 1) * 128]
                                    ).then_inc(ld, 16)
                    for f in range(8):
                        s.dma_start(out=sbB[:, 1024 + f * 128:1024 + (f + 1) * 128],
                                    in_=agx_in[f * 128:(f + 1) * 128,
                                               tc * 128:(tc + 1) * 128]
                                    ).then_inc(ld, 16)
                    for m in range(2):
                        for f in range(8):
                            s.dma_start(
                                out=wb1[:, (m * 8 + f) * 512:(m * 8 + f + 1) * 512],
                                in_=l0wg_ab[f * 128:(f + 1) * 128,
                                            m * 512:(m + 1) * 512]).then_inc(ld, 16)
                S.add(ld, 16 * 32)
                ld_ao = ld_pre + 16 * 8
                ld_xh = ld_pre + 16 * 16
                ld_wg = ld_pre + 16 * 32

                @block.vector
                def _(v, tc=tc, ld_pre=ld_pre, sd_pre=sd_pre, td_pre=td_pre):
                    v = VD(v)
                    v.wait_ge(ld, ld_pre + 16 * 32)
                    v.tensor_copy(sbEf[:, 0:1024], sbB[:, 0:1024]).then_inc(vd, 1)
                S.add(vd, 1)
                vq = S.get(vd)

                emit_transposes(sbEf[:, 0:1024],
                                lambda f: sbD[:, f * 128:(f + 1) * 128],
                                8, extra_tensor_wait=(vd, vq))
                sd_ao = S.get(sd)

                @block.vector
                def _(v, tc=tc, sd_ao=sd_ao, td_pre=td_pre):
                    v = VD(v)
                    v.wait_ge(sd, sd_ao)
                    ao = sbD[:, 0:1024]
                    v.tensor_reduce(out=cols[:, 0:1], in_=ao, axis=AX.X, op=OP.add)
                    v.scalar_tensor_tensor(out=sbD[:, 1024:2048], in0=ao, scalar=1.0,
                                           in1=ao, op0=OP.mult, op1=OP.mult,
                                           accum_out=cols[:, 1:2])
                    v.tensor_scalar(out=cols[:, 2:3], in0=cols[:, 0:1],
                                    scalar1=1.0 / D, scalar2=0.0,
                                    op0=OP.mult, op1=OP.bypass)
                    v.tensor_scalar(out=cols[:, 3:4], in0=cols[:, 1:2],
                                    scalar1=1.0 / D, scalar2=0.0,
                                    op0=OP.mult, op1=OP.bypass)
                    v.tensor_tensor(out=cols[:, 4:5], in0=cols[:, 2:3],
                                    in1=cols[:, 2:3], op=OP.mult)
                    v.scalar_tensor_tensor(out=cols[:, 4:5], in0=cols[:, 3:4],
                                           scalar=LN_EPS, in1=cols[:, 4:5],
                                           op0=OP.add, op1=OP.subtract
                                           ).then_inc(vd, 1)
                S.add(vd, 1)
                vd_ln = S.get(vd)

                @block.scalar
                def _(sc, vd_ln=vd_ln):
                    sc = SD(sc)
                    sc.wait_ge(vd, vd_ln)
                    sc.activation(out=cols[:, 4:5], in_=cols[:, 4:5],
                                  func=AF.Sqrt).then_inc(sd, 1)
                S.add(sd, 1)
                sd_sq2 = S.get(sd)

                @block.vector
                def _(v, sd_sq2=sd_sq2):
                    v = VD(v)
                    v.wait_ge(sd, sd_sq2)
                    ao = sbD[:, 0:1024]
                    v.reciprocal(out=cols[:, 5:6], in_=cols[:, 4:5])
                    v.scalar_tensor_tensor(out=cols[:, 6:7], in0=cols[:, 2:3],
                                           scalar=-1.0, in1=cols[:, 5:6],
                                           op0=OP.mult, op1=OP.mult)
                    v.tensor_scalar(out=ao, in0=ao,
                                    scalar1=cols[:, 5:6], scalar2=cols[:, 6:7],
                                    op0=OP.mult, op1=OP.add)
                    v.tensor_tensor(out=ao, in0=ao, in1=lngb[:, 0:1024],
                                    op=OP.mult)
                    v.tensor_tensor(out=ao, in0=ao, in1=lngb[:, 1024:2048],
                                    op=OP.add).then_inc(vd, 1)
                S.add(vd, 1)
                vd_ao = S.get(vd)

                @block.tensor
                def _(t, ld_pre=ld_pre, vd_pre=vd_pre, sd_pre=sd_pre, tc=tc):
                    t.wait_ge(ld, ld_pre + 16 * 32)
                    if tc >= 1:
                        t.wait_ge(vd, vd_pre)
                    for m in range(2):
                        for f in range(8):
                            mm = t.matmul(psA[:, m * 512:(m + 1) * 512],
                                          sbB[:, 1024 + f * 128:1024 + (f + 1) * 128],
                                          wb1[:, (m * 8 + f) * 512:(m * 8 + f + 1) * 512],
                                          start=(f == 0), stop=(f == 7))
                    mm.then_inc(td, 1)
                S.add(td, 1)
                td_g = S.get(td)

                @block.scalar
                def _(sc, td_g=td_g):
                    sc = SD(sc)
                    sc.wait_ge(td, td_g)
                    for m in range(2):
                        sc.activation(out=sbA[:, 1024 + m * 512:1024 + (m + 1) * 512],
                                      in_=psA[:, m * 512:(m + 1) * 512],
                                      func=AF.Silu).then_inc(sd, 1)
                S.add(sd, 2)
                sd_g = S.get(sd)

                @block.vector
                def _(v, sd_g=sd_g, vd_ao=vd_ao):
                    v = VD(v)
                    v.wait_ge(sd, sd_g)
                    v.tensor_tensor(out=sbD[:, 1024:2048], in0=sbD[:, 0:1024],
                                    in1=sbA[:, 1024:2048], op=OP.mult
                                    ).then_inc(vd, 1)
                S.add(vd, 1)
                vq2 = S.get(vd)

                emit_transposes(sbD[:, 1024:2048],
                                lambda f: sbA[:, f * 128:(f + 1) * 128],
                                8, extra_tensor_wait=(vd, vq2))
                sd_pt = S.get(sd)

                ld_w2 = S.get(ld)

                @block.sync
                def _(s, ld_w2=ld_w2, sd_pt=sd_pt, td_pre=td_pre):
                    s.wait_ge(sd, sd_pt)
                    for m in range(2):
                        for f in range(8):
                            s.dma_start(
                                out=wb1[:, (m * 8 + f) * 512:(m * 8 + f + 1) * 512],
                                in_=l0wo_ab[f * 128:(f + 1) * 128,
                                            m * 512:(m + 1) * 512]).then_inc(ld, 16)
                S.add(ld, 16 * 16)
                ld_wo = S.get(ld)

                @block.tensor
                def _(t, ld_wo=ld_wo, sd_pt=sd_pt):
                    t.wait_ge(ld, ld_wo)
                    t.wait_ge(sd, sd_pt)
                    for m in range(2):
                        for f in range(8):
                            mm = t.matmul(psA[:, 1024 + m * 512:1024 + (m + 1) * 512],
                                          sbA[:, f * 128:(f + 1) * 128],
                                          wb1[:, (m * 8 + f) * 512:(m * 8 + f + 1) * 512],
                                          start=(f == 0), stop=(f == 7))
                    mm.then_inc(td, 1)
                S.add(td, 1)
                td_o = S.get(td)

                @block.vector
                def _(v, td_o=td_o, tc=tc):
                    v = VD(v)
                    v.wait_ge(td, td_o)
                    for m in range(2):
                        xr = x_res[:, tc * D + m * 512:tc * D + (m + 1) * 512]
                        v.tensor_tensor(out=xr, in0=psA[:, 1024 + m * 512:
                                                        1024 + (m + 1) * 512],
                                        in1=xr, op=OP.add).then_inc(vd, 1)
                S.add(vd, 2)

        MAGICF = MAGIC

        def emit_fp4(v, x_ap, dst, C, u, t1, t2, sc0):
            nb = C // 16
            bm = sbE[:, sc0:sc0 + nb]
            sc_ = sbE[:, sc0 + nb:sc0 + 2 * nb]
            f1 = sbE[:, sc0 + 2 * nb:sc0 + 3 * nb]
            sh = sbE[:, sc0 + 3 * nb:sc0 + 4 * nb]
            x3 = x_ap.rearrange("p (b n) -> p b n", n=16)
            v.tensor_reduce(out=bm, in_=x3, axis=AX.X, op=OP.max,
                            apply_absolute_value=True)
            v.tensor_scalar(out=sc_, in0=bm, scalar1=1.0 / 6.0, scalar2=1e-12,
                            op0=OP.mult, op1=OP.max)
            v.reciprocal(out=f1, in_=sc_)
            v.tensor_scalar(out=sh, in0=sc_, scalar1=0.5, scalar2=0.0,
                            op0=OP.mult, op1=OP.bypass)
            v.tensor_scalar(out=f1, in0=f1, scalar1=2.0, scalar2=0.0,
                            op0=OP.mult, op1=OP.bypass)
            ub = u.rearrange("p (b n) -> p b n", n=16)
            v.tensor_tensor(out=ub, in0=x3,
                            in1=f1[:].unsqueeze(2).broadcast_to([128, nb, 16]),
                            op=OP.mult)
            v.tensor_scalar(out=t1, in0=u, scalar1=MAGICF, scalar2=MAGICF,
                            op0=OP.add, op1=OP.subtract)
            v.tensor_scalar(out=dst, in0=t1, scalar1=4.0, scalar2=-4.0,
                            op0=OP.min, op1=OP.max)
            v.tensor_scalar(out=t1, in0=u, scalar1=0.5, scalar2=MAGICF,
                            op0=OP.mult, op1=OP.add)
            v.tensor_scalar(out=t1, in0=t1, scalar1=MAGICF, scalar2=2.0,
                            op0=OP.subtract, op1=OP.mult)
            v.tensor_scalar(out=t2, in0=t1, scalar1=8.0, scalar2=-8.0,
                            op0=OP.min, op1=OP.max)
            v.tensor_tensor(out=dst, in0=dst, in1=t2, op=OP.add)
            v.tensor_scalar(out=t2, in0=t1, scalar1=4.0, scalar2=-4.0,
                            op0=OP.min, op1=OP.max)
            v.tensor_tensor(out=dst, in0=dst, in1=t2, op=OP.subtract)
            v.tensor_scalar(out=t1, in0=u, scalar1=0.25, scalar2=MAGICF,
                            op0=OP.mult, op1=OP.add)
            v.tensor_scalar(out=t1, in0=t1, scalar1=MAGICF, scalar2=4.0,
                            op0=OP.subtract, op1=OP.mult)
            v.tensor_scalar(out=t2, in0=t1, scalar1=12.0, scalar2=-12.0,
                            op0=OP.min, op1=OP.max)
            v.tensor_tensor(out=dst, in0=dst, in1=t2, op=OP.add)
            v.tensor_scalar(out=t2, in0=t1, scalar1=8.0, scalar2=-8.0,
                            op0=OP.min, op1=OP.max)
            v.tensor_tensor(out=dst, in0=dst, in1=t2, op=OP.subtract)
            db = dst.rearrange("p (b n) -> p b n", n=16)
            v.tensor_tensor(out=db, in0=db,
                            in1=sh[:].unsqueeze(2).broadcast_to([128, nb, 16]),
                            op=OP.mult)

        def l1_wq(w_nat, w_stash, R, C):
            ntile = R // 128
            nch = C // 128
            fw = wb0[:].bitcast(F32)
            u_ = sbD[:, 0:C]
            t1_ = wb1[:].bitcast(F32)[:, 0:C]
            t2_ = sbA[:].bitcast(F32)[:, 0:C]
            dst_ = sbC[:].bitcast(F32)[:, 0:C]
            ld_b = S.get(ld)
            vd_b = S.get(vd)

            @block.sync
            def _(s):
                s.wait_ge(vd, vd_b)
                s.wait_ge(td, S.get(td))
                s.wait_ge(st, S.get(st))
                for it in range(ntile):
                    if it >= 1:
                        s.wait_ge(vd, vd_b + it)
                    s.dma_start(out=fw[:, 0:C],
                                in_=w_nat[it * 128:(it + 1) * 128, :]).then_inc(ld, 16)
            S.add(ld, 16 * ntile)

            for it in range(ntile):
                st_pre = S.get(st)

                @block.vector
                def _(v, it=it, st_pre=st_pre):
                    v = VD(v)
                    v.wait_ge(ld, ld_b + 16 * (it + 1))
                    if it >= 1:
                        v.wait_ge(td, S.get(td))
                        v.wait_ge(st, st_pre)
                    emit_fp4(v, fw[:, 0:C], dst_, C, u_, t1_, t2_, 1024)
                    v.memset(cols[:, 40:41], 0.0).then_inc(vd, 1)
                S.add(vd, 1)
                vq = S.get(vd)

                for g0 in range(0, nch, 8):
                    gn = min(8, nch - g0)
                    sd_b2 = S.get(sd)
                    emit_transposes(dst_[:, g0 * 128:(g0 + gn) * 128],
                                    lambda f: sbB[:, f * 128:(f + 1) * 128],
                                    gn, extra_tensor_wait=(vd, vq) if g0 == 0
                                    else (st, S.get(st)))

                    @block.gpsimd
                    def _(g, it=it, g0=g0, gn=gn, sd_b2=sd_b2):
                        for f in range(gn):
                            g.wait_ge(sd, sd_b2 + f + 1)
                            g.dma_start(
                                out=w_stash[(g0 + f) * 128:(g0 + f + 1) * 128,
                                            it * 128:(it + 1) * 128],
                                in_=sbB[:, f * 128:(f + 1) * 128]).then_inc(st, 16)
                    S.add(st, 16 * gn)

        def emit_l1ffn():
            G = 2
            INNER = 2816
            MSL = [512, 512, 512, 512, 512, 256]
            xqTb = sbE[:].bitcast(BF16)
            for grp in range(TC // G):
                grp_td = S.get(td)
                grp_vd = S.get(vd)
                grp_st = S.get(st)

                for j in range(G):
                    tc = grp * G + j
                    vpre = S.get(vd)
                    spre = S.get(sd)
                    xhbuf = sbD[:, 0:1024]
                    xqf = sbC[:].bitcast(F32)[:, 0:1024] if j == 0 else                         sbC[:].bitcast(F32)[:, 1024:2048]

                    @block.vector
                    def _(v, tc=tc, j=j, vpre=vpre, spre=spre, grp=grp,
                          xqf=xqf, xhbuf=xhbuf):
                        v = VD(v)
                        if grp == 0 and j == 0:
                            v.wait_ge(td, grp_td)
                            v.wait_ge(st, grp_st)
                        xr = x_res[:, tc * D:(tc + 1) * D]
                        v.tensor_reduce(out=cols[:, 0:1], in_=xr, axis=AX.X,
                                        op=OP.add)
                        v.scalar_tensor_tensor(out=xhbuf, in0=xr, scalar=1.0,
                                               in1=xr, op0=OP.mult, op1=OP.mult,
                                               accum_out=cols[:, 1:2])
                        v.tensor_scalar(out=cols[:, 2:3], in0=cols[:, 0:1],
                                        scalar1=1.0 / D, scalar2=0.0,
                                        op0=OP.mult, op1=OP.bypass)
                        v.tensor_scalar(out=cols[:, 3:4], in0=cols[:, 1:2],
                                        scalar1=1.0 / D, scalar2=0.0,
                                        op0=OP.mult, op1=OP.bypass)
                        v.tensor_tensor(out=cols[:, 4:5], in0=cols[:, 2:3],
                                        in1=cols[:, 2:3], op=OP.mult)
                        v.scalar_tensor_tensor(out=cols[:, 4:5], in0=cols[:, 3:4],
                                               scalar=LN_EPS, in1=cols[:, 4:5],
                                               op0=OP.add, op1=OP.subtract
                                               ).then_inc(vd, 1)
                        v.wait_ge(sd, spre + 1)
                        v.reciprocal(out=cols[:, 5:6], in_=cols[:, 4:5])
                        v.scalar_tensor_tensor(out=cols[:, 6:7], in0=cols[:, 2:3],
                                               scalar=-1.0, in1=cols[:, 5:6],
                                               op0=OP.mult, op1=OP.mult)
                        v.tensor_scalar(out=xhbuf, in0=xr,
                                        scalar1=cols[:, 5:6], scalar2=cols[:, 6:7],
                                        op0=OP.mult, op1=OP.add)
                        emit_fp4(v, xhbuf, xqf, 1024,
                                 sbD[:, 1024:2048], sbD[:, 2048:3072],
                                 sbD[:, 3072:4096], 1024)
                        v.memset(cols[:, 41:42], 0.0).then_inc(vd, 1)

                    @block.scalar
                    def _(sc, vpre=vpre):
                        sc = SD(sc)
                        sc.wait_ge(vd, vpre + 1)
                        sc.activation(out=cols[:, 4:5], in_=cols[:, 4:5],
                                      func=AF.Sqrt).then_inc(sd, 1)
                    S.add(vd, 2)
                    S.add(sd, 1)
                    vq = S.get(vd)

                    emit_transposes(
                        xqf,
                        lambda f, j=j: xqTb[:, (j * 8 + f) * 128:
                                            (j * 8 + f + 1) * 128],
                        8, extra_tensor_wait=(vd, vq))

                prep_vd = S.get(vd)
                prep_sd = S.get(sd)
                prep_td = S.get(td)

                @block.sync
                def _(s, grp_td=grp_td, grp=grp, prep_td=prep_td,
                      prep_vd=prep_vd):
                    s.wait_ge(td, grp_td)
                    for m in range(6):
                        if m >= 2:
                            s.wait_ge(td, prep_td + (m - 1))
                        wsem = wl0 if m % 2 == 0 else wl1
                        msl = MSL[m]
                        for k in range(8):
                            s.dma_start(
                                out=wbufs[m % 2][:, k * 512:k * 512 + msl],
                                in_=wgq1[k * 128:(k + 1) * 128,
                                         m * 512:m * 512 + msl]).then_inc(wsem, 16)
                            s.dma_start(
                                out=wbufs[m % 2][:, 4096 + k * 512:
                                                 4096 + k * 512 + msl],
                                in_=wuq1[k * 128:(k + 1) * 128,
                                         m * 512:m * 512 + msl]).then_inc(wsem, 16)
                for m in range(6):
                    S.add(wl0 if m % 2 == 0 else wl1, 256)

                @block.tensor
                def _(t, prep_vd=prep_vd, prep_sd=prep_sd, prep_td=prep_td,
                      wlb=[S.get(wl0) - 3 * 256, S.get(wl1) - 3 * 256]):
                    t.wait_ge(vd, prep_vd)
                    t.wait_ge(sd, prep_sd)
                    for m in range(6):
                        t.wait_ge(wl0 if m % 2 == 0 else wl1,
                                  wlb[m % 2] + 256 * (m // 2 + 1))
                        if m >= 1:
                            t.wait_ge(vd, prep_vd + m)
                            t.wait_ge(sd, prep_sd + 2 * m)
                        msl = MSL[m]
                        for j in range(G):
                            for k in range(8):
                                t.matmul(bank(j)[:, 0:msl],
                                         xqTb[:, (j * 8 + k) * 128:
                                              (j * 8 + k + 1) * 128],
                                         wbufs[m % 2][:, k * 512:k * 512 + msl],
                                         start=(k == 0), stop=(k == 7))
                            for k in range(8):
                                mm = t.matmul(bank(2 + j)[:, 0:msl],
                                              xqTb[:, (j * 8 + k) * 128:
                                                   (j * 8 + k + 1) * 128],
                                              wbufs[m % 2][:, 4096 + k * 512:
                                                           4096 + k * 512 + msl],
                                              start=(k == 0), stop=(k == 7))
                                if j == G - 1 and k == 7:
                                    mm.then_inc(td, 1)
                S.add(td, 6)

                @block.scalar
                def _(sc, prep_td=prep_td):
                    sc = SD(sc)
                    for m in range(6):
                        sc.wait_ge(td, prep_td + (m + 1))
                        msl = MSL[m]
                        for j in range(G):
                            sc.activation(
                                out=sbB[:, (j * 6 + m) * 512:
                                        (j * 6 + m) * 512 + msl],
                                in_=bank(j)[:, 0:msl],
                                func=AF.Silu).then_inc(sd, 1)
                S.add(sd, 12)

                @block.vector
                def _(v, prep_sd=prep_sd):
                    v = VD(v)
                    for m in range(6):
                        msl = MSL[m]
                        for j in range(G):
                            v.wait_ge(sd, prep_sd + 2 * m + j + 1)
                            hb = (sbD[:, 0:2816] if j == 0
                                  else sbC[:].bitcast(F32)[:, 0:2816])
                            tt = v.tensor_tensor(
                                out=hb[:, m * 512:m * 512 + msl],
                                in0=bank(2 + j)[:, 0:msl],
                                in1=sbB[:, (j * 6 + m) * 512:
                                        (j * 6 + m) * 512 + msl],
                                op=OP.mult)
                            if j == G - 1:
                                tt.then_inc(vd, 1)
                S.add(vd, 6)

                for j in range(G):
                    hb = sbD[:, 0:2816] if j == 0 else sbC[:].bitcast(F32)[:, 0:2816]
                    hq = wb0[:].bitcast(F32)[:, 0:2816]
                    t1j = (sbA[:].bitcast(F32)[:, 0:2816] if j == 0
                           else sbD[:, 0:2816])
                    td_now = S.get(td)

                    @block.vector
                    def _(v, j=j, hb=hb, hq=hq, t1j=t1j, td_now=td_now):
                        v = VD(v)
                        if j == 1:
                            v.wait_ge(td, td_now)
                        emit_fp4(v, hb, hq, 2816,
                                 wb1[:].bitcast(F32)[:, 0:2816],
                                 t1j,
                                 sbB[:].bitcast(F32)[:, 0:2816], 1024)
                        v.memset(cols[:, 42:43], 0.0).then_inc(vd, 1)
                    S.add(vd, 1)
                    vq = S.get(vd)

                    emit_transposes(
                        hq,
                        lambda f, j=j: sbA[:, (f * G + j) * 128:
                                           (f * G + j + 1) * 128],
                        22, extra_tensor_wait=(vd, vq))

                hq_sd = S.get(sd)
                hq_td = S.get(td)

                @block.sync
                def _(s, hq_td=hq_td, prep_td=prep_td):
                    s.wait_ge(td, hq_td)
                    dsems = [dl0, dl1, dl2, dl3]
                    for k in range(22):
                        if k >= 4:
                            s.wait_ge(td, hq_td + (k - 3))
                        s.dma_start(out=wbufs[(k % 4) // 2][:, (k % 2) * 1024:
                                                            (k % 2 + 1) * 1024],
                                    in_=wdq1[k * 128:(k + 1) * 128, :]
                                    ).then_inc(dsems[k % 4], 16)
                for k in range(22):
                    S.add([dl0, dl1, dl2, dl3][k % 4], 16)

                @block.tensor
                def _(t, hq_sd=hq_sd,
                      dbs=[S.get(x) for x in [dl0, dl1, dl2, dl3]],
                      dcnt=[(22 + 3 - i) // 4 for i in range(4)]):
                    t.wait_ge(sd, hq_sd)
                    dsems = [dl0, dl1, dl2, dl3]
                    for k in range(22):
                        t.wait_ge(dsems[k % 4],
                                  dbs[k % 4] - 16 * dcnt[k % 4] + 16 * (k // 4 + 1))
                        wv = wbufs[(k % 4) // 2][:, (k % 2) * 1024:
                                                 (k % 2 + 1) * 1024]
                        for j in range(G):
                            for mo in range(2):
                                mm = t.matmul(bank(j * 2 + mo),
                                              sbA[:, (k * G + j) * 128:
                                                  (k * G + j + 1) * 128],
                                              wv[:, mo * 512:(mo + 1) * 512],
                                              start=(k == 0), stop=(k == 21))
                                if j == G - 1 and mo == 1:
                                    mm.then_inc(td, 1)
                S.add(td, 22)
                dn_td = S.get(td)

                @block.vector
                def _(v, dn_td=dn_td, grp=grp):
                    v = VD(v)
                    v.wait_ge(td, dn_td)
                    for j in range(G):
                        tc = grp * G + j
                        for mo in range(2):
                            xr = x_res[:, tc * D + mo * 512:
                                       tc * D + (mo + 1) * 512]
                            stt = v.scalar_tensor_tensor(
                                out=xr, in0=bank(j * 2 + mo),
                                scalar=0.5, in1=xr,
                                op0=OP.mult, op1=OP.add)
                            if j == G - 1 and mo == 1:
                                stt.then_inc(vd, 1)
                S.add(vd, 1)


        def emit_l1attn():
            sbDb_ = sbD[:].bitcast(BF16)
            sbDb2 = [sbDb_[:, 0:512], sbDb_[:, 512:1024]]
            sbEb_ = sbE[:].bitcast(BF16)
            sbEb2 = [sbEb_[:, 2048:2336], sbEb_[:, 2336:2624],
                     sbEb_[:, 2624:2880]]

            def pya(i):
                return psA[:, 1024 + i * 512:1024 + i * 512 + 65]

            cast_stash(wdq_a, wdq_ab, 1024, 4096)
            cast_stash(wuq_a, wuqr_ab, 4096, 1024, oc0=0)
            cast_stash(wqr_a, wuqr_ab, 4096, 512, oc0=1024)
            cast_stash(wkv_a, wkv_ab, 1024, 288)
            cast_stash(wukv_c, wukv_cb, 256, 256)
            cast_stash(wo_a, wo_ab, 1024, 1024)

            entry_td = S.get(td)
            for tc in range(TC):
                vpre = S.get(vd)
                spre = S.get(sd)
                st_pre = S.get(st)
                xhat = sbD[:, 0:1024]

                @block.vector
                def _(v, tc=tc, vpre=vpre, spre=spre, st_pre=st_pre):
                    v = VD(v)
                    if tc == 0:
                        v.wait_ge(td, entry_td)
                        v.wait_ge(st, st_pre)
                    xr = x_res[:, tc * D:(tc + 1) * D]
                    v.tensor_reduce(out=cols[:, 0:1], in_=xr, axis=AX.X, op=OP.add)
                    v.scalar_tensor_tensor(out=xhat, in0=xr, scalar=1.0,
                                           in1=xr, op0=OP.mult, op1=OP.mult,
                                           accum_out=cols[:, 1:2])
                    v.tensor_scalar(out=cols[:, 2:3], in0=cols[:, 0:1],
                                    scalar1=1.0 / D, scalar2=0.0,
                                    op0=OP.mult, op1=OP.bypass)
                    v.tensor_scalar(out=cols[:, 3:4], in0=cols[:, 1:2],
                                    scalar1=1.0 / D, scalar2=0.0,
                                    op0=OP.mult, op1=OP.bypass)
                    v.tensor_tensor(out=cols[:, 4:5], in0=cols[:, 2:3],
                                    in1=cols[:, 2:3], op=OP.mult)
                    v.scalar_tensor_tensor(out=cols[:, 4:5], in0=cols[:, 3:4],
                                           scalar=LN_EPS, in1=cols[:, 4:5],
                                           op0=OP.add, op1=OP.subtract
                                           ).then_inc(vd, 1)
                    v.wait_ge(sd, spre + 1)
                    v.reciprocal(out=cols[:, 5:6], in_=cols[:, 4:5])
                    v.scalar_tensor_tensor(out=cols[:, 6:7], in0=cols[:, 2:3],
                                           scalar=-1.0, in1=cols[:, 5:6],
                                           op0=OP.mult, op1=OP.mult)
                    v.tensor_scalar(out=xhat, in0=xr,
                                    scalar1=cols[:, 5:6], scalar2=cols[:, 6:7],
                                    op0=OP.mult, op1=OP.add).then_inc(vd, 1)

                @block.scalar
                def _(sc, vpre=vpre):
                    sc = SD(sc)
                    sc.wait_ge(vd, vpre + 1)
                    sc.activation(out=cols[:, 4:5], in_=cols[:, 4:5],
                                  func=AF.Sqrt).then_inc(sd, 1)
                S.add(vd, 2)
                S.add(sd, 1)
                vq = S.get(vd)

                emit_transposes(xhat,
                                lambda f: sbA[:, f * 128:(f + 1) * 128],
                                8, extra_tensor_wait=(vd, vq))
                xt_sd = S.get(sd)

                ldq_b = [S.get(wl0), S.get(wl1)]

                @block.sync
                def _(s, tc=tc, xt_sd=xt_sd, vpre=vpre):
                    if tc >= 1:
                        s.wait_ge(td, S.get(td))
                    for m in range(8):
                        wsem = wl0 if m % 2 == 0 else wl1
                        for k in range(8):
                            s.dma_start(
                                out=wbufs[m % 2][:, k * 512:(k + 1) * 512],
                                in_=wdq_ab[k * 128:(k + 1) * 128,
                                           m * 512:(m + 1) * 512]).then_inc(wsem, 16)
                    for k in range(8):
                        s.dma_start(out=wb1[:, 4096 + k * 288:4096 + (k + 1) * 288],
                                    in_=wkv_ab[k * 128:(k + 1) * 128, :]
                                    ).then_inc(wl1, 16)
                for m in range(8):
                    S.add(wl0 if m % 2 == 0 else wl1, 128)
                S.add(wl1, 128)

                @block.tensor
                def _(t, tc=tc, xt_sd=xt_sd, ldq_b=ldq_b, vpre=vpre):
                    t.wait_ge(sd, xt_sd)
                    for m in range(8):
                        t.wait_ge(wl0 if m % 2 == 0 else wl1,
                                  ldq_b[m % 2] + 128 * (m // 2 + 1))
                        if m >= 2:
                            t.wait_ge(vd, vpre + 2 + (m - 1))
                        for k in range(8):
                            mm = t.matmul(bank(m % 2),
                                          sbA[:, k * 128:(k + 1) * 128],
                                          wbufs[m % 2][:, k * 512:(k + 1) * 512],
                                          start=(k == 0), stop=(k == 7))
                        mm.then_inc(td, 1)
                    t.wait_ge(wl1, ldq_b[1] + 128 * 4 + 128)
                    for k in range(8):
                        mm = t.matmul(bank(2)[:, 0:288],
                                      sbA[:, k * 128:(k + 1) * 128],
                                      wb1[:, 4096 + k * 288:4096 + (k + 1) * 288],
                                      start=(k == 0), stop=(k == 7))
                    mm.then_inc(td, 1)
                S.add(td, 9)
                td_cq = S.get(td)

                @block.vector
                def _(v, tc=tc, vpre=vpre, td_b=td_cq - 9):
                    v = VD(v)
                    for m in range(8):
                        v.wait_ge(td, td_b + m + 1)
                        v.tensor_copy(sbD[:, m * 512:(m + 1) * 512],
                                      bank(m % 2)).then_inc(vd, 1)
                    v.wait_ge(td, td_b + 9)
                    v.tensor_copy(sbB[:, 0:256], bank(2)[:, 0:256])
                    v.tensor_tensor(out=sbB[:, 256:288], in0=bank(2)[:, 256:288],
                                    in1=sct[:, tc * 32:(tc + 1) * 32], op=OP.mult
                                    ).then_inc(vd, 1)
                S.add(vd, 9)
                vd_kv = S.get(vd)

                @block.gpsimd
                def _(g, tc=tc, vd_kv=vd_kv):
                    g.wait_ge(vd, vd_kv)
                    g.dma_start(out=agkv_in[tc * 128:(tc + 1) * 128, :],
                                in_=sbB[:, 0:288]).then_inc(st, 16)
                S.add(st, 16)
                vq2 = S.get(vd)

                emit_transposes(sbD[:, 0:4096],
                                lambda f: sbA[:, 1024 + f * 128:1024 + (f + 1) * 128],
                                32, extra_tensor_wait=(vd, vq2))
                cqt_sd = S.get(sd)

                @block.gpsimd
                def _(g, tc=tc, cqt_sd=cqt_sd):
                    g.wait_ge(sd, cqt_sd)
                    for f in range(32):
                        g.dma_start(
                            out=cqT_scr[f * 128:(f + 1) * 128,
                                        tc * 128:(tc + 1) * 128],
                            in_=sbA[:, 1024 + f * 128:1024 + (f + 1) * 128]
                            ).then_inc(st, 16)
                S.add(st, 16 * 32)

            a1_st = S.get(st)
            for grp in range(8):
                g_td = S.get(td)
                g_vd = S.get(vd)
                ldb = [S.get(wl0), S.get(wl1)]
                ldc = [S.get(dl0), S.get(dl1)]

                @block.sync
                def _(s, grp=grp, g_td=g_td, a1_st=a1_st):
                    if grp == 0:
                        s.wait_ge(st, a1_st)
                    else:
                        s.wait_ge(td, g_td)
                    for k in range(32):
                        wsem = wl0 if k % 2 == 0 else wl1
                        xsem = dl0 if k % 2 == 0 else dl1
                        if k >= 2:
                            s.wait_ge(td, g_td + (k - 1) * 1)
                        s.dma_start(out=wbufs[k % 2][:, 0:1536],
                                    in_=wuqr_ab[k * 128:(k + 1) * 128, :]
                                    ).then_inc(wsem, 16)
                        for j in range(2):
                            s.dma_start(
                                out=wbufs[k % 2][:, 2048 + j * 128:
                                                 2048 + (j + 1) * 128],
                                in_=cqT_scr[k * 128:(k + 1) * 128,
                                            (grp * 2 + j) * 128:
                                            (grp * 2 + j + 1) * 128]
                                ).then_inc(xsem, 16)
                for k in range(32):
                    S.add(wl0 if k % 2 == 0 else wl1, 16)
                    S.add(dl0 if k % 2 == 0 else dl1, 32)

                @block.tensor
                def _(t, grp=grp, ldb=ldb, ldc=ldc, g_vd=g_vd):
                    t.wait_ge(vd, g_vd)
                    for k in range(32):
                        t.wait_ge(wl0 if k % 2 == 0 else wl1,
                                  ldb[k % 2] + 16 * (k // 2 + 1))
                        t.wait_ge(dl0 if k % 2 == 0 else dl1,
                                  ldc[k % 2] + 32 * (k // 2 + 1))
                        for j in range(2):
                            for mq in range(3):
                                mm = t.matmul(
                                    bank(j * 3 + mq),
                                    wbufs[k % 2][:, 2048 + j * 128:
                                                 2048 + (j + 1) * 128],
                                    wbufs[k % 2][:, mq * 512:(mq + 1) * 512],
                                    start=(k == 0), stop=(k == 31))
                        mm.then_inc(td, 1)
                S.add(td, 32)
                td_q = S.get(td)

                @block.vector
                def _(v, grp=grp, td_q=td_q):
                    v = VD(v)
                    v.wait_ge(td, td_q)
                    for j in range(2):
                        tc = grp * 2 + j
                        v.tensor_copy(sbB[:, j * 2048:j * 2048 + 512],
                                      bank(j * 3 + 0))
                        v.tensor_copy(sbB[:, j * 2048 + 512:j * 2048 + 1024],
                                      bank(j * 3 + 1))
                        qr3 = sbB[:, j * 2048 + 1024:j * 2048 + 1536]\
                            .rearrange("p (h r) -> p h r", r=32)
                        tt = v.tensor_tensor(
                            out=qr3,
                            in0=bank(j * 3 + 2).rearrange("p (h r) -> p h r", r=32),
                            in1=sct[:, tc * 32:(tc + 1) * 32].unsqueeze(1)
                            .broadcast_to([128, 16, 32]),
                            op=OP.mult)
                        if j == 1:
                            tt.then_inc(vd, 1)
                S.add(vd, 1)
                vd_q = S.get(vd)

                @block.gpsimd
                def _(g, grp=grp, vd_q=vd_q):
                    g.wait_ge(vd, vd_q)
                    for j in range(2):
                        tc = grp * 2 + j
                        for j2 in range(8):
                            base = j2 * 2048 + tc * 128
                            g.dma_start(
                                out=a2aq_in[base:base + 128, 0:64],
                                in_=sbB[:, j * 2048 + j2 * 64:
                                        j * 2048 + (j2 + 1) * 64]).then_inc(st, 16)
                            g.dma_start(
                                out=a2aq_in[base:base + 128, 64:96],
                                in_=sbB[:, j * 2048 + 1024 + j2 * 32:
                                        j * 2048 + 1024 + (j2 + 1) * 32]
                                ).then_inc(st, 16)
                            g.dma_start(
                                out=a2aq_in[base:base + 128, 96:160],
                                in_=sbB[:, j * 2048 + (j2 + 8) * 64:
                                        j * 2048 + (j2 + 9) * 64]).then_inc(st, 16)
                            g.dma_start(
                                out=a2aq_in[base:base + 128, 160:192],
                                in_=sbB[:, j * 2048 + 1024 + (j2 + 8) * 32:
                                        j * 2048 + 1024 + (j2 + 9) * 32]
                                ).then_inc(st, 16)
                S.add(st, 16 * 64)

            st_c = S.get(st)

            @block.gpsimd
            def _(g):
                g.wait_ge(st, st_c)
                g.collective_compute(
                    "AllGather", OP.bypass,
                    replica_groups=[list(range(NCORES))],
                    ins=[agkv_in.ap().opt()],
                    outs=[agkv_out.ap().opt()],
                ).then_inc(cc, 1)
                with allow_ici_all_to_all():
                    g.collective_compute(
                        "AllToAll", OP.bypass,
                        replica_groups=[list(range(NCORES))],
                        ins=[a2aq_in.ap().opt()],
                        outs=[a2aq_out.ap().opt()],
                    ).then_inc(cc, 1)
            cc_l1 = S.add(cc, 2)

            ldw_b = S.get(ld)

            @block.sync
            def _(s):
                s.wait_ge(cc, cc_l1)
                s.dma_start(out=sbC[:, 6656:6912], in_=wukv_cb[0:128, :]
                            ).then_inc(ld, 16)
                s.dma_start(out=sbC[:, 6912:7168], in_=wukv_cb[128:256, :]
                            ).then_inc(ld, 16)
            ld_wukv = S.add(ld, 32)

            for bb in range(4):
                bvd = S.get(vd)

                @block.vector
                def _(v, bb=bb, bvd=bvd):
                    v = VD(v)
                    v.wait_ge(ld, ld_wukv)
                    v.wait_ge(td, S.get(td))
                    v.memset(cols[:, 43:44], 0.0).then_inc(vd, 1)
                S.add(vd, 1)

                kv_vd_base = S.get(vd)
                for kb in range(32):
                    ld_pre2 = S.get(dl2)

                    @block.sync
                    def _(s, bb=bb, kb=kb):
                        if kb >= 2:
                            s.wait_ge(vd, kv_vd_base + kb - 1)
                        s.dma_start(
                            out=sbEb2[kb % 2][:],
                            in_=agkv_out[bb * 4096 + kb * 128:
                                         bb * 4096 + (kb + 1) * 128, :]
                            ).then_inc(dl2, 16)
                    S.add(dl2, 16)

                    @block.vector
                    def _(v, kb=kb, ld_pre2=ld_pre2):
                        v = VD(v)
                        v.wait_ge(dl2, ld_pre2 + 16)
                        v.tensor_copy(sbE[:, 512:800], sbEb2[kb % 2][:]
                                      ).then_inc(vd, 1)
                    S.add(vd, 1)
                    vq3 = S.get(vd)

                    emit_transposes(sbE[:, 512:768],
                                    lambda f, kb=kb: sbB[:, f * 4096 + kb * 128:
                                                         f * 4096 + (kb + 1) * 128],
                                    2, extra_tensor_wait=(vd, vq3))
                    t_b3 = S.get(td)
                    s_b3 = S.get(sd)

                    @block.tensor
                    def _(t, kb=kb, s_b3=s_b3):
                        t.wait_ge(sd, s_b3)
                        t.transpose(out=pb(0)[0:32, :],
                                    in_=sbE[:, 768:800],
                                    identity=idf[:]).then_inc(td, 1)
                    S.add(td, 1)

                    @block.scalar
                    def _(sc, kb=kb, t_b3=t_b3):
                        sc = SD(sc)
                        sc.wait_ge(td, t_b3 + 1)
                        sc.copy(out=sbC[0:32, kb * 128:(kb + 1) * 128],
                                in_=pb(0)[0:32, :]).then_inc(sd, 1)
                    S.add(sd, 1)

                for hh in range(2):
                    p = bb * 2 + hh
                    kv_td = S.get(td)
                    kv_vd = S.get(vd)
                    kv_sd = S.get(sd)

                    @block.tensor
                    def _(t, hh=hh, kv_sd=kv_sd, kv_vd=kv_vd):
                        t.wait_ge(sd, kv_sd)
                        t.wait_ge(vd, kv_vd)
                        for mb in range(8):
                            if mb >= 1:
                                t.wait_ge(sd, kv_sd + mb)
                            for c2 in range(2):
                                mm = t.matmul(
                                    bank(0)[0:64, :],
                                    sbC[:, 6656 + c2 * 256 + hh * 64:
                                        6656 + c2 * 256 + (hh + 1) * 64],
                                    sbB[:, c2 * 4096 + mb * 512:
                                        c2 * 4096 + (mb + 1) * 512],
                                    start=(c2 == 0), stop=(c2 == 1))
                            mm.then_inc(td, 1)
                        for kb in range(32):
                            if kb >= 1:
                                t.wait_ge(sd, kv_sd + 8 + kb)
                            for c2 in range(2):
                                mm = t.matmul(
                                    bank(1)[:, 0:64],
                                    sbB[:, c2 * 4096 + kb * 128:
                                        c2 * 4096 + (kb + 1) * 128],
                                    sbC[:, 6656 + c2 * 256 + 128 + hh * 64:
                                        6656 + c2 * 256 + 128 + (hh + 1) * 64],
                                    start=(c2 == 0), stop=(c2 == 1))
                            mm.then_inc(td, 1)
                    S.add(td, 40)
                    kv2_td = S.get(td)

                    @block.scalar
                    def _(sc, kv_td=kv_td):
                        sc = SD(sc)
                        for mb in range(8):
                            sc.wait_ge(td, kv_td + mb + 1)
                            sc.copy(out=sbA[0:64, mb * 512:(mb + 1) * 512],
                                    in_=bank(0)[0:64, :]).then_inc(sd, 1)
                        for kb in range(32):
                            sc.wait_ge(td, kv_td + 8 + kb + 1)
                            sc.copy(out=sbC[:, 4096 + kb * 65:4096 + kb * 65 + 64],
                                    in_=bank(1)[:, 0:64]).then_inc(sd, 1)
                    S.add(sd, 40)
                    kv2_sd = S.get(sd)


                    @block.vector
                    def _(v, hh=hh, p=p, kv2_sd=kv2_sd):
                        v = VD(v)
                        v.wait_ge(sd, kv2_sd)
                        v.memset(sbC[:, 4096:6176]
                                 .rearrange("p (k o) -> p k o", o=65)[:, :, 64:65],
                                 1.0).then_inc(vd, 1)
                    S.add(vd, 1)

                    @block.gpsimd
                    def _(g, kv2_sd=kv2_sd):
                        g.wait_ge(sd, kv2_sd)
                        g.dma_start(out=sbA[64:96, 0:4096],
                                    in_=sbC[:, 0:4096][0:32, :]).then_inc(gd, 16)
                    S.add(gd, 16)

                    q_vd_base = S.get(vd)
                    for qb in range(32):
                        ld_q = S.get(dl3)
                        r = 2 * bb + qb // 16
                        loc = (qb % 16) * 128

                        @block.sync
                        def _(s, qb=qb, r=r, loc=loc, hh=hh):
                            if qb >= 2:
                                s.wait_ge(vd, q_vd_base + qb - 1)
                            s.dma_start(out=sbEb2[qb % 2][:, 0:96],
                                        in_=a2aq_out[r * 2048 + loc:
                                                     r * 2048 + loc + 128,
                                                     hh * 96:(hh + 1) * 96]
                                        ).then_inc(dl3, 16)
                        S.add(dl3, 16)

                        @block.vector
                        def _(v, qb=qb, ld_q=ld_q):
                            v = VD(v)
                            v.wait_ge(dl3, ld_q + 16)
                            v.tensor_copy(sbE[:, 512:608], sbEb2[qb % 2][:, 0:96]
                                          ).then_inc(vd, 1)
                        S.add(vd, 1)
                        vq4 = S.get(vd)

                        @block.scalar
                        def _(sc, qb=qb, vq4=vq4):
                            sc = SD(sc)
                            sc.wait_ge(vd, vq4)
                            sc.activation(out=sbE[:, 608:704], in_=sbE[:, 512:608],
                                          func=AF.Square,
                                          accum_out=sbE[:, 704 + qb:705 + qb]
                                          ).then_inc(sd, 1)
                        S.add(sd, 1)
                        sq_sd = S.get(sd)

                        t_b4 = S.get(td)

                        @block.tensor
                        def _(t, qb=qb, sq_sd=sq_sd):
                            t.wait_ge(sd, sq_sd)
                            t.transpose(out=pb(qb % 2)[0:96, :],
                                        in_=sbE[:, 512:608],
                                        identity=idf[:]).then_inc(td, 1)
                        S.add(td, 1)

                        @block.scalar
                        def _(sc, qb=qb, t_b4=t_b4):
                            sc = SD(sc)
                            sc.wait_ge(td, t_b4 + 1)
                            sc.copy(out=sbA[0:96, 4096 + qb * 128:
                                            4096 + (qb + 1) * 128],
                                    in_=pb(qb % 2)[0:96, :]).then_inc(sd, 1)
                        S.add(sd, 1)

                    bb_vd = S.get(vd)
                    bb_td = S.get(td)
                    bb_sd = S.get(sd)

                    @block.vector
                    def _(v, bb_sd=bb_sd, gd_kr=S.get(gd)):
                        v = VD(v)
                        v.wait_ge(sd, bb_sd)
                        v.wait_ge(gd, gd_kr)
                        v.tensor_reduce(out=cols[:, 44:45], in_=sbE[:, 704:736],
                                        axis=AX.X, op=OP.max)
                        v.tensor_reduce(out=cols[0:96, 45:46], in_=sbA[0:96, 0:4096],
                                        axis=AX.X, op=OP.max,
                                        apply_absolute_value=True).then_inc(vd, 1)
                    S.add(vd, 1)
                    bv1 = S.get(vd)

                    @block.tensor
                    def _(t, bv1=bv1):
                        t.wait_ge(vd, bv1)
                        t.transpose(out=pb(0)[0:1, :], in_=cols[:, 44:45],
                                    identity=idf[:])
                        t.transpose(out=pb(1)[0:1, 0:96], in_=cols[0:96, 45:46],
                                    identity=idf[0:96, 0:96]).then_inc(td, 1)
                    S.add(td, 1)
                    bt1 = S.get(td)

                    @block.vector
                    def _(v, bt1=bt1):
                        v = VD(v)
                        v.wait_ge(td, bt1)
                        v.tensor_reduce(out=cols[0:1, 46:47], in_=pb(0)[0:1, :],
                                        axis=AX.X, op=OP.max)
                        v.tensor_reduce(out=cols[0:1, 47:48], in_=pb(1)[0:1, 0:96],
                                        axis=AX.X, op=OP.max).then_inc(vd, 1)
                    S.add(vd, 1)
                    bv2 = S.get(vd)

                    @block.scalar
                    def _(sc, bv2=bv2):
                        sc = SD(sc)
                        sc.wait_ge(vd, bv2)
                        sc.activation(out=cols[0:1, 46:47], in_=cols[0:1, 46:47],
                                      func=AF.Sqrt).then_inc(sd, 1)
                    S.add(sd, 1)
                    bs1 = S.get(sd)

                    @block.vector
                    def _(v, bs1=bs1):
                        v = VD(v)
                        v.wait_ge(sd, bs1)
                        v.tensor_tensor(out=cols[0:1, 48:49], in0=cols[0:1, 46:47],
                                        in1=cols[0:1, 47:48], op=OP.mult)
                        v.tensor_scalar(out=cols[0:1, 48:49], in0=cols[0:1, 48:49],
                                        scalar1=-1.0, scalar2=0.0,
                                        op0=OP.mult, op1=OP.bypass).then_inc(vd, 1)
                    S.add(vd, 1)
                    bv3 = S.get(vd)

                    @block.gpsimd
                    def _(g, bv3=bv3):
                        g.wait_ge(vd, bv3)
                        g.dma_start(out=cols[:, 49:50],
                                    in_=cols[0:1, 48:49].unsqueeze(1)
                                    .broadcast_to([1, 128, 1])).then_inc(gd, 16)
                    S.add(gd, 16)
                    gd_b = S.get(gd)

                    for tq in range(8):
                        nkb = 4 * tq + 4
                        t0 = S.get(td)
                        s0 = S.get(sd)
                        v0 = S.get(vd)

                        @block.tensor
                        def _(t, tq=tq, nkb=nkb, t0=t0, s0=s0, v0=v0,
                              gd_b=gd_b):
                            if tq == 0:
                                t.wait_ge(gd, gd_b)
                            for kb in range(nkb):
                                if kb >= 2:
                                    t.wait_ge(sd, s0 + kb - 1)
                                t.matmul(bank(4 + kb % 2),
                                         sbA[0:96, kb * 128:(kb + 1) * 128],
                                         sbA[0:96, 4096 + tq * 512:
                                             4096 + (tq + 1) * 512],
                                         start=True, stop=True).then_inc(td, 1)
                                if kb >= 1:
                                    kk = kb - 1
                                    if kk >= nkb - 4:
                                        t.wait_ge(vd, v0 + 1 + (kk - (nkb - 4)) + 1)
                                    else:
                                        t.wait_ge(sd, s0 + kk + 1)
                                    t.matmul(psB[0:65, 0:512],
                                             sbC[:, 4096 + kk * 65:
                                                 4096 + (kk + 1) * 65],
                                             sbDb2[kk % 2][:],
                                             start=(kk == 0), stop=False
                                             ).then_inc(td, 1)
                            kk = nkb - 1
                            t.wait_ge(vd, v0 + 1 + 4)
                            t.matmul(psB[0:65, 0:512],
                                     sbC[:, 4096 + kk * 65:4096 + (kk + 1) * 65],
                                     sbDb2[kk % 2][:],
                                     start=False, stop=True).then_inc(td, 1)
                        S.add(td, 2 * nkb)

                        @block.scalar
                        def _(sc, tq=tq, nkb=nkb, t0=t0):
                            sc = SD(sc)
                            for kb in range(nkb):
                                sc.wait_ge(td, t0 + max(2 * kb, 1))
                                sc.activation(out=sbDb2[kb % 2][:],
                                              in_=bank(4 + kb % 2),
                                              func=AF.Exp, scale=0.1020620726,
                                              bias=cols[:, 49:50]).then_inc(sd, 1)
                        S.add(sd, nkb)

                        @block.vector
                        def _(v, tq=tq, nkb=nkb, s0=s0):
                            v = VD(v)
                            v.memset(cols[:, 50:51], 0.0).then_inc(vd, 1)
                            for dd in range(4):
                                kb = nkb - 4 + dd
                                v.wait_ge(sd, s0 + kb + 1)
                                v.tensor_tensor(out=sbDb2[kb % 2][:],
                                                in0=sbDb2[kb % 2][:],
                                                in1=msk[:, dd * 512:(dd + 1) * 512],
                                                op=OP.mult).then_inc(vd, 1)
                        S.add(vd, 5)

                        av_td = S.get(td)

                        @block.scalar
                        def _(sc, av_td=av_td):
                            sc = SD(sc)
                            sc.wait_ge(td, av_td)
                            sc.copy(out=sbE[0:65, 0:512], in_=psB[0:65, 0:512]
                                    ).then_inc(sd, 1)
                        S.add(sd, 1)
                        yv_sd = S.get(sd)

                        for sub in range(4):
                            s_b5 = S.get(sd)
                            v_b5 = S.get(vd)

                            @block.tensor
                            def _(t, sub=sub, yv_sd=yv_sd, v_b5=v_b5):
                                t.wait_ge(sd, yv_sd)
                                if sub >= 2:
                                    t.wait_ge(vd, v_b5)
                                t.transpose(out=pya(sub % 2),
                                            in_=sbE[0:65, 0:512]
                                            [0:65, sub * 128:(sub + 1) * 128],
                                            identity=idf[0:65, 0:65]
                                            ).then_inc(td, 1)
                            S.add(td, 1)
                            tv = S.get(td)

                            @block.vector
                            def _(v, sub=sub, tv=tv, tq=tq, hh=hh, bb=bb):
                                v = VD(v)
                                v.wait_ge(td, tv)
                                v.reciprocal(out=cols[:, 51:52],
                                             in_=pya(sub % 2)[:, 64:65])
                                v.tensor_scalar(out=sbEb2[2][:, sub * 64:
                                                            (sub + 1) * 64],
                                                in0=pya(sub % 2)[:, 0:64],
                                                scalar1=cols[:, 51:52], scalar2=0.0,
                                                op0=OP.mult, op1=OP.bypass
                                                ).then_inc(vd, 1)
                            S.add(vd, 1)
                            yv = S.get(vd)

                            @block.gpsimd
                            def _(g, sub=sub, tq=tq, hh=hh, bb=bb, yv=yv):
                                g.wait_ge(vd, yv)
                                gt = tq * 512 + sub * 128
                                j2 = 2 * bb + (1 if gt >= 2048 else 0)
                                loc = gt % 2048
                                g.dma_start(
                                    out=a2ay_in[j2 * 2048 + loc:
                                                j2 * 2048 + loc + 128,
                                                hh * 64:(hh + 1) * 64],
                                    in_=sbEb2[2][:, sub * 64:(sub + 1) * 64]
                                    ).then_inc(st, 16)
                            S.add(st, 16)

            st_y = S.get(st)

            @block.gpsimd
            def _(g):
                g.wait_ge(st, st_y)
                with allow_ici_all_to_all():
                    g.collective_compute(
                        "AllToAll", OP.bypass,
                        replica_groups=[list(range(NCORES))],
                        ins=[a2ay_in.ap().opt()],
                        outs=[a2ay_out.ap().opt()],
                    ).then_inc(cc, 1)
            cc_y = S.add(cc, 1)

            for tc in range(TC):
                ld_pre = S.get(ld)
                vd_pre = S.get(vd)

                @block.sync
                def _(s, tc=tc, vd_pre=vd_pre):
                    if tc == 0:
                        s.wait_ge(cc, cc_y)
                        s.wait_ge(vd, vd_pre)
                    else:
                        s.wait_ge(vd, vd_pre)
                    for r in range(8):
                        s.dma_start(out=sbB[:, r * 128:(r + 1) * 128],
                                    in_=a2ay_out[r * 2048 + tc * 128:
                                                 r * 2048 + (tc + 1) * 128, :]
                                    ).then_inc(ld, 16)
                    for m in range(2):
                        for f in range(8):
                            s.dma_start(
                                out=wb1[:, (m * 8 + f) * 512:(m * 8 + f + 1) * 512],
                                in_=wo_ab[f * 128:(f + 1) * 128,
                                          m * 512:(m + 1) * 512]).then_inc(ld, 16)
                S.add(ld, 16 * 24)
                ld_all = S.get(ld)

                @block.vector
                def _(v, ld_all=ld_all):
                    v = VD(v)
                    v.wait_ge(ld, ld_all)
                    v.tensor_copy(sbD[:, 0:1024], sbB[:, 0:1024]).then_inc(vd, 1)
                S.add(vd, 1)
                vq5 = S.get(vd)

                emit_transposes(sbD[:, 0:1024],
                                lambda f: sbA[:, f * 128:(f + 1) * 128],
                                8, extra_tensor_wait=(vd, vq5))
                sd_yt = S.get(sd)

                @block.tensor
                def _(t, sd_yt=sd_yt, vd_pre=vd_pre, tc=tc):
                    t.wait_ge(sd, sd_yt)
                    for m in range(2):
                        for f in range(8):
                            mm = t.matmul(bank(m),
                                          sbA[:, f * 128:(f + 1) * 128],
                                          wb1[:, (m * 8 + f) * 512:
                                              (m * 8 + f + 1) * 512],
                                          start=(f == 0), stop=(f == 7))
                    mm.then_inc(td, 1)
                S.add(td, 1)
                td_y = S.get(td)

                @block.vector
                def _(v, td_y=td_y, tc=tc):
                    v = VD(v)
                    v.wait_ge(td, td_y)
                    for m in range(2):
                        xr = x_res[:, tc * D + m * 512:tc * D + (m + 1) * 512]
                        tt = v.tensor_tensor(out=xr, in0=bank(m), in1=xr,
                                             op=OP.add)
                        if m == 1:
                            tt.then_inc(vd, 1)
                S.add(vd, 1)


        if N_SUBLAYERS >= 1 and DO_FFN:
            emit_l0ffn()
        if N_SUBLAYERS >= 2:
            emit_l0attn()
        if N_SUBLAYERS >= 3:
            emit_l0ffn()
        if N_SUBLAYERS >= 4:
            l1_wq(wg1, wgq1, 2816, 1024)
            l1_wq(wu1, wuq1, 2816, 1024)
            l1_wq(wd1, wdq1, 1024, 2816)
            emit_l1ffn()
        if N_SUBLAYERS >= 5:
            emit_l1attn()
        if N_SUBLAYERS >= 6:
            emit_l1ffn()

        if DBG:
            dumps = [("d_agx", agx_in, [1024, 2048]),
                     ("d_sK", scanK, [512, 4096]), ("d_sQ", scanQ, [512, 4096]),
                     ("d_sBK", scanBK, [512, 4096]), ("d_sBV", scanBV, [512, 4096]),
                     ("d_a2a", a2a0_in, [1024, 2048]),
                     ("d_a2o", a2a0_out, [1024, 2048])]
            daps = [(nc.dram_tensor(nm, shp, BF16, kind="ExternalOutput").ap(), t)
                    for nm, t, shp in dumps]
            d_vd = S.get(vd); d_td = S.get(td); d_sd = S.get(sd); d_st0 = S.get(st)

            @block.gpsimd
            def _(g):
                g.wait_ge(vd, d_vd)
                g.wait_ge(td, d_td)
                g.wait_ge(sd, d_sd)
                g.wait_ge(ld, S.get(ld))
                g.wait_ge(st, d_st0)
                for w_ap, t in daps:
                    g.dma_start(out=w_ap, in_=t.ap()).then_inc(st, 16)
                g.wait_ge(st, d_st0 + 16 * len(daps))
            S.add(st, 16 * len(daps))

        fin_vd = S.get(vd)
        fin_td = S.get(td)
        fin_sd = S.get(sd)
        st_b = S.get(st)

        @block.gpsimd
        def _(g):
            g.wait_ge(vd, fin_vd)
            g.wait_ge(td, fin_td)
            g.wait_ge(sd, fin_sd)
            g.wait_ge(ld, S.get(ld))
            for tc in range(TC):
                g.dma_start(out=y_out[tc * 128:(tc + 1) * 128, :],
                            in_=x_res[:, tc * D:(tc + 1) * D]).then_inc(st, 16)
            g.wait_ge(st, st_b + 16 * TC)
        S.add(st, 16 * TC)

    return nc


def kernel(**inputs):
    x = np.ascontiguousarray(np.asarray(inputs["L0_attn_x"], dtype=np.float32))
    xf = x.reshape(NCORES * NT, D)
    ident_f = np.eye(128, dtype=np.float32)
    wg = np.ascontiguousarray(np.asarray(inputs["L0_ffn_w_gate"], np.float32))
    wu = np.ascontiguousarray(np.asarray(inputs["L0_ffn_w_up"], np.float32))
    wd = np.ascontiguousarray(np.asarray(inputs["L0_ffn_w_down"], np.float32))

    Wq = np.asarray(inputs["L0_attn_Wq"], np.float32)
    Wk = np.asarray(inputs["L0_attn_Wk"], np.float32)
    Wv = np.asarray(inputs["L0_attn_Wv"], np.float32)
    Wbeta = np.asarray(inputs["L0_attn_Wbeta"], np.float32)
    Wg_a = np.asarray(inputs["L0_attn_Wg"], np.float32)
    Wout_a = np.asarray(inputs["L0_attn_Wout"], np.float32)
    ln_g = np.asarray(inputs["L0_attn_ln_g"], np.float32)
    ln_b = np.asarray(inputs["L0_attn_ln_b"], np.float32)

    perm = np.empty(1024, np.int64)
    for r in range(8):
        for hh in range(2):
            for dd_ in range(64):
                perm[r * 128 + hh * 64 + dd_] = (r + 8 * hh) * 64 + dd_
    lng_r = np.ascontiguousarray(np.broadcast_to(ln_g[perm][None, :], (128, 1024)),
                                 dtype=np.float32).astype(np.float32)
    lnb_r = np.ascontiguousarray(np.broadcast_to(ln_b[perm][None, :], (128, 1024)),
                                 dtype=np.float32).astype(np.float32)
    l0wg_a = np.ascontiguousarray(Wg_a[:, perm])
    l0wo_a = np.ascontiguousarray(Wout_a[perm, :])
    esel = np.zeros((128, 128), dtype=BF)
    esel[0:64, 62] = 1
    esel[64:128, 63] = 1

    W_DQ = np.asarray(inputs["L1_attn_W_DQ"], np.float32)
    W_UQ = np.asarray(inputs["L1_attn_W_UQ"], np.float32)
    W_DKV = np.asarray(inputs["L1_attn_W_DKV"], np.float32)
    W_UK = np.asarray(inputs["L1_attn_W_UK"], np.float32)
    W_UV = np.asarray(inputs["L1_attn_W_UV"], np.float32)
    W_QR = np.asarray(inputs["L1_attn_W_QR"], np.float32)
    W_KR = np.asarray(inputs["L1_attn_W_KR"], np.float32)
    W_O = np.asarray(inputs["L1_attn_W_O"], np.float32)
    wkv_cat = np.ascontiguousarray(np.concatenate([W_DKV, W_KR], axis=1))
    wo_perm = np.ascontiguousarray(W_O[perm, :])
    tpos = np.arange(4096, dtype=np.float64)
    inv = 1.0 / 10000.0 ** (np.arange(0, 32, 2, dtype=np.float64) / 32)
    fr = tpos[:, None] * inv[None, :]
    emb = np.concatenate([fr, fr], -1)
    scale_all = (np.cos(emb) + np.sin(emb)).astype(np.float32)
    mask4 = np.zeros((128, 4, 512), dtype=BF)
    jj = np.arange(512)[None, :]
    pp = np.arange(128)[:, None]
    for dmi in range(4):
        mask4[:, dmi, :] = (jj >= pp + 128 * dmi).astype(BF)
    mask4 = np.ascontiguousarray(mask4.reshape(128, 2048))

    nc = build_nc()
    in_maps = []
    for c in range(NCORES):
        h0, h1 = c, c + 8
        pwc = np.zeros((1024, 392), np.float32)
        pwc[:, 0:64] = Wq[:, h0 * 64:(h0 + 1) * 64]
        pwc[:, 64:128] = Wq[:, h1 * 64:(h1 + 1) * 64]
        pwc[:, 128:192] = Wk[:, h0 * 64:(h0 + 1) * 64]
        pwc[:, 192:256] = Wk[:, h1 * 64:(h1 + 1) * 64]
        pwc[:, 256:320] = Wv[:, h0 * 64:(h0 + 1) * 64]
        pwc[:, 320:384] = Wv[:, h1 * 64:(h1 + 1) * 64]
        pwc[:, 384] = Wbeta[:, h0]
        pwc[:, 385] = Wbeta[:, h1]
        in_maps.append({
            "x": np.ascontiguousarray(xf[c * NT:(c + 1) * NT]),
            "ident_f": ident_f,
            "l0wg": wg, "l0wu": wu, "l0wd": wd,
            "l0pw": pwc, "l0wg_a": l0wg_a, "l0wo_a": l0wo_a,
            "lng_r": lng_r, "lnb_r": lnb_r, "esel": esel,
            "l1wg": np.ascontiguousarray(np.asarray(inputs["L1_ffn_w_gate"], np.float32)),
            "l1wu": np.ascontiguousarray(np.asarray(inputs["L1_ffn_w_up"], np.float32)),
            "l1wd": np.ascontiguousarray(np.asarray(inputs["L1_ffn_w_down"], np.float32)),
            "l1wdq": W_DQ, "l1wuq": W_UQ, "l1wqr": W_QR,
            "l1wkv": wkv_cat,
            "l1wukv": np.ascontiguousarray(np.concatenate(
                [W_UK[:, h0 * 64:(h0 + 1) * 64], W_UK[:, h1 * 64:(h1 + 1) * 64],
                 W_UV[:, h0 * 64:(h0 + 1) * 64], W_UV[:, h1 * 64:(h1 + 1) * 64]],
                axis=1)),
            "l1wo": wo_perm,
            "scal_t": np.ascontiguousarray(
                scale_all[2048 * (c % 2):2048 * (c % 2) + 2048].reshape(
                    16, 128, 32).transpose(1, 0, 2).reshape(128, 512)),
            "mask4": mask4,
        })
    res = run_bass_kernel_spmd(nc, in_maps, core_ids=list(range(NCORES)))
    y = np.concatenate([res.results[c]["y"] for c in range(NCORES)], axis=0)
    return y.reshape(4, 4096, D).astype(np.float32)


# version 03
# speedup vs baseline: 1.2298x; 1.2298x over previous
import numpy as np
import ml_dtypes
import concourse.bass as bass
import concourse.mybir as mybir
from concourse.bass_utils import run_bass_kernel_spmd
from concourse.aot_env import allow_ici_all_to_all

F32 = mybir.dt.float32
BF16 = mybir.dt.bfloat16
AF = mybir.ActivationFunctionType
OP = mybir.AluOpType
AX = mybir.AxisListType
BF = ml_dtypes.bfloat16

NCORES = 8
NT = 2048
TC = 16
D = 1024
MAGIC = float(np.float32(1.5 * 2 ** 23))
LN_EPS = 1e-5

N_SUBLAYERS = 6
DO_WQ = True
DO_FFN = True
FFN_STAGES = 4
FFN_NGRP = 8
DBG = False



_V_OPS = {"tensor_tensor", "tensor_scalar", "tensor_reduce", "tensor_copy",
          "scalar_tensor_tensor", "reciprocal", "memset", "tensor_tensor_scan",
          "select", "tensor_add", "tensor_mul", "tensor_sub", "tensor_max",
          "iota", "tensor_tensor_reduce", "tensor_relu", "bn_stats", "bn_aggr"}


class VD:

    def __init__(self, v):
        self._v = v

    def __getattr__(self, name):
        f = getattr(self._v, name)
        if name in _V_OPS:
            def wrapped(*a, **k):
                r = f(*a, **k)
                self._v.drain()
                return r
            return wrapped
        return f



_S_OPS = {"activation", "copy", "add", "mul", "sign", "sqrt", "square", "memzero"}


class SD:

    def __init__(self, s):
        self._s = s

    def __getattr__(self, name):
        f = getattr(self._s, name)
        if name in _S_OPS:
            def wrapped(*a, **k):
                r = f(*a, **k)
                self._s.drain()
                return r
            return wrapped
        return f


class Sched:
    def __init__(self):
        self.c = {}

    def get(self, sem):
        return self.c.setdefault(sem.name, 0)

    def add(self, sem, n):
        self.c[sem.name] = self.get(sem) + n
        return self.c[sem.name]


def build_nc():
    nc = bass.Bass("TRN2", target_bir_lowering=False, detect_race_conditions=False)

    x_in = nc.dram_tensor("x", [NT, D], F32, kind="ExternalInput").ap()
    y_out = nc.dram_tensor("y", [NT, D], F32, kind="ExternalOutput").ap()
    ident_f = nc.dram_tensor("ident_f", [128, 128], F32, kind="ExternalInput").ap()

    wg0 = nc.dram_tensor("l0wg", [4096, 1024], F32, kind="ExternalInput").ap()
    wu0 = nc.dram_tensor("l0wu", [4096, 1024], F32, kind="ExternalInput").ap()
    wd0 = nc.dram_tensor("l0wd", [1024, 4096], F32, kind="ExternalInput").ap()

    wgi0 = nc.dram_tensor("wgi0", [1024, 4096], BF16)
    wui0 = nc.dram_tensor("wui0", [1024, 4096], BF16)
    wdi0 = nc.dram_tensor("wdi0", [4096, 1024], BF16)

    l0pw = nc.dram_tensor("l0pw", [1024, 392], F32, kind="ExternalInput").ap()
    l0wg_a = nc.dram_tensor("l0wg_a", [1024, 1024], F32, kind="ExternalInput").ap()
    l0wo_a = nc.dram_tensor("l0wo_a", [1024, 1024], F32, kind="ExternalInput").ap()
    lng_r = nc.dram_tensor("lng_r", [128, 1024], F32, kind="ExternalInput").ap()
    lnb_r = nc.dram_tensor("lnb_r", [128, 1024], F32, kind="ExternalInput").ap()
    esel_d = nc.dram_tensor("esel", [128, 128], BF16, kind="ExternalInput").ap()
    eselS_d = nc.dram_tensor("eselS", [128, 4096], BF16, kind="ExternalInput").ap()
    l0pw_b = nc.dram_tensor("l0pw_b", [1024, 392], BF16)
    l0wg_ab = nc.dram_tensor("l0wg_ab", [1024, 1024], BF16)
    l0wo_ab = nc.dram_tensor("l0wo_ab", [1024, 1024], BF16)
    agx_in = nc.dram_tensor("agx_in", [1024, 2048], BF16)
    agx_out = nc.dram_tensor("agx_out", [8192, 2048], BF16)
    scanK = nc.dram_tensor("scanK", [8 * 64, 4096], BF16)
    scanQ = nc.dram_tensor("scanQ", [8 * 64, 4096], BF16)
    scanBK = nc.dram_tensor("scanBK", [8 * 64, 4096], BF16)
    scanBV = nc.dram_tensor("scanBV", [8 * 64, 4096], BF16)
    a2a0_in = nc.dram_tensor("a2a0_in", [1024, 2048], BF16)
    a2a0_out = nc.dram_tensor("a2a0_out", [1024, 2048], BF16)

    wg1 = nc.dram_tensor("l1wg", [2816, 1024], F32, kind="ExternalInput").ap()
    wu1 = nc.dram_tensor("l1wu", [2816, 1024], F32, kind="ExternalInput").ap()
    wd1 = nc.dram_tensor("l1wd", [1024, 2816], F32, kind="ExternalInput").ap()
    wgq1 = nc.dram_tensor("wgq1", [1024, 2816], BF16)
    wuq1 = nc.dram_tensor("wuq1", [1024, 2816], BF16)
    wdq1 = nc.dram_tensor("wdq1", [2816, 1024], BF16)

    wdq_a = nc.dram_tensor("l1wdq", [1024, 4096], F32, kind="ExternalInput").ap()
    wuq_a = nc.dram_tensor("l1wuq", [4096, 1024], F32, kind="ExternalInput").ap()
    wqr_a = nc.dram_tensor("l1wqr", [4096, 512], F32, kind="ExternalInput").ap()
    wkv_a = nc.dram_tensor("l1wkv", [1024, 288], F32, kind="ExternalInput").ap()
    wukv_c = nc.dram_tensor("l1wukv", [256, 256], F32, kind="ExternalInput").ap()
    wo_a = nc.dram_tensor("l1wo", [1024, 1024], F32, kind="ExternalInput").ap()
    scal_t = nc.dram_tensor("scal_t", [128, 512], F32, kind="ExternalInput").ap()
    mask4_d = nc.dram_tensor("mask4", [128, 2048], BF16, kind="ExternalInput").ap()
    wdq_ab = nc.dram_tensor("wdq_ab", [1024, 4096], BF16)
    wuqr_ab = nc.dram_tensor("wuqr_ab", [4096, 1536], BF16)
    wkv_ab = nc.dram_tensor("wkv_ab", [1024, 288], BF16)
    wukv_cb = nc.dram_tensor("wukv_cb", [256, 256], BF16)
    wo_ab = nc.dram_tensor("wo_ab", [1024, 1024], BF16)
    agkv_in = nc.dram_tensor("agkv_in", [2048, 288], BF16)
    agkv_out = nc.dram_tensor("agkv_out", [16384, 288], BF16)
    cqT_scr = nc.dram_tensor("cqT_scr", [4096, 2048], BF16)
    a2aq_in = nc.dram_tensor("a2aq_in", [16384, 192], BF16)
    a2aq_out = nc.dram_tensor("a2aq_out", [16384, 192], BF16)
    a2ay_in = nc.dram_tensor("a2ay_in", [16384, 128], BF16)
    a2ay_out = nc.dram_tensor("a2ay_out", [16384, 128], BF16)

    from contextlib import ExitStack
    with ExitStack() as _es:
        x_res = _es.enter_context(nc.sbuf_tensor("x_res", [128, TC * D], F32))
        sbA = _es.enter_context(nc.sbuf_tensor("sbA", [128, 8192], BF16))
        sbB = _es.enter_context(nc.sbuf_tensor("sbB", [128, 8192], BF16))
        sbC = _es.enter_context(nc.sbuf_tensor("sbC", [128, 8192], BF16))
        sbD = _es.enter_context(nc.sbuf_tensor("sbD", [128, 4096], F32))
        sbE = _es.enter_context(nc.sbuf_tensor("sbE", [128, 2048], F32))
        wb0 = _es.enter_context(nc.sbuf_tensor("wb0", [128, 8192], BF16))
        wb1 = _es.enter_context(nc.sbuf_tensor("wb1", [128, 8192], BF16))
        cols = _es.enter_context(nc.sbuf_tensor("cols", [128, 64], F32))
        idf = _es.enter_context(nc.sbuf_tensor("idf", [128, 128], F32))
        esl = _es.enter_context(nc.sbuf_tensor("esl", [128, 128], BF16))
        eslS = _es.enter_context(nc.sbuf_tensor("eslS", [128, 4096], BF16))
        lngb = _es.enter_context(nc.sbuf_tensor("lngb", [128, 2048], F32))
        sct = _es.enter_context(nc.sbuf_tensor("sct", [128, 512], F32))
        msk = _es.enter_context(nc.sbuf_tensor("msk", [128, 2048], BF16))
        psA = _es.enter_context(nc.psum_tensor("psA", [128, 3072], F32))
        psB = _es.enter_context(nc.psum_tensor("psB", [128, 1024], F32))
        ld = _es.enter_context(nc.semaphore("ld"))
        gd = _es.enter_context(nc.semaphore("gd"))
        td = _es.enter_context(nc.semaphore("td"))
        vd = _es.enter_context(nc.semaphore("vd"))
        sd = _es.enter_context(nc.semaphore("sd"))
        st = _es.enter_context(nc.semaphore("st"))
        cc = _es.enter_context(nc.semaphore("cc"))
        wl0 = _es.enter_context(nc.semaphore("wl0"))
        wl1 = _es.enter_context(nc.semaphore("wl1"))
        dl0 = _es.enter_context(nc.semaphore("dl0"))
        dl1 = _es.enter_context(nc.semaphore("dl1"))
        dl2 = _es.enter_context(nc.semaphore("dl2"))
        dl3 = _es.enter_context(nc.semaphore("dl3"))
        r0 = _es.enter_context(nc.semaphore("r0"))
        r1 = _es.enter_context(nc.semaphore("r1"))
        block = _es.enter_context(nc.Block())
        S = Sched()
        wbufs = [wb0, wb1]

        def bank(i):
            return psA[:, i * 512:(i + 1) * 512]

        def pb(slot):
            return psB[:, slot * 512:slot * 512 + 128]

        @block.sync
        def _(s):
            for tc in range(TC):
                s.dma_start(out=x_res[:, tc * D:(tc + 1) * D],
                            in_=x_in[tc * 128:(tc + 1) * 128, :]).then_inc(ld, 16)
            s.dma_start(out=idf[:], in_=ident_f[:]).then_inc(ld, 16)
            s.dma_start(out=esl[:], in_=esel_d[:]).then_inc(ld, 16)
            s.dma_start(out=eslS[:], in_=eselS_d[:]).then_inc(ld, 16)
            s.dma_start(out=lngb[:, 0:1024], in_=lng_r[:]).then_inc(ld, 16)
            s.dma_start(out=lngb[:, 1024:2048], in_=lnb_r[:]).then_inc(ld, 16)
            s.dma_start(out=sct[:], in_=scal_t[:]).then_inc(ld, 16)
            s.dma_start(out=msk[:], in_=mask4_d[:]).then_inc(ld, 16)
        ld_x = S.add(ld, 16 * (TC + 7))

        def emit_transposes(src_f32, dst_fn, nchunks, extra_tensor_wait=None):
            t_b = S.get(td)
            s_b = S.get(sd)

            @block.tensor
            def _(t):
                if extra_tensor_wait is not None:
                    sem, val = extra_tensor_wait
                    t.wait_ge(sem, val)
                t.wait_ge(sd, s_b)
                for f in range(nchunks):
                    if f >= 2:
                        t.wait_ge(sd, s_b + (f - 1))
                    t.transpose(out=pb(f % 2),
                                in_=src_f32[:, f * 128:(f + 1) * 128],
                                identity=idf[:]).then_inc(td, 1)

            @block.scalar
            def _(sc):
                sc = SD(sc)
                for f in range(nchunks):
                    sc.wait_ge(td, t_b + f + 1)
                    sc.copy(out=dst_fn(f), in_=pb(f % 2)).then_inc(sd, 1)

            S.add(td, nchunks)
            S.add(sd, nchunks)

        def l0_wq(w_nat, w_stash, R, Cin, sw_i, isw_i):
            ntile = R // 128
            nch = Cin // 128
            fb = [wb0[:].bitcast(F32), wb1[:].bitcast(F32)]
            ld_b = S.get(ld)
            vd_b = S.get(vd)

            @block.sync
            def _(s):
                s.wait_ge(vd, vd_b)
                for it in range(ntile):
                    if it >= 1:
                        s.wait_ge(vd, vd_b + it)
                    s.dma_start(out=fb[it % 2][:, 0:Cin],
                                in_=w_nat[it * 128:(it + 1) * 128, :]).then_inc(ld, 16)
            S.add(ld, 16 * ntile)

            @block.vector
            def _(v):
                v = VD(v)
                for it in range(ntile):
                    v.wait_ge(ld, ld_b + 16 * (it + 1))
                    v.tensor_reduce(out=cols[:, 17:18], in_=fb[it % 2][:, 0:Cin],
                                    axis=AX.X, op=OP.add, apply_absolute_value=True)
                    if it == 0:
                        v.tensor_copy(cols[:, 16:17], cols[:, 17:18]
                                      ).then_inc(vd, 1)
                    else:
                        v.tensor_tensor(out=cols[:, 16:17], in0=cols[:, 16:17],
                                        in1=cols[:, 17:18], op=OP.add
                                        ).then_inc(vd, 1)
            S.add(vd, ntile)
            vd_p1 = S.get(vd)

            @block.tensor
            def _(t):
                t.wait_ge(vd, vd_p1)
                t.transpose(out=pb(0)[0:1, :], in_=cols[:, 16:17],
                            identity=idf[:]).then_inc(td, 1)
            td_tr = S.add(td, 1)

            @block.vector
            def _(v):
                v = VD(v)
                v.wait_ge(td, td_tr)
                v.tensor_reduce(out=cols[0:1, 18:19], in_=pb(0)[0:1, :],
                                axis=AX.X, op=OP.add)
                v.tensor_scalar(out=cols[0:1, 18:19], in0=cols[0:1, 18:19],
                                scalar1=1.0 / (R * Cin), scalar2=1e-5,
                                op0=OP.mult, op1=OP.max)
                v.reciprocal(out=cols[0:1, 19:20], in_=cols[0:1, 18:19]
                             ).then_inc(vd, 1)
            vd_sc = S.add(vd, 1)

            @block.gpsimd
            def _(g):
                g.wait_ge(vd, vd_sc)
                g.dma_start(out=cols[:, sw_i:sw_i + 1],
                            in_=cols[0:1, 19:20].unsqueeze(1)
                            .broadcast_to([1, 128, 1])).then_inc(gd, 16)
                g.dma_start(out=cols[:, isw_i:isw_i + 1],
                            in_=cols[0:1, 18:19].unsqueeze(1)
                            .broadcast_to([1, 128, 1])).then_inc(gd, 16)
            gd_sc = S.add(gd, 32)

            ld_b2 = S.get(ld)
            vd_b2 = S.get(vd)

            @block.sync
            def _(s):
                s.wait_ge(vd, vd_p1)
                for it in range(ntile):
                    if it >= 1:
                        s.wait_ge(vd, vd_b2 + it)
                    s.dma_start(out=fb[it % 2][:, 0:Cin],
                                in_=w_nat[it * 128:(it + 1) * 128, :]).then_inc(ld, 16)
            S.add(ld, 16 * ntile)

            for it in range(ntile):
                sd_prev = S.get(sd)

                @block.vector
                def _(v, it=it, st_prev=S.get(st)):
                    v = VD(v)
                    v.wait_ge(st, st_prev)
                    if it == 0:
                        v.wait_ge(gd, gd_sc)
                    v.wait_ge(ld, ld_b2 + 16 * (it + 1))
                    if it >= 1:
                        v.wait_ge(td, S.get(td))
                    v.tensor_scalar(out=sbD[:, 0:Cin], in0=fb[it % 2][:, 0:Cin],
                                    scalar1=cols[:, sw_i:sw_i + 1], scalar2=MAGIC,
                                    op0=OP.mult, op1=OP.add)
                    v.tensor_scalar(out=sbD[:, 0:Cin], in0=sbD[:, 0:Cin],
                                    scalar1=MAGIC, scalar2=1.0,
                                    op0=OP.subtract, op1=OP.min)
                    v.tensor_scalar(out=sbD[:, 0:Cin], in0=sbD[:, 0:Cin],
                                    scalar1=-1.0, scalar2=0.0,
                                    op0=OP.max, op1=OP.bypass).then_inc(vd, 1)
                vq = S.add(vd, 1)

                emit_transposes(sbD[:, 0:Cin],
                                lambda f: sbB[:, (f % 8) * 128:(f % 8 + 1) * 128],
                                nch, extra_tensor_wait=(vd, vq))
                sd_now = S.get(sd)

                @block.gpsimd
                def _(g, it=it, sd_prev=sd_prev):
                    for f in range(nch):
                        g.wait_ge(sd, sd_prev + f + 1)
                        g.dma_start(
                            out=w_stash[f * 128:(f + 1) * 128,
                                        it * 128:(it + 1) * 128],
                            in_=sbB[:, (f % 8) * 128:(f % 8 + 1) * 128]
                            ).then_inc(st, 16)
                S.add(st, 16 * nch)
                if nch > 8:
                    pass

            return

        def l0_wq_big(w_nat, w_stash, R, Cin, sw_i, isw_i):
            ntile = R // 128
            nch = Cin // 128
            fb = [wb0[:].bitcast(F32), wb1[:].bitcast(F32)]
            ld_b = S.get(ld)
            vd_b = S.get(vd)

            @block.sync
            def _(s):
                s.wait_ge(vd, vd_b)
                for it in range(ntile):
                    if it >= 1:
                        s.wait_ge(vd, vd_b + it)
                    s.dma_start(out=fb[it % 2][:, 0:Cin],
                                in_=w_nat[it * 128:(it + 1) * 128, :]).then_inc(ld, 16)
            S.add(ld, 16 * ntile)

            @block.vector
            def _(v):
                v = VD(v)
                for it in range(ntile):
                    v.wait_ge(ld, ld_b + 16 * (it + 1))
                    v.tensor_reduce(out=cols[:, 17:18], in_=fb[it % 2][:, 0:Cin],
                                    axis=AX.X, op=OP.add, apply_absolute_value=True)
                    if it == 0:
                        v.tensor_copy(cols[:, 16:17], cols[:, 17:18]
                                      ).then_inc(vd, 1)
                    else:
                        v.tensor_tensor(out=cols[:, 16:17], in0=cols[:, 16:17],
                                        in1=cols[:, 17:18], op=OP.add
                                        ).then_inc(vd, 1)
            S.add(vd, ntile)
            vd_p1 = S.get(vd)

            @block.tensor
            def _(t):
                t.wait_ge(vd, vd_p1)
                t.transpose(out=pb(0)[0:1, :], in_=cols[:, 16:17],
                            identity=idf[:]).then_inc(td, 1)
            td_tr = S.add(td, 1)

            @block.vector
            def _(v):
                v = VD(v)
                v.wait_ge(td, td_tr)
                v.tensor_reduce(out=cols[0:1, 18:19], in_=pb(0)[0:1, :],
                                axis=AX.X, op=OP.add)
                v.tensor_scalar(out=cols[0:1, 18:19], in0=cols[0:1, 18:19],
                                scalar1=1.0 / (R * Cin), scalar2=1e-5,
                                op0=OP.mult, op1=OP.max)
                v.reciprocal(out=cols[0:1, 19:20], in_=cols[0:1, 18:19]
                             ).then_inc(vd, 1)
            vd_sc = S.add(vd, 1)

            @block.gpsimd
            def _(g):
                g.wait_ge(vd, vd_sc)
                g.dma_start(out=cols[:, sw_i:sw_i + 1],
                            in_=cols[0:1, 19:20].unsqueeze(1)
                            .broadcast_to([1, 128, 1])).then_inc(gd, 16)
                g.dma_start(out=cols[:, isw_i:isw_i + 1],
                            in_=cols[0:1, 18:19].unsqueeze(1)
                            .broadcast_to([1, 128, 1])).then_inc(gd, 16)
            gd_sc = S.add(gd, 32)

            ld_b2 = S.get(ld)
            vd_b2 = S.get(vd)

            @block.sync
            def _(s):
                s.wait_ge(vd, vd_p1)
                for it in range(ntile):
                    if it >= 1:
                        s.wait_ge(vd, vd_b2 + it)
                    s.dma_start(out=fb[it % 2][:, 0:Cin],
                                in_=w_nat[it * 128:(it + 1) * 128, :]).then_inc(ld, 16)
            S.add(ld, 16 * ntile)

            for it in range(ntile):
                @block.vector
                def _(v, it=it, st_prev=S.get(st)):
                    v = VD(v)
                    v.wait_ge(st, st_prev)
                    if it == 0:
                        v.wait_ge(gd, gd_sc)
                    v.wait_ge(ld, ld_b2 + 16 * (it + 1))
                    if it >= 1:
                        v.wait_ge(td, S.get(td))
                    v.tensor_scalar(out=sbD[:, 0:Cin], in0=fb[it % 2][:, 0:Cin],
                                    scalar1=cols[:, sw_i:sw_i + 1], scalar2=MAGIC,
                                    op0=OP.mult, op1=OP.add)
                    v.tensor_scalar(out=sbD[:, 0:Cin], in0=sbD[:, 0:Cin],
                                    scalar1=MAGIC, scalar2=1.0,
                                    op0=OP.subtract, op1=OP.min)
                    v.tensor_scalar(out=sbD[:, 0:Cin], in0=sbD[:, 0:Cin],
                                    scalar1=-1.0, scalar2=0.0,
                                    op0=OP.max, op1=OP.bypass).then_inc(vd, 1)
                vq = S.add(vd, 1)

                for g0 in range(0, nch, 8):
                    gn = min(8, nch - g0)
                    st_before = S.get(st)
                    sd_before = S.get(sd)
                    emit_transposes(sbD[:, g0 * 128:(g0 + gn) * 128],
                                    lambda f: sbB[:, f * 128:(f + 1) * 128],
                                    gn, extra_tensor_wait=(vd, vq) if g0 == 0
                                    else (st, st_before))

                    @block.gpsimd
                    def _(g, sd_before=sd_before, it=it, g0=g0, gn=gn):
                        for f in range(gn):
                            g.wait_ge(sd, sd_before + f + 1)
                            g.dma_start(
                                out=w_stash[(g0 + f) * 128:(g0 + f + 1) * 128,
                                            it * 128:(it + 1) * 128],
                                in_=sbB[:, f * 128:(f + 1) * 128]).then_inc(st, 16)
                    S.add(st, 16 * gn)

        if DO_WQ:
            l0_wq(wg0, wgi0, 4096, 1024, 20, 21)
            l0_wq(wu0, wui0, 4096, 1024, 22, 23)
            l0_wq_big(wd0, wdi0, 1024, 4096, 24, 25)
        wq_st = S.get(st)

        def emit_l0ffn():
            G = 2
            xiTb = sbE[:].bitcast(BF16)
            entry_td = S.get(td)
            for grp in range(min(FFN_NGRP, TC // G)):
                grp_td = S.get(td)
                grp_vd = S.get(vd)

                for j in range(G):
                    tc = grp * G + j
                    vpre = S.get(vd)
                    spre = S.get(sd)
                    hb = sbD[:, 0:4096] if j == 0 else sbC[:].bitcast(F32)[:, 0:4096]
                    xq = hb[:, 0:1024]

                    @block.vector
                    def _(v, tc=tc, j=j, xq=xq, vpre=vpre, spre=spre, grp=grp):
                        v = VD(v)
                        if grp == 0 and j == 0:
                            v.wait_ge(ld, ld_x)
                            v.wait_ge(td, entry_td)
                        xr = x_res[:, tc * D:(tc + 1) * D]
                        v.tensor_reduce(out=cols[:, 0:1], in_=xr, axis=AX.X,
                                        op=OP.add)
                        v.scalar_tensor_tensor(out=xq, in0=xr, scalar=1.0,
                                               in1=xr, op0=OP.mult, op1=OP.mult,
                                               accum_out=cols[:, 1:2])
                        v.tensor_scalar(out=cols[:, 2:3], in0=cols[:, 0:1],
                                        scalar1=1.0 / D, scalar2=0.0,
                                        op0=OP.mult, op1=OP.bypass)
                        v.tensor_scalar(out=cols[:, 3:4], in0=cols[:, 1:2],
                                        scalar1=1.0 / D, scalar2=0.0,
                                        op0=OP.mult, op1=OP.bypass)
                        v.tensor_tensor(out=cols[:, 4:5], in0=cols[:, 2:3],
                                        in1=cols[:, 2:3], op=OP.mult)
                        v.scalar_tensor_tensor(out=cols[:, 4:5], in0=cols[:, 3:4],
                                               scalar=LN_EPS, in1=cols[:, 4:5],
                                               op0=OP.add, op1=OP.subtract
                                               ).then_inc(vd, 1)
                        v.wait_ge(sd, spre + 1)
                        v.reciprocal(out=cols[:, 5:6], in_=cols[:, 4:5])
                        v.scalar_tensor_tensor(out=cols[:, 6:7], in0=cols[:, 2:3],
                                               scalar=-1.0, in1=cols[:, 5:6],
                                               op0=OP.mult, op1=OP.mult)
                        v.tensor_scalar(out=xq, in0=xr,
                                        scalar1=cols[:, 5:6], scalar2=cols[:, 6:7],
                                        op0=OP.mult, op1=OP.add)
                        v.tensor_reduce(out=cols[:, 8:9], in_=xq, axis=AX.X,
                                        op=OP.max, apply_absolute_value=True)
                        v.tensor_scalar(out=cols[:, 8:9], in0=cols[:, 8:9],
                                        scalar1=1e-5, scalar2=0.0,
                                        op0=OP.max, op1=OP.bypass)
                        v.reciprocal(out=cols[:, 9:10], in_=cols[:, 8:9])
                        v.tensor_scalar(out=cols[:, 10 + 2 * j:11 + 2 * j],
                                        in0=cols[:, 9:10], scalar1=127.0,
                                        scalar2=0.0, op0=OP.mult, op1=OP.bypass)
                        v.tensor_scalar(out=cols[:, 11 + 2 * j:12 + 2 * j],
                                        in0=cols[:, 8:9], scalar1=1.0 / 127.0,
                                        scalar2=0.0, op0=OP.mult, op1=OP.bypass)
                        v.tensor_scalar(out=xq, in0=xq,
                                        scalar1=cols[:, 10 + 2 * j:11 + 2 * j],
                                        scalar2=MAGIC, op0=OP.mult, op1=OP.add)
                        v.tensor_scalar(out=xq, in0=xq, scalar1=MAGIC,
                                        scalar2=127.0, op0=OP.subtract, op1=OP.min)
                        v.tensor_scalar(out=xq, in0=xq, scalar1=-128.0,
                                        scalar2=0.0, op0=OP.max, op1=OP.bypass
                                        ).then_inc(vd, 1)
                    vq = S.add(vd, 2)

                    @block.scalar
                    def _(sc, vpre=vpre):
                        sc = SD(sc)
                        sc.wait_ge(vd, vpre + 1)
                        sc.activation(out=cols[:, 4:5], in_=cols[:, 4:5],
                                      func=AF.Sqrt).then_inc(sd, 1)
                    S.add(sd, 1)

                    emit_transposes(
                        xq,
                        lambda f, j=j: xiTb[:, (j * 8 + f) * 128:
                                            (j * 8 + f + 1) * 128],
                        8, extra_tensor_wait=(vd, vq))

                prep_vd = S.get(vd)
                prep_sd = S.get(sd)
                prep_td = S.get(td)
                if FFN_STAGES < 2:
                    continue

                ld_mb = S.get(ld)

                @block.sync
                def _(s, grp_td=grp_td, grp=grp, st_e=S.get(st)):
                    if grp == 0:
                        s.wait_ge(st, max(wq_st, st_e))
                        s.wait_ge(vd, grp_vd)
                        s.wait_ge(td, grp_td)
                    else:
                        s.wait_ge(td, grp_td)
                    for m in range(8):
                        if m >= 2:
                            s.wait_ge(td, prep_td + 32 * (m - 1))
                        wsem = wl0 if m % 2 == 0 else wl1
                        for k in range(8):
                            s.dma_start(
                                out=wbufs[m % 2][:, k * 512:(k + 1) * 512],
                                in_=wgi0[k * 128:(k + 1) * 128,
                                         m * 512:(m + 1) * 512]).then_inc(wsem, 16)
                            s.dma_start(
                                out=wbufs[m % 2][:, 4096 + k * 512:
                                                 4096 + (k + 1) * 512],
                                in_=wui0[k * 128:(k + 1) * 128,
                                         m * 512:(m + 1) * 512]).then_inc(wsem, 16)
                for m in range(8):
                    S.add(wl0 if m % 2 == 0 else wl1, 256)

                @block.tensor
                def _(t, ld_mb=ld_mb, prep_vd=prep_vd, prep_sd=prep_sd,
                      prep_td=prep_td, wl0_b=S.get(wl0) - 4 * 256,
                      wl1_b=S.get(wl1) - 4 * 256):
                    t.wait_ge(vd, prep_vd)
                    t.wait_ge(sd, prep_sd)
                    for m in range(8):
                        if m % 2 == 0:
                            t.wait_ge(wl0, wl0_b + 256 * (m // 2 + 1))
                        else:
                            t.wait_ge(wl1, wl1_b + 256 * (m // 2 + 1))
                        if m >= 1:
                            t.wait_ge(vd, prep_vd + 4 + 3 * m)
                            t.wait_ge(sd, prep_sd + 2 * m)
                        for j in range(G):
                            for k in range(8):
                                t.matmul(bank(j),
                                         xiTb[:, (j * 8 + k) * 128:
                                              (j * 8 + k + 1) * 128],
                                         wbufs[m % 2][:, k * 512:(k + 1) * 512],
                                         start=(k == 0), stop=(k == 7))
                            for k in range(8):
                                mm = t.matmul(bank(2 + j),
                                              xiTb[:, (j * 8 + k) * 128:
                                                   (j * 8 + k + 1) * 128],
                                              wbufs[m % 2][:, 4096 + k * 512:
                                                           4096 + (k + 1) * 512],
                                              start=(k == 0), stop=(k == 7))
                                if j == G - 1 and k == 7:
                                    mm.then_inc(td, 32)
                S.add(td, 32 * 8)

                @block.scalar
                def _(sc, prep_td=prep_td, prep_vd=prep_vd):
                    sc = SD(sc)
                    sc.wait_ge(vd, prep_vd + 4)
                    for m in range(8):
                        sc.wait_ge(td, prep_td + 32 * (m + 1))
                        for j in range(G):
                            sc.activation(
                                out=sbB[:, (j * 8 + m) * 512:
                                        (j * 8 + m + 1) * 512],
                                in_=bank(j), func=AF.Silu,
                                scale=cols[:, 26 + j:27 + j]).then_inc(sd, 1)
                S.add(sd, 16)

                @block.vector
                def _(v, prep_sd=prep_sd):
                    v = VD(v)
                    v.tensor_tensor(out=cols[:, 26:27], in0=cols[:, 11:12],
                                    in1=cols[:, 21:22], op=OP.mult)
                    v.tensor_tensor(out=cols[:, 27:28], in0=cols[:, 13:14],
                                    in1=cols[:, 21:22], op=OP.mult)
                    v.tensor_tensor(out=cols[:, 28:29], in0=cols[:, 11:12],
                                    in1=cols[:, 23:24], op=OP.mult)
                    v.tensor_tensor(out=cols[:, 29:30], in0=cols[:, 13:14],
                                    in1=cols[:, 23:24], op=OP.mult)
                    v.memset(cols[:, 32:34], 0.0).then_inc(vd, 4)
                    for m in range(8):
                        for j in range(G):
                            v.wait_ge(sd, prep_sd + 2 * m + j + 1)
                            hb = (sbD[:, 0:4096] if j == 0
                                  else sbC[:].bitcast(F32)[:, 0:4096])
                            v.scalar_tensor_tensor(
                                out=hb[:, m * 512:(m + 1) * 512],
                                in0=bank(2 + j), scalar=cols[:, 28 + j:29 + j],
                                in1=sbB[:, (j * 8 + m) * 512:
                                        (j * 8 + m + 1) * 512],
                                op0=OP.mult, op1=OP.mult)
                            v.tensor_reduce(out=cols[:, 34:35],
                                            in_=hb[:, m * 512:(m + 1) * 512],
                                            axis=AX.X, op=OP.max,
                                            apply_absolute_value=True)
                            tt = v.tensor_tensor(out=cols[:, 32 + j:33 + j],
                                                 in0=cols[:, 32 + j:33 + j],
                                                 in1=cols[:, 34:35], op=OP.max)
                            if j == G - 1:
                                tt.then_inc(vd, 3)
                S.add(vd, 4 + 24)

                if FFN_STAGES < 3:
                    continue
                for j in range(G):
                    hb = sbD[:, 0:4096] if j == 0 else sbC[:].bitcast(F32)[:, 0:4096]

                    @block.vector
                    def _(v, j=j, hb=hb):
                        v = VD(v)
                        v.tensor_scalar(out=cols[:, 35:36],
                                        in0=cols[:, 32 + j:33 + j],
                                        scalar1=1e-5, scalar2=0.0,
                                        op0=OP.max, op1=OP.bypass)
                        v.reciprocal(out=cols[:, 36:37], in_=cols[:, 35:36])
                        v.tensor_scalar(out=cols[:, 37:38], in0=cols[:, 36:37],
                                        scalar1=127.0, scalar2=0.0,
                                        op0=OP.mult, op1=OP.bypass)
                        v.tensor_scalar(out=cols[:, 30 + j:31 + j],
                                        in0=cols[:, 35:36], scalar1=1.0 / 127.0,
                                        scalar2=0.0, op0=OP.mult, op1=OP.bypass)
                        v.tensor_scalar(out=hb, in0=hb, scalar1=cols[:, 37:38],
                                        scalar2=MAGIC, op0=OP.mult, op1=OP.add)
                        v.tensor_scalar(out=hb, in0=hb, scalar1=MAGIC,
                                        scalar2=127.0, op0=OP.subtract, op1=OP.min)
                        v.tensor_scalar(out=hb, in0=hb, scalar1=-128.0,
                                        scalar2=0.0, op0=OP.max, op1=OP.bypass
                                        ).then_inc(vd, 1)
                    vq = S.add(vd, 1)

                    emit_transposes(
                        hb,
                        lambda f, j=j: sbA[:, (f * G + j) * 128:
                                           (f * G + j + 1) * 128],
                        32, extra_tensor_wait=(vd, vq))

                hq_sd = S.get(sd)
                hq_td = S.get(td)

                if FFN_STAGES < 4:
                    continue
                ld_db = S.get(ld)

                @block.sync
                def _(s, ld_db=ld_db, hq_td=hq_td, prep_td=prep_td):
                    s.wait_ge(td, prep_td + 32 * 8)
                    dsems = [dl0, dl1, dl2, dl3]
                    for k in range(32):
                        if k >= 4:
                            s.wait_ge(td, hq_td + 4 * (k - 3))
                        s.dma_start(out=wbufs[(k % 4) // 2][:, (k % 2) * 1024:
                                                            (k % 2 + 1) * 1024],
                                    in_=wdi0[k * 128:(k + 1) * 128, :]
                                    ).then_inc(dsems[k % 4], 16)
                for k in range(32):
                    S.add([dl0, dl1, dl2, dl3][k % 4], 16)

                @block.tensor
                def _(t, ld_db=ld_db, hq_sd=hq_sd,
                      dbs=[S.get(x) - 8 * 16 for x in [dl0, dl1, dl2, dl3]]):
                    t.wait_ge(sd, hq_sd)
                    dsems = [dl0, dl1, dl2, dl3]
                    for k in range(32):
                        t.wait_ge(dsems[k % 4], dbs[k % 4] + 16 * (k // 4 + 1))
                        wv = wbufs[(k % 4) // 2][:, (k % 2) * 1024:
                                                 (k % 2 + 1) * 1024]
                        for j in range(G):
                            for mo in range(2):
                                mm = t.matmul(bank(j * 2 + mo),
                                              sbA[:, (k * G + j) * 128:
                                                  (k * G + j + 1) * 128],
                                              wv[:, mo * 512:(mo + 1) * 512],
                                              start=(k == 0), stop=(k == 31))
                                if j == G - 1 and mo == 1:
                                    mm.then_inc(td, 4)
                S.add(td, 4 * 32)
                dn_td = S.get(td)

                @block.vector
                def _(v, dn_td=dn_td, grp=grp):
                    v = VD(v)
                    v.wait_ge(td, dn_td)
                    for j in range(G):
                        tc = grp * G + j
                        v.tensor_tensor(out=cols[:, 38:39],
                                        in0=cols[:, 30 + j:31 + j],
                                        in1=cols[:, 25:26], op=OP.mult)
                        v.tensor_scalar(out=cols[:, 38:39], in0=cols[:, 38:39],
                                        scalar1=0.5, scalar2=0.0,
                                        op0=OP.mult, op1=OP.bypass)
                        for mo in range(2):
                            xr = x_res[:, tc * D + mo * 512:
                                       tc * D + (mo + 1) * 512]
                            stt = v.scalar_tensor_tensor(
                                out=xr, in0=bank(j * 2 + mo),
                                scalar=cols[:, 38:39], in1=xr,
                                op0=OP.mult, op1=OP.add)
                            if j == G - 1 and mo == 1:
                                stt.then_inc(vd, 1)
                S.add(vd, 1)



        def cast_stash(w_in, w_out, R, C, oc0=0):
            ntile = R // 128
            fview = wb0[:].bitcast(F32)
            stage = wb1
            ld_b = S.get(ld)
            vd_b = S.get(vd)
            st_b = S.get(st)

            @block.sync
            def _(s):
                s.wait_ge(vd, vd_b)
                s.wait_ge(td, S.get(td))
                for it in range(ntile):
                    if it >= 1:
                        s.wait_ge(vd, vd_b + it)
                    s.dma_start(out=fview[:, 0:C],
                                in_=w_in[it * 128:(it + 1) * 128, :]).then_inc(ld, 16)
            S.add(ld, 16 * ntile)

            @block.vector
            def _(v):
                v = VD(v)
                for it in range(ntile):
                    v.wait_ge(ld, ld_b + 16 * (it + 1))
                    if it >= 1:
                        v.wait_ge(st, st_b + 16 * it)
                    v.tensor_copy(stage[:, 0:C], fview[:, 0:C]).then_inc(vd, 1)
            S.add(vd, ntile)

            @block.gpsimd
            def _(g):
                for it in range(ntile):
                    g.wait_ge(vd, vd_b + it + 1)
                    g.dma_start(out=w_out[it * 128:(it + 1) * 128, oc0:oc0 + C],
                                in_=stage[:, 0:C]).then_inc(st, 16)
            S.add(st, 16 * ntile)

        def emit_l0attn():
            cast_stash(l0pw, l0pw_b, 1024, 392)
            cast_stash(l0wg_a, l0wg_ab, 1024, 1024)
            cast_stash(l0wo_a, l0wo_ab, 1024, 1024)

            entry_td = S.get(td)
            xhat = sbD[:, 0:1024]
            stg = sbA
            for tc in range(TC):
                vpre = S.get(vd)
                spre = S.get(sd)
                st_pre = S.get(st)

                @block.vector
                def _(v, tc=tc, vpre=vpre, spre=spre, st_pre=st_pre):
                    v = VD(v)
                    if tc == 0:
                        v.wait_ge(td, entry_td)
                    else:
                        v.wait_ge(st, st_pre)
                    xr = x_res[:, tc * D:(tc + 1) * D]
                    v.tensor_reduce(out=cols[:, 0:1], in_=xr, axis=AX.X, op=OP.add)
                    v.scalar_tensor_tensor(out=xhat, in0=xr, scalar=1.0,
                                           in1=xr, op0=OP.mult, op1=OP.mult,
                                           accum_out=cols[:, 1:2])
                    v.tensor_scalar(out=cols[:, 2:3], in0=cols[:, 0:1],
                                    scalar1=1.0 / D, scalar2=0.0,
                                    op0=OP.mult, op1=OP.bypass)
                    v.tensor_scalar(out=cols[:, 3:4], in0=cols[:, 1:2],
                                    scalar1=1.0 / D, scalar2=0.0,
                                    op0=OP.mult, op1=OP.bypass)
                    v.tensor_tensor(out=cols[:, 4:5], in0=cols[:, 2:3],
                                    in1=cols[:, 2:3], op=OP.mult)
                    v.scalar_tensor_tensor(out=cols[:, 4:5], in0=cols[:, 3:4],
                                           scalar=LN_EPS, in1=cols[:, 4:5],
                                           op0=OP.add, op1=OP.subtract
                                           ).then_inc(vd, 1)
                    v.wait_ge(sd, spre + 1)
                    v.reciprocal(out=cols[:, 5:6], in_=cols[:, 4:5])
                    v.scalar_tensor_tensor(out=cols[:, 6:7], in0=cols[:, 2:3],
                                           scalar=-1.0, in1=cols[:, 5:6],
                                           op0=OP.mult, op1=OP.mult)
                    v.tensor_scalar(out=xhat, in0=xr,
                                    scalar1=cols[:, 5:6], scalar2=cols[:, 6:7],
                                    op0=OP.mult, op1=OP.add).then_inc(vd, 1)

                @block.scalar
                def _(sc, vpre=vpre):
                    sc = SD(sc)
                    sc.wait_ge(vd, vpre + 1)
                    sc.activation(out=cols[:, 4:5], in_=cols[:, 4:5],
                                  func=AF.Sqrt).then_inc(sd, 1)
                S.add(vd, 2)
                S.add(sd, 1)
                vq = S.get(vd)

                emit_transposes(xhat,
                                lambda f: stg[:, f * 128:(f + 1) * 128],
                                8, extra_tensor_wait=(vd, vq))
                sd_now = S.get(sd)

                @block.gpsimd
                def _(g, tc=tc, sd_now=sd_now):
                    g.wait_ge(sd, sd_now)
                    for f in range(8):
                        g.dma_start(
                            out=agx_in[f * 128:(f + 1) * 128,
                                       tc * 128:(tc + 1) * 128],
                            in_=stg[:, f * 128:(f + 1) * 128]).then_inc(st, 16)
                S.add(st, 16 * 8)

            st_ag = S.get(st)

            @block.gpsimd
            def _(g):
                g.wait_ge(st, st_ag)
                g.collective_compute(
                    "AllGather", OP.bypass,
                    replica_groups=[list(range(NCORES))],
                    ins=[agx_in.ap().opt()],
                    outs=[agx_out.ap().opt()],
                ).then_inc(cc, 1)
            gd_ag = S.add(cc, 1)

            pw = sbC
            ld_pw = S.get(ld)

            @block.sync
            def _(s):
                s.wait_ge(vd, S.get(vd))
                for f in range(8):
                    s.dma_start(out=pw[:, f * 392:(f + 1) * 392],
                                in_=l0pw_b[f * 128:(f + 1) * 128, :]
                                ).then_inc(ld, 16)
            S.add(ld, 16 * 8)
            ld_pw_done = S.get(ld)

            nbt = 4 * 32
            td_proj0 = S.get(td)
            for ib in range(nbt):
                b = ib // 32
                tb = ib % 32
                r = 2 * b + (1 if tb >= 16 else 0)
                col0 = 128 * (tb % 16)
                xs = sbB[:, (ib % 2) * 1024:(ib % 2) * 1024 + 1024]
                ld_pre = S.get(ld)
                td_pre = S.get(td)
                vd_pre = S.get(vd)
                sd_pre = S.get(sd)
                st_pre = S.get(st)

                @block.sync
                def _(s, ib=ib, r=r, col0=col0, xs=xs):
                    if ib == 0:
                        s.wait_ge(cc, gd_ag)
                        s.wait_ge(ld, ld_pw_done)
                    if ib >= 2:
                        s.wait_ge(td, td_proj0 + 5 * (ib - 2) + 1)
                    xsem = dl0 if ib % 2 == 0 else dl1
                    for f in range(8):
                        s.dma_start(out=xs[:, f * 128:(f + 1) * 128],
                                    in_=agx_out[1024 * r + f * 128:
                                                1024 * r + (f + 1) * 128,
                                                col0:col0 + 128]).then_inc(xsem, 16)
                S.add(dl0 if ib % 2 == 0 else dl1, 16 * 8)

                @block.tensor
                def _(t, ib=ib, xs=xs, vd_pre=vd_pre,
                      sd_pre=sd_pre, xv=S.get(dl0 if ib % 2 == 0 else dl1)):
                    t.wait_ge(dl0 if ib % 2 == 0 else dl1, xv)
                    if ib >= 1:
                        t.wait_ge(vd, vd_pre)
                        t.wait_ge(sd, sd_pre)
                    for f in range(8):
                        mm = t.matmul(psA[:, 0:392], xs[:, f * 128:(f + 1) * 128],
                                      pw[:, f * 392:(f + 1) * 392],
                                      start=(f == 0), stop=(f == 7))
                    mm.then_inc(td, 1)
                S.add(td, 1)
                td_mm = S.get(td)

                @block.scalar
                def _(sc, td_mm=td_mm, vd_pre=vd_pre):
                    sc = SD(sc)
                    sc.wait_ge(td, td_mm)
                    for i in range(4):
                        sc.activation(out=sbD[:, 512:576], in_=psA[:, i * 64:(i + 1) * 64],
                                      func=AF.Square, accum_out=cols[:, 8 + i:9 + i])
                    sc.activation(out=cols[:, 12:14], in_=psA[:, 384:386],
                                  func=AF.Sigmoid).then_inc(sd, 1)
                S.add(sd, 1)
                sd_sq = S.get(sd)

                @block.vector
                def _(v, sd_sq=sd_sq):
                    v = VD(v)
                    v.wait_ge(sd, sd_sq)
                    v.tensor_copy(cols[:, 16:20], cols[:, 8:12]).then_inc(vd, 1)
                S.add(vd, 1)
                vd_cp = S.get(vd)

                @block.scalar
                def _(sc, vd_cp=vd_cp):
                    sc = SD(sc)
                    sc.wait_ge(vd, vd_cp)
                    sc.activation(out=cols[:, 16:20], in_=cols[:, 16:20],
                                  func=AF.Sqrt).then_inc(sd, 1)
                S.add(sd, 1)
                sd_sqrt = S.get(sd)

                @block.vector
                def _(v, sd_sqrt=sd_sqrt, ib=ib, st_pre=st_pre):
                    v = VD(v)
                    v.wait_ge(sd, sd_sqrt)
                    if ib >= 1:
                        v.wait_ge(st, st_pre)
                    v.tensor_scalar(out=cols[:, 16:20], in0=cols[:, 16:20],
                                    scalar1=1e-12, scalar2=0.0,
                                    op0=OP.max, op1=OP.bypass)
                    v.reciprocal(out=cols[:, 16:20], in_=cols[:, 16:20])
                    v.tensor_scalar(out=cols[:, 14:16], in0=cols[:, 12:14],
                                    scalar1=-1.0, scalar2=0.0,
                                    op0=OP.mult, op1=OP.bypass)
                    for i in range(4):
                        v.tensor_scalar(out=sbD[:, i * 64:(i + 1) * 64],
                                        in0=psA[:, i * 64:(i + 1) * 64],
                                        scalar1=cols[:, 16 + i:17 + i], scalar2=0.0,
                                        op0=OP.mult, op1=OP.bypass)
                    for hh in range(2):
                        v.tensor_scalar(out=sbD[:, (4 + hh) * 64:(5 + hh) * 64],
                                        in0=sbD[:, (2 + hh) * 64:(3 + hh) * 64],
                                        scalar1=cols[:, 14 + hh:15 + hh], scalar2=0.0,
                                        op0=OP.mult, op1=OP.bypass)
                    for hh in range(2):
                        tsv = v.tensor_scalar(out=sbD[:, (6 + hh) * 64:(7 + hh) * 64],
                                              in0=psA[:, (4 + hh) * 64:(5 + hh) * 64],
                                              scalar1=cols[:, 12 + hh:13 + hh],
                                              scalar2=0.0,
                                              op0=OP.mult, op1=OP.bypass)
                        if hh == 1:
                            tsv.then_inc(vd, 1)
                S.add(vd, 1)
                vq = S.get(vd)

                emit_transposes(sbD[:, 0:512],
                                lambda f: stg[:, 1024 + f * 128:1024 + (f + 1) * 128],
                                4, extra_tensor_wait=(vd, vq))
                sd_tr = S.get(sd)

                @block.gpsimd
                def _(g, b=b, tb=tb, sd_tr=sd_tr):
                    g.wait_ge(sd, sd_tr)
                    for f, dst in enumerate([scanQ, scanK, scanBK, scanBV]):
                        for hh in range(2):
                            p = b * 2 + hh
                            g.dma_start(
                                out=dst[p * 64:(p + 1) * 64,
                                        tb * 128:(tb + 1) * 128],
                                in_=stg[hh * 64:(hh + 1) * 64,
                                        1024 + f * 128:1024 + (f + 1) * 128]
                            ).then_inc(st, 16)
                S.add(st, 16 * 8)

            proj_st = S.get(st)
            proj_vd = S.get(vd)
            proj_sd = S.get(sd)
            proj_td = S.get(td)

            sbDb = sbD[:].bitcast(BF16)
            sbEb = sbE[:].bitcast(BF16)
            zcol = cols[:, 60:61]
            onec = cols[:, 61:62]

            @block.vector
            def _(v):
                v = VD(v)
                v.wait_ge(st, proj_st)
                v.wait_ge(td, proj_td)
                v.wait_ge(sd, proj_sd)
                v.memset(zcol, 0.0)
                v.memset(onec, 1.0).then_inc(vd, 1)
            S.add(vd, 1)
            scan_v0 = S.get(vd)

            for p in range(8):
                ld_p = S.get(ld)

                @block.sync
                def _(s, p=p, ld_p=ld_p):
                    if p == 0:
                        s.wait_ge(vd, scan_v0)
                    else:
                        s.wait_ge(vd, S.get(vd))
                    for half in range(2):
                        s.dma_start(out=sbA[half * 64:(half + 1) * 64, 0:4096],
                                    in_=scanK[p * 64:(p + 1) * 64, :]
                                    ).then_inc(ld, 16)
                        s.dma_start(out=sbA[half * 64:(half + 1) * 64, 4096:8192],
                                    in_=scanQ[p * 64:(p + 1) * 64, :]
                                    ).then_inc(ld, 16)
                    s.dma_start(out=sbB[0:64, 0:4096],
                                in_=scanBK[p * 64:(p + 1) * 64, :]).then_inc(ld, 16)
                    s.dma_start(out=sbB[64:128, 0:4096],
                                in_=scanBV[p * 64:(p + 1) * 64, :]).then_inc(ld, 16)
                S.add(ld, 16 * 6)
                ld_pair = S.get(ld)

                pe_sd = S.get(sd)
                vbases = []
                evac_end = [None] * 32
                last_evac_end = pe_sd

                def _bkr(tau):
                    return sbB[:, 4096:8192] if tau % 2 == 0 else wb1[:, 0:4096]

                def _bvr(tau):
                    return sbC[:, 0:4096] if tau % 2 == 0 else wb1[:, 4096:8192]

                def emit_repl(tau, first, p=p, ld_pair=ld_pair):
                    nonlocal last_evac_end
                    td_r0 = S.get(td)
                    evac_base = S.get(sd)
                    vd_guard = (vbases[tau - 2] + 2) if tau >= 2 else 0
                    sd_guard = last_evac_end

                    @block.tensor
                    def _(t, tau=tau, first=first, td_r0=td_r0,
                          evac_base=evac_base, vd_guard=vd_guard,
                          sd_guard=sd_guard, ld_pair=ld_pair):
                        if first:
                            t.wait_ge(ld, ld_pair)
                        if vd_guard:
                            t.wait_ge(vd, vd_guard)
                        t.wait_ge(sd, sd_guard)
                        for c in range(8):
                            if c >= 2:
                                t.wait_ge(sd, evac_base + 2 * (c - 1))
                            t.matmul(bank(4 + c % 2),
                                     eslS[0:64, tau * 128:(tau + 1) * 128],
                                     sbB[0:64, c * 512:(c + 1) * 512],
                                     start=True, stop=True).then_inc(td, 1)
                            t.matmul(psB[:, (c % 2) * 512:(c % 2) * 512 + 512],
                                     eslS[64:128, tau * 128:(tau + 1) * 128],
                                     sbB[64:128, c * 512:(c + 1) * 512],
                                     start=True, stop=True).then_inc(td, 1)
                    S.add(td, 16)

                    @block.scalar
                    def _(sc, tau=tau, td_r0=td_r0):
                        sc = SD(sc)
                        bkr, bvr = _bkr(tau), _bvr(tau)
                        for c in range(8):
                            sc.wait_ge(td, td_r0 + 2 * c + 1)
                            sc.copy(out=bkr[:, c * 512:(c + 1) * 512],
                                    in_=bank(4 + c % 2)).then_inc(sd, 1)
                            sc.wait_ge(td, td_r0 + 2 * c + 2)
                            sc.copy(out=bvr[:, c * 512:(c + 1) * 512],
                                    in_=psB[:, (c % 2) * 512:(c % 2) * 512 + 512]
                                    ).then_inc(sd, 1)
                    S.add(sd, 16)
                    last_evac_end = S.get(sd)
                    evac_end[tau] = last_evac_end

                emit_repl(0, True)
                emit_repl(1, False)

                for tau in range(32):
                    vd_pre = S.get(vd)
                    sd_pre = S.get(sd)
                    vbases.append(vd_pre)

                    @block.vector
                    def _(v, p=p, tau=tau, vd_pre=vd_pre, sd_pre=sd_pre,
                          ev_end=evac_end[tau]):
                        v = VD(v)
                        bkr, bvr = _bkr(tau), _bvr(tau)
                        v.wait_ge(sd, ev_end)
                        v.tensor_tensor(out=sbD[:, 0:4096], in0=sbA[:, 0:4096],
                                        in1=bkr[:], op=OP.mult).then_inc(vd, 1)
                        v.tensor_tensor(out=sbC[:, 4096:8192], in0=sbA[:, 0:4096],
                                        in1=bvr[:], op=OP.mult).then_inc(vd, 1)
                        v.wait_ge(sd, sd_pre + 1)
                        v.tensor_tensor_scan(out=sbEb[:, 0:4096],
                                             data0=sbD[:, 0:4096],
                                             data1=sbC[:, 4096:8192],
                                             initial=zcol,
                                             op0=OP.mult, op1=OP.add
                                             ).then_inc(vd, 1)
                        v.tensor_tensor(out=wb0[:, 4096:8192],
                                        in0=sbEb[:, 0:4096],
                                        in1=sbA[:, 4096:8192], op=OP.mult
                                        ).then_inc(vd, 1)
                    S.add(vd, 4)

                    @block.scalar
                    def _(sc, vd_pre=vd_pre):
                        sc = SD(sc)
                        sc.wait_ge(vd, vd_pre + 1)
                        sc.activation(out=sbD[:, 0:4096], in_=sbD[:, 0:4096],
                                      func=AF.Identity, bias=onec,
                                      scale=1.0).then_inc(sd, 1)
                    S.add(sd, 1)

                    if tau + 2 <= 31:
                        emit_repl(tau + 2, False)

                    @block.tensor
                    def _(t, tau=tau, vd_pre=vd_pre, pe_sd=pe_sd):
                        t.wait_ge(vd, vd_pre + 4)
                        if tau == 0:
                            t.wait_ge(sd, pe_sd)
                        for m in range(8):
                            pbank = psA[(m // 4) * 64:(m // 4) * 64 + 64,
                                        (m % 4) * 512:(m % 4 + 1) * 512]
                            mm = t.matmul(pbank,
                                          esl[:, 62 - 2 * tau:126 - 2 * tau],
                                          wb0[:, 4096 + m * 512:4096 + (m + 1) * 512],
                                          start=(tau == 0), stop=(tau == 31),
                                          skip_group_check=True)
                            if m == 7:
                                mm.then_inc(td, 1)
                    S.add(td, 1)

                td_pair = S.get(td)

                @block.scalar
                def _(sc, td_pair=td_pair):
                    sc = SD(sc)
                    sc.wait_ge(td, td_pair)
                    for m in range(8):
                        pbank = psA[(m // 4) * 64:(m // 4) * 64 + 64,
                                    (m % 4) * 512:(m % 4 + 1) * 512]
                        sc.copy(out=wb0[(m // 4) * 64:(m // 4) * 64 + 64,
                                        (m % 4) * 512:(m % 4 + 1) * 512],
                                in_=pbank).then_inc(sd, 1)
                S.add(sd, 8)

                sd_pair = S.get(sd)
                b = p // 2
                hh = p % 2

                @block.gpsimd
                def _(g, b=b, hh=hh, sd_pair=sd_pair):
                    g.wait_ge(sd, sd_pair)
                    for jh in range(2):
                        j = 2 * b + jh
                        g.dma_start(
                            out=a2a0_in[j * 128 + hh * 64:j * 128 + (hh + 1) * 64, :],
                            in_=wb0[jh * 64:(jh + 1) * 64, 0:2048]
                            ).then_inc(st, 16)
                S.add(st, 16 * 2)


            st_a2a = S.get(st)

            @block.gpsimd
            def _(g):
                g.wait_ge(st, st_a2a)
                with allow_ici_all_to_all():
                    g.collective_compute(
                        "AllToAll", OP.bypass,
                        replica_groups=[list(range(NCORES))],
                        ins=[a2a0_in.ap().opt()],
                        outs=[a2a0_out.ap().opt()],
                    ).then_inc(cc, 1)
            gd_a2a = S.add(cc, 1)

            sbEf = sbE
            for tc in range(TC):
                ld_pre = S.get(ld)
                vd_pre = S.get(vd)
                sd_pre = S.get(sd)
                td_pre = S.get(td)

                @block.sync
                def _(s, tc=tc, ld_pre=ld_pre, vd_pre=vd_pre, td_pre=td_pre):
                    if tc == 0:
                        s.wait_ge(cc, gd_a2a)
                    else:
                        s.wait_ge(vd, vd_pre)
                    for f in range(8):
                        s.dma_start(out=sbB[:, f * 128:(f + 1) * 128],
                                    in_=a2a0_out[f * 128:(f + 1) * 128,
                                                 tc * 128:(tc + 1) * 128]
                                    ).then_inc(ld, 16)
                    for f in range(8):
                        s.dma_start(out=sbB[:, 1024 + f * 128:1024 + (f + 1) * 128],
                                    in_=agx_in[f * 128:(f + 1) * 128,
                                               tc * 128:(tc + 1) * 128]
                                    ).then_inc(ld, 16)
                    for m in range(2):
                        for f in range(8):
                            s.dma_start(
                                out=wb1[:, (m * 8 + f) * 512:(m * 8 + f + 1) * 512],
                                in_=l0wg_ab[f * 128:(f + 1) * 128,
                                            m * 512:(m + 1) * 512]).then_inc(ld, 16)
                S.add(ld, 16 * 32)
                ld_ao = ld_pre + 16 * 8
                ld_xh = ld_pre + 16 * 16
                ld_wg = ld_pre + 16 * 32

                @block.vector
                def _(v, tc=tc, ld_pre=ld_pre, sd_pre=sd_pre, td_pre=td_pre):
                    v = VD(v)
                    v.wait_ge(ld, ld_pre + 16 * 32)
                    v.tensor_copy(sbEf[:, 0:1024], sbB[:, 0:1024]).then_inc(vd, 1)
                S.add(vd, 1)
                vq = S.get(vd)

                emit_transposes(sbEf[:, 0:1024],
                                lambda f: sbD[:, f * 128:(f + 1) * 128],
                                8, extra_tensor_wait=(vd, vq))
                sd_ao = S.get(sd)

                @block.vector
                def _(v, tc=tc, sd_ao=sd_ao, td_pre=td_pre):
                    v = VD(v)
                    v.wait_ge(sd, sd_ao)
                    ao = sbD[:, 0:1024]
                    v.tensor_reduce(out=cols[:, 0:1], in_=ao, axis=AX.X, op=OP.add)
                    v.scalar_tensor_tensor(out=sbD[:, 1024:2048], in0=ao, scalar=1.0,
                                           in1=ao, op0=OP.mult, op1=OP.mult,
                                           accum_out=cols[:, 1:2])
                    v.tensor_scalar(out=cols[:, 2:3], in0=cols[:, 0:1],
                                    scalar1=1.0 / D, scalar2=0.0,
                                    op0=OP.mult, op1=OP.bypass)
                    v.tensor_scalar(out=cols[:, 3:4], in0=cols[:, 1:2],
                                    scalar1=1.0 / D, scalar2=0.0,
                                    op0=OP.mult, op1=OP.bypass)
                    v.tensor_tensor(out=cols[:, 4:5], in0=cols[:, 2:3],
                                    in1=cols[:, 2:3], op=OP.mult)
                    v.scalar_tensor_tensor(out=cols[:, 4:5], in0=cols[:, 3:4],
                                           scalar=LN_EPS, in1=cols[:, 4:5],
                                           op0=OP.add, op1=OP.subtract
                                           ).then_inc(vd, 1)
                S.add(vd, 1)
                vd_ln = S.get(vd)

                @block.scalar
                def _(sc, vd_ln=vd_ln):
                    sc = SD(sc)
                    sc.wait_ge(vd, vd_ln)
                    sc.activation(out=cols[:, 4:5], in_=cols[:, 4:5],
                                  func=AF.Sqrt).then_inc(sd, 1)
                S.add(sd, 1)
                sd_sq2 = S.get(sd)

                @block.vector
                def _(v, sd_sq2=sd_sq2):
                    v = VD(v)
                    v.wait_ge(sd, sd_sq2)
                    ao = sbD[:, 0:1024]
                    v.reciprocal(out=cols[:, 5:6], in_=cols[:, 4:5])
                    v.scalar_tensor_tensor(out=cols[:, 6:7], in0=cols[:, 2:3],
                                           scalar=-1.0, in1=cols[:, 5:6],
                                           op0=OP.mult, op1=OP.mult)
                    v.tensor_scalar(out=ao, in0=ao,
                                    scalar1=cols[:, 5:6], scalar2=cols[:, 6:7],
                                    op0=OP.mult, op1=OP.add)
                    v.tensor_tensor(out=ao, in0=ao, in1=lngb[:, 0:1024],
                                    op=OP.mult)
                    v.tensor_tensor(out=ao, in0=ao, in1=lngb[:, 1024:2048],
                                    op=OP.add).then_inc(vd, 1)
                S.add(vd, 1)
                vd_ao = S.get(vd)

                @block.tensor
                def _(t, ld_pre=ld_pre, vd_pre=vd_pre, sd_pre=sd_pre, tc=tc):
                    t.wait_ge(ld, ld_pre + 16 * 32)
                    if tc >= 1:
                        t.wait_ge(vd, vd_pre)
                    for m in range(2):
                        for f in range(8):
                            mm = t.matmul(psA[:, m * 512:(m + 1) * 512],
                                          sbB[:, 1024 + f * 128:1024 + (f + 1) * 128],
                                          wb1[:, (m * 8 + f) * 512:(m * 8 + f + 1) * 512],
                                          start=(f == 0), stop=(f == 7))
                    mm.then_inc(td, 1)
                S.add(td, 1)
                td_g = S.get(td)

                @block.scalar
                def _(sc, td_g=td_g):
                    sc = SD(sc)
                    sc.wait_ge(td, td_g)
                    for m in range(2):
                        sc.activation(out=sbA[:, 1024 + m * 512:1024 + (m + 1) * 512],
                                      in_=psA[:, m * 512:(m + 1) * 512],
                                      func=AF.Silu).then_inc(sd, 1)
                S.add(sd, 2)
                sd_g = S.get(sd)

                @block.vector
                def _(v, sd_g=sd_g, vd_ao=vd_ao):
                    v = VD(v)
                    v.wait_ge(sd, sd_g)
                    v.tensor_tensor(out=sbD[:, 1024:2048], in0=sbD[:, 0:1024],
                                    in1=sbA[:, 1024:2048], op=OP.mult
                                    ).then_inc(vd, 1)
                S.add(vd, 1)
                vq2 = S.get(vd)

                emit_transposes(sbD[:, 1024:2048],
                                lambda f: sbA[:, f * 128:(f + 1) * 128],
                                8, extra_tensor_wait=(vd, vq2))
                sd_pt = S.get(sd)

                ld_w2 = S.get(ld)

                @block.sync
                def _(s, ld_w2=ld_w2, sd_pt=sd_pt, td_pre=td_pre):
                    s.wait_ge(sd, sd_pt)
                    for m in range(2):
                        for f in range(8):
                            s.dma_start(
                                out=wb1[:, (m * 8 + f) * 512:(m * 8 + f + 1) * 512],
                                in_=l0wo_ab[f * 128:(f + 1) * 128,
                                            m * 512:(m + 1) * 512]).then_inc(ld, 16)
                S.add(ld, 16 * 16)
                ld_wo = S.get(ld)

                @block.tensor
                def _(t, ld_wo=ld_wo, sd_pt=sd_pt):
                    t.wait_ge(ld, ld_wo)
                    t.wait_ge(sd, sd_pt)
                    for m in range(2):
                        for f in range(8):
                            mm = t.matmul(psA[:, 1024 + m * 512:1024 + (m + 1) * 512],
                                          sbA[:, f * 128:(f + 1) * 128],
                                          wb1[:, (m * 8 + f) * 512:(m * 8 + f + 1) * 512],
                                          start=(f == 0), stop=(f == 7))
                    mm.then_inc(td, 1)
                S.add(td, 1)
                td_o = S.get(td)

                @block.vector
                def _(v, td_o=td_o, tc=tc):
                    v = VD(v)
                    v.wait_ge(td, td_o)
                    for m in range(2):
                        xr = x_res[:, tc * D + m * 512:tc * D + (m + 1) * 512]
                        v.tensor_tensor(out=xr, in0=psA[:, 1024 + m * 512:
                                                        1024 + (m + 1) * 512],
                                        in1=xr, op=OP.add).then_inc(vd, 1)
                S.add(vd, 2)

        MAGICF = MAGIC

        def emit_fp4(v, x_ap, dst, C, u, t1, t2, sc0):
            nb = C // 16
            bm = sbE[:, sc0:sc0 + nb]
            sc_ = sbE[:, sc0 + nb:sc0 + 2 * nb]
            f1 = sbE[:, sc0 + 2 * nb:sc0 + 3 * nb]
            sh = sbE[:, sc0 + 3 * nb:sc0 + 4 * nb]
            x3 = x_ap.rearrange("p (b n) -> p b n", n=16)
            v.tensor_reduce(out=bm, in_=x3, axis=AX.X, op=OP.max,
                            apply_absolute_value=True)
            v.tensor_scalar(out=sc_, in0=bm, scalar1=1.0 / 6.0, scalar2=1e-12,
                            op0=OP.mult, op1=OP.max)
            v.reciprocal(out=f1, in_=sc_)
            v.tensor_scalar(out=sh, in0=sc_, scalar1=0.5, scalar2=0.0,
                            op0=OP.mult, op1=OP.bypass)
            v.tensor_scalar(out=f1, in0=f1, scalar1=2.0, scalar2=0.0,
                            op0=OP.mult, op1=OP.bypass)
            ub = u.rearrange("p (b n) -> p b n", n=16)
            v.tensor_tensor(out=ub, in0=x3,
                            in1=f1[:].unsqueeze(2).broadcast_to([128, nb, 16]),
                            op=OP.mult)
            v.tensor_scalar(out=t1, in0=u, scalar1=MAGICF, scalar2=MAGICF,
                            op0=OP.add, op1=OP.subtract)
            v.tensor_scalar(out=dst, in0=t1, scalar1=4.0, scalar2=-4.0,
                            op0=OP.min, op1=OP.max)
            v.tensor_scalar(out=t1, in0=u, scalar1=0.5, scalar2=MAGICF,
                            op0=OP.mult, op1=OP.add)
            v.tensor_scalar(out=t1, in0=t1, scalar1=MAGICF, scalar2=2.0,
                            op0=OP.subtract, op1=OP.mult)
            v.tensor_scalar(out=t2, in0=t1, scalar1=8.0, scalar2=-8.0,
                            op0=OP.min, op1=OP.max)
            v.tensor_tensor(out=dst, in0=dst, in1=t2, op=OP.add)
            v.tensor_scalar(out=t2, in0=t1, scalar1=4.0, scalar2=-4.0,
                            op0=OP.min, op1=OP.max)
            v.tensor_tensor(out=dst, in0=dst, in1=t2, op=OP.subtract)
            v.tensor_scalar(out=t1, in0=u, scalar1=0.25, scalar2=MAGICF,
                            op0=OP.mult, op1=OP.add)
            v.tensor_scalar(out=t1, in0=t1, scalar1=MAGICF, scalar2=4.0,
                            op0=OP.subtract, op1=OP.mult)
            v.tensor_scalar(out=t2, in0=t1, scalar1=12.0, scalar2=-12.0,
                            op0=OP.min, op1=OP.max)
            v.tensor_tensor(out=dst, in0=dst, in1=t2, op=OP.add)
            v.tensor_scalar(out=t2, in0=t1, scalar1=8.0, scalar2=-8.0,
                            op0=OP.min, op1=OP.max)
            v.tensor_tensor(out=dst, in0=dst, in1=t2, op=OP.subtract)
            db = dst.rearrange("p (b n) -> p b n", n=16)
            v.tensor_tensor(out=db, in0=db,
                            in1=sh[:].unsqueeze(2).broadcast_to([128, nb, 16]),
                            op=OP.mult)

        def l1_wq(w_nat, w_stash, R, C):
            ntile = R // 128
            nch = C // 128
            fw = wb0[:].bitcast(F32)
            u_ = sbD[:, 0:C]
            t1_ = wb1[:].bitcast(F32)[:, 0:C]
            t2_ = sbA[:].bitcast(F32)[:, 0:C]
            dst_ = sbC[:].bitcast(F32)[:, 0:C]
            ld_b = S.get(ld)
            vd_b = S.get(vd)

            @block.sync
            def _(s):
                s.wait_ge(vd, vd_b)
                s.wait_ge(td, S.get(td))
                s.wait_ge(st, S.get(st))
                for it in range(ntile):
                    if it >= 1:
                        s.wait_ge(vd, vd_b + it)
                    s.dma_start(out=fw[:, 0:C],
                                in_=w_nat[it * 128:(it + 1) * 128, :]).then_inc(ld, 16)
            S.add(ld, 16 * ntile)

            for it in range(ntile):
                st_pre = S.get(st)

                @block.vector
                def _(v, it=it, st_pre=st_pre):
                    v = VD(v)
                    v.wait_ge(ld, ld_b + 16 * (it + 1))
                    if it >= 1:
                        v.wait_ge(td, S.get(td))
                        v.wait_ge(st, st_pre)
                    emit_fp4(v, fw[:, 0:C], dst_, C, u_, t1_, t2_, 1024)
                    v.memset(cols[:, 40:41], 0.0).then_inc(vd, 1)
                S.add(vd, 1)
                vq = S.get(vd)

                for g0 in range(0, nch, 8):
                    gn = min(8, nch - g0)
                    sd_b2 = S.get(sd)
                    emit_transposes(dst_[:, g0 * 128:(g0 + gn) * 128],
                                    lambda f: sbB[:, f * 128:(f + 1) * 128],
                                    gn, extra_tensor_wait=(vd, vq) if g0 == 0
                                    else (st, S.get(st)))

                    @block.gpsimd
                    def _(g, it=it, g0=g0, gn=gn, sd_b2=sd_b2):
                        for f in range(gn):
                            g.wait_ge(sd, sd_b2 + f + 1)
                            g.dma_start(
                                out=w_stash[(g0 + f) * 128:(g0 + f + 1) * 128,
                                            it * 128:(it + 1) * 128],
                                in_=sbB[:, f * 128:(f + 1) * 128]).then_inc(st, 16)
                    S.add(st, 16 * gn)

        def emit_l1ffn():
            G = 2
            INNER = 2816
            MSL = [512, 512, 512, 512, 512, 256]
            xqTb = sbE[:].bitcast(BF16)
            for grp in range(TC // G):
                grp_td = S.get(td)
                grp_vd = S.get(vd)
                grp_st = S.get(st)

                for j in range(G):
                    tc = grp * G + j
                    vpre = S.get(vd)
                    spre = S.get(sd)
                    xhbuf = sbD[:, 0:1024]
                    xqf = sbC[:].bitcast(F32)[:, 0:1024] if j == 0 else                         sbC[:].bitcast(F32)[:, 1024:2048]

                    @block.vector
                    def _(v, tc=tc, j=j, vpre=vpre, spre=spre, grp=grp,
                          xqf=xqf, xhbuf=xhbuf):
                        v = VD(v)
                        if grp == 0 and j == 0:
                            v.wait_ge(td, grp_td)
                            v.wait_ge(st, grp_st)
                        xr = x_res[:, tc * D:(tc + 1) * D]
                        v.tensor_reduce(out=cols[:, 0:1], in_=xr, axis=AX.X,
                                        op=OP.add)
                        v.scalar_tensor_tensor(out=xhbuf, in0=xr, scalar=1.0,
                                               in1=xr, op0=OP.mult, op1=OP.mult,
                                               accum_out=cols[:, 1:2])
                        v.tensor_scalar(out=cols[:, 2:3], in0=cols[:, 0:1],
                                        scalar1=1.0 / D, scalar2=0.0,
                                        op0=OP.mult, op1=OP.bypass)
                        v.tensor_scalar(out=cols[:, 3:4], in0=cols[:, 1:2],
                                        scalar1=1.0 / D, scalar2=0.0,
                                        op0=OP.mult, op1=OP.bypass)
                        v.tensor_tensor(out=cols[:, 4:5], in0=cols[:, 2:3],
                                        in1=cols[:, 2:3], op=OP.mult)
                        v.scalar_tensor_tensor(out=cols[:, 4:5], in0=cols[:, 3:4],
                                               scalar=LN_EPS, in1=cols[:, 4:5],
                                               op0=OP.add, op1=OP.subtract
                                               ).then_inc(vd, 1)
                        v.wait_ge(sd, spre + 1)
                        v.reciprocal(out=cols[:, 5:6], in_=cols[:, 4:5])
                        v.scalar_tensor_tensor(out=cols[:, 6:7], in0=cols[:, 2:3],
                                               scalar=-1.0, in1=cols[:, 5:6],
                                               op0=OP.mult, op1=OP.mult)
                        v.tensor_scalar(out=xhbuf, in0=xr,
                                        scalar1=cols[:, 5:6], scalar2=cols[:, 6:7],
                                        op0=OP.mult, op1=OP.add)
                        emit_fp4(v, xhbuf, xqf, 1024,
                                 sbD[:, 1024:2048], sbD[:, 2048:3072],
                                 sbD[:, 3072:4096], 1024)
                        v.memset(cols[:, 41:42], 0.0).then_inc(vd, 1)

                    @block.scalar
                    def _(sc, vpre=vpre):
                        sc = SD(sc)
                        sc.wait_ge(vd, vpre + 1)
                        sc.activation(out=cols[:, 4:5], in_=cols[:, 4:5],
                                      func=AF.Sqrt).then_inc(sd, 1)
                    S.add(vd, 2)
                    S.add(sd, 1)
                    vq = S.get(vd)

                    emit_transposes(
                        xqf,
                        lambda f, j=j: xqTb[:, (j * 8 + f) * 128:
                                            (j * 8 + f + 1) * 128],
                        8, extra_tensor_wait=(vd, vq))

                prep_vd = S.get(vd)
                prep_sd = S.get(sd)
                prep_td = S.get(td)

                @block.sync
                def _(s, grp_td=grp_td, grp=grp, prep_td=prep_td,
                      prep_vd=prep_vd):
                    s.wait_ge(td, grp_td)
                    for m in range(6):
                        if m >= 2:
                            s.wait_ge(td, prep_td + (m - 1))
                        wsem = wl0 if m % 2 == 0 else wl1
                        msl = MSL[m]
                        for k in range(8):
                            s.dma_start(
                                out=wbufs[m % 2][:, k * 512:k * 512 + msl],
                                in_=wgq1[k * 128:(k + 1) * 128,
                                         m * 512:m * 512 + msl]).then_inc(wsem, 16)
                            s.dma_start(
                                out=wbufs[m % 2][:, 4096 + k * 512:
                                                 4096 + k * 512 + msl],
                                in_=wuq1[k * 128:(k + 1) * 128,
                                         m * 512:m * 512 + msl]).then_inc(wsem, 16)
                for m in range(6):
                    S.add(wl0 if m % 2 == 0 else wl1, 256)

                @block.tensor
                def _(t, prep_vd=prep_vd, prep_sd=prep_sd, prep_td=prep_td,
                      wlb=[S.get(wl0) - 3 * 256, S.get(wl1) - 3 * 256]):
                    t.wait_ge(vd, prep_vd)
                    t.wait_ge(sd, prep_sd)
                    for m in range(6):
                        t.wait_ge(wl0 if m % 2 == 0 else wl1,
                                  wlb[m % 2] + 256 * (m // 2 + 1))
                        if m >= 1:
                            t.wait_ge(vd, prep_vd + m)
                            t.wait_ge(sd, prep_sd + 2 * m)
                        msl = MSL[m]
                        for j in range(G):
                            for k in range(8):
                                t.matmul(bank(j)[:, 0:msl],
                                         xqTb[:, (j * 8 + k) * 128:
                                              (j * 8 + k + 1) * 128],
                                         wbufs[m % 2][:, k * 512:k * 512 + msl],
                                         start=(k == 0), stop=(k == 7))
                            for k in range(8):
                                mm = t.matmul(bank(2 + j)[:, 0:msl],
                                              xqTb[:, (j * 8 + k) * 128:
                                                   (j * 8 + k + 1) * 128],
                                              wbufs[m % 2][:, 4096 + k * 512:
                                                           4096 + k * 512 + msl],
                                              start=(k == 0), stop=(k == 7))
                                if j == G - 1 and k == 7:
                                    mm.then_inc(td, 1)
                S.add(td, 6)

                @block.scalar
                def _(sc, prep_td=prep_td):
                    sc = SD(sc)
                    for m in range(6):
                        sc.wait_ge(td, prep_td + (m + 1))
                        msl = MSL[m]
                        for j in range(G):
                            sc.activation(
                                out=sbB[:, (j * 6 + m) * 512:
                                        (j * 6 + m) * 512 + msl],
                                in_=bank(j)[:, 0:msl],
                                func=AF.Silu).then_inc(sd, 1)
                S.add(sd, 12)

                @block.vector
                def _(v, prep_sd=prep_sd):
                    v = VD(v)
                    for m in range(6):
                        msl = MSL[m]
                        for j in range(G):
                            v.wait_ge(sd, prep_sd + 2 * m + j + 1)
                            hb = (sbD[:, 0:2816] if j == 0
                                  else sbC[:].bitcast(F32)[:, 0:2816])
                            tt = v.tensor_tensor(
                                out=hb[:, m * 512:m * 512 + msl],
                                in0=bank(2 + j)[:, 0:msl],
                                in1=sbB[:, (j * 6 + m) * 512:
                                        (j * 6 + m) * 512 + msl],
                                op=OP.mult)
                            if j == G - 1:
                                tt.then_inc(vd, 1)
                S.add(vd, 6)

                for j in range(G):
                    hb = sbD[:, 0:2816] if j == 0 else sbC[:].bitcast(F32)[:, 0:2816]
                    hq = wb0[:].bitcast(F32)[:, 0:2816]
                    t1j = (sbA[:].bitcast(F32)[:, 0:2816] if j == 0
                           else sbD[:, 0:2816])
                    td_now = S.get(td)

                    @block.vector
                    def _(v, j=j, hb=hb, hq=hq, t1j=t1j, td_now=td_now):
                        v = VD(v)
                        if j == 1:
                            v.wait_ge(td, td_now)
                        emit_fp4(v, hb, hq, 2816,
                                 wb1[:].bitcast(F32)[:, 0:2816],
                                 t1j,
                                 sbB[:].bitcast(F32)[:, 0:2816], 1024)
                        v.memset(cols[:, 42:43], 0.0).then_inc(vd, 1)
                    S.add(vd, 1)
                    vq = S.get(vd)

                    emit_transposes(
                        hq,
                        lambda f, j=j: sbA[:, (f * G + j) * 128:
                                           (f * G + j + 1) * 128],
                        22, extra_tensor_wait=(vd, vq))

                hq_sd = S.get(sd)
                hq_td = S.get(td)

                @block.sync
                def _(s, hq_td=hq_td, prep_td=prep_td):
                    s.wait_ge(td, hq_td)
                    dsems = [dl0, dl1, dl2, dl3]
                    for k in range(22):
                        if k >= 4:
                            s.wait_ge(td, hq_td + (k - 3))
                        s.dma_start(out=wbufs[(k % 4) // 2][:, (k % 2) * 1024:
                                                            (k % 2 + 1) * 1024],
                                    in_=wdq1[k * 128:(k + 1) * 128, :]
                                    ).then_inc(dsems[k % 4], 16)
                for k in range(22):
                    S.add([dl0, dl1, dl2, dl3][k % 4], 16)

                @block.tensor
                def _(t, hq_sd=hq_sd,
                      dbs=[S.get(x) for x in [dl0, dl1, dl2, dl3]],
                      dcnt=[(22 + 3 - i) // 4 for i in range(4)]):
                    t.wait_ge(sd, hq_sd)
                    dsems = [dl0, dl1, dl2, dl3]
                    for k in range(22):
                        t.wait_ge(dsems[k % 4],
                                  dbs[k % 4] - 16 * dcnt[k % 4] + 16 * (k // 4 + 1))
                        wv = wbufs[(k % 4) // 2][:, (k % 2) * 1024:
                                                 (k % 2 + 1) * 1024]
                        for j in range(G):
                            for mo in range(2):
                                mm = t.matmul(bank(j * 2 + mo),
                                              sbA[:, (k * G + j) * 128:
                                                  (k * G + j + 1) * 128],
                                              wv[:, mo * 512:(mo + 1) * 512],
                                              start=(k == 0), stop=(k == 21))
                                if j == G - 1 and mo == 1:
                                    mm.then_inc(td, 1)
                S.add(td, 22)
                dn_td = S.get(td)

                @block.vector
                def _(v, dn_td=dn_td, grp=grp):
                    v = VD(v)
                    v.wait_ge(td, dn_td)
                    for j in range(G):
                        tc = grp * G + j
                        for mo in range(2):
                            xr = x_res[:, tc * D + mo * 512:
                                       tc * D + (mo + 1) * 512]
                            stt = v.scalar_tensor_tensor(
                                out=xr, in0=bank(j * 2 + mo),
                                scalar=0.5, in1=xr,
                                op0=OP.mult, op1=OP.add)
                            if j == G - 1 and mo == 1:
                                stt.then_inc(vd, 1)
                S.add(vd, 1)


        def emit_l1attn():
            sbDb_ = sbD[:].bitcast(BF16)
            sbDb2 = [sbDb_[:, 0:512], sbDb_[:, 512:1024]]
            sbEb_ = sbE[:].bitcast(BF16)
            sbEb2 = [sbEb_[:, 2048:2336], sbEb_[:, 2336:2624],
                     sbEb_[:, 2624:2880]]

            def pya(i):
                return psA[:, 1024 + i * 512:1024 + i * 512 + 65]

            cast_stash(wdq_a, wdq_ab, 1024, 4096)
            cast_stash(wuq_a, wuqr_ab, 4096, 1024, oc0=0)
            cast_stash(wqr_a, wuqr_ab, 4096, 512, oc0=1024)
            cast_stash(wkv_a, wkv_ab, 1024, 288)
            cast_stash(wukv_c, wukv_cb, 256, 256)
            cast_stash(wo_a, wo_ab, 1024, 1024)

            entry_td = S.get(td)
            for tc in range(TC):
                vpre = S.get(vd)
                spre = S.get(sd)
                st_pre = S.get(st)
                xhat = sbD[:, 0:1024]

                @block.vector
                def _(v, tc=tc, vpre=vpre, spre=spre, st_pre=st_pre):
                    v = VD(v)
                    if tc == 0:
                        v.wait_ge(td, entry_td)
                        v.wait_ge(st, st_pre)
                    xr = x_res[:, tc * D:(tc + 1) * D]
                    v.tensor_reduce(out=cols[:, 0:1], in_=xr, axis=AX.X, op=OP.add)
                    v.scalar_tensor_tensor(out=xhat, in0=xr, scalar=1.0,
                                           in1=xr, op0=OP.mult, op1=OP.mult,
                                           accum_out=cols[:, 1:2])
                    v.tensor_scalar(out=cols[:, 2:3], in0=cols[:, 0:1],
                                    scalar1=1.0 / D, scalar2=0.0,
                                    op0=OP.mult, op1=OP.bypass)
                    v.tensor_scalar(out=cols[:, 3:4], in0=cols[:, 1:2],
                                    scalar1=1.0 / D, scalar2=0.0,
                                    op0=OP.mult, op1=OP.bypass)
                    v.tensor_tensor(out=cols[:, 4:5], in0=cols[:, 2:3],
                                    in1=cols[:, 2:3], op=OP.mult)
                    v.scalar_tensor_tensor(out=cols[:, 4:5], in0=cols[:, 3:4],
                                           scalar=LN_EPS, in1=cols[:, 4:5],
                                           op0=OP.add, op1=OP.subtract
                                           ).then_inc(vd, 1)
                    v.wait_ge(sd, spre + 1)
                    v.reciprocal(out=cols[:, 5:6], in_=cols[:, 4:5])
                    v.scalar_tensor_tensor(out=cols[:, 6:7], in0=cols[:, 2:3],
                                           scalar=-1.0, in1=cols[:, 5:6],
                                           op0=OP.mult, op1=OP.mult)
                    v.tensor_scalar(out=xhat, in0=xr,
                                    scalar1=cols[:, 5:6], scalar2=cols[:, 6:7],
                                    op0=OP.mult, op1=OP.add).then_inc(vd, 1)

                @block.scalar
                def _(sc, vpre=vpre):
                    sc = SD(sc)
                    sc.wait_ge(vd, vpre + 1)
                    sc.activation(out=cols[:, 4:5], in_=cols[:, 4:5],
                                  func=AF.Sqrt).then_inc(sd, 1)
                S.add(vd, 2)
                S.add(sd, 1)
                vq = S.get(vd)

                emit_transposes(xhat,
                                lambda f: sbA[:, f * 128:(f + 1) * 128],
                                8, extra_tensor_wait=(vd, vq))
                xt_sd = S.get(sd)

                ldq_b = [S.get(wl0), S.get(wl1)]

                @block.sync
                def _(s, tc=tc, xt_sd=xt_sd, vpre=vpre):
                    if tc >= 1:
                        s.wait_ge(td, S.get(td))
                    for m in range(8):
                        wsem = wl0 if m % 2 == 0 else wl1
                        for k in range(8):
                            s.dma_start(
                                out=wbufs[m % 2][:, k * 512:(k + 1) * 512],
                                in_=wdq_ab[k * 128:(k + 1) * 128,
                                           m * 512:(m + 1) * 512]).then_inc(wsem, 16)
                    for k in range(8):
                        s.dma_start(out=wb1[:, 4096 + k * 288:4096 + (k + 1) * 288],
                                    in_=wkv_ab[k * 128:(k + 1) * 128, :]
                                    ).then_inc(wl1, 16)
                for m in range(8):
                    S.add(wl0 if m % 2 == 0 else wl1, 128)
                S.add(wl1, 128)

                @block.tensor
                def _(t, tc=tc, xt_sd=xt_sd, ldq_b=ldq_b, vpre=vpre):
                    t.wait_ge(sd, xt_sd)
                    for m in range(8):
                        t.wait_ge(wl0 if m % 2 == 0 else wl1,
                                  ldq_b[m % 2] + 128 * (m // 2 + 1))
                        if m >= 2:
                            t.wait_ge(vd, vpre + 2 + (m - 1))
                        for k in range(8):
                            mm = t.matmul(bank(m % 2),
                                          sbA[:, k * 128:(k + 1) * 128],
                                          wbufs[m % 2][:, k * 512:(k + 1) * 512],
                                          start=(k == 0), stop=(k == 7))
                        mm.then_inc(td, 1)
                    t.wait_ge(wl1, ldq_b[1] + 128 * 4 + 128)
                    for k in range(8):
                        mm = t.matmul(bank(2)[:, 0:288],
                                      sbA[:, k * 128:(k + 1) * 128],
                                      wb1[:, 4096 + k * 288:4096 + (k + 1) * 288],
                                      start=(k == 0), stop=(k == 7))
                    mm.then_inc(td, 1)
                S.add(td, 9)
                td_cq = S.get(td)

                @block.vector
                def _(v, tc=tc, vpre=vpre, td_b=td_cq - 9):
                    v = VD(v)
                    for m in range(8):
                        v.wait_ge(td, td_b + m + 1)
                        v.tensor_copy(sbD[:, m * 512:(m + 1) * 512],
                                      bank(m % 2)).then_inc(vd, 1)
                    v.wait_ge(td, td_b + 9)
                    v.tensor_copy(sbB[:, 0:256], bank(2)[:, 0:256])
                    v.tensor_tensor(out=sbB[:, 256:288], in0=bank(2)[:, 256:288],
                                    in1=sct[:, tc * 32:(tc + 1) * 32], op=OP.mult
                                    ).then_inc(vd, 1)
                S.add(vd, 9)
                vd_kv = S.get(vd)

                @block.gpsimd
                def _(g, tc=tc, vd_kv=vd_kv):
                    g.wait_ge(vd, vd_kv)
                    g.dma_start(out=agkv_in[tc * 128:(tc + 1) * 128, :],
                                in_=sbB[:, 0:288]).then_inc(st, 16)
                S.add(st, 16)
                vq2 = S.get(vd)

                emit_transposes(sbD[:, 0:4096],
                                lambda f: sbA[:, 1024 + f * 128:1024 + (f + 1) * 128],
                                32, extra_tensor_wait=(vd, vq2))
                cqt_sd = S.get(sd)

                @block.gpsimd
                def _(g, tc=tc, cqt_sd=cqt_sd):
                    g.wait_ge(sd, cqt_sd)
                    for f in range(32):
                        g.dma_start(
                            out=cqT_scr[f * 128:(f + 1) * 128,
                                        tc * 128:(tc + 1) * 128],
                            in_=sbA[:, 1024 + f * 128:1024 + (f + 1) * 128]
                            ).then_inc(st, 16)
                S.add(st, 16 * 32)

            a1_st = S.get(st)
            for grp in range(8):
                g_td = S.get(td)
                g_vd = S.get(vd)
                ldb = [S.get(wl0), S.get(wl1)]
                ldc = [S.get(dl0), S.get(dl1)]

                @block.sync
                def _(s, grp=grp, g_td=g_td, a1_st=a1_st):
                    if grp == 0:
                        s.wait_ge(st, a1_st)
                    else:
                        s.wait_ge(td, g_td)
                    for k in range(32):
                        wsem = wl0 if k % 2 == 0 else wl1
                        xsem = dl0 if k % 2 == 0 else dl1
                        if k >= 2:
                            s.wait_ge(td, g_td + (k - 1) * 1)
                        s.dma_start(out=wbufs[k % 2][:, 0:1536],
                                    in_=wuqr_ab[k * 128:(k + 1) * 128, :]
                                    ).then_inc(wsem, 16)
                        for j in range(2):
                            s.dma_start(
                                out=wbufs[k % 2][:, 2048 + j * 128:
                                                 2048 + (j + 1) * 128],
                                in_=cqT_scr[k * 128:(k + 1) * 128,
                                            (grp * 2 + j) * 128:
                                            (grp * 2 + j + 1) * 128]
                                ).then_inc(xsem, 16)
                for k in range(32):
                    S.add(wl0 if k % 2 == 0 else wl1, 16)
                    S.add(dl0 if k % 2 == 0 else dl1, 32)

                @block.tensor
                def _(t, grp=grp, ldb=ldb, ldc=ldc, g_vd=g_vd):
                    t.wait_ge(vd, g_vd)
                    for k in range(32):
                        t.wait_ge(wl0 if k % 2 == 0 else wl1,
                                  ldb[k % 2] + 16 * (k // 2 + 1))
                        t.wait_ge(dl0 if k % 2 == 0 else dl1,
                                  ldc[k % 2] + 32 * (k // 2 + 1))
                        for j in range(2):
                            for mq in range(3):
                                mm = t.matmul(
                                    bank(j * 3 + mq),
                                    wbufs[k % 2][:, 2048 + j * 128:
                                                 2048 + (j + 1) * 128],
                                    wbufs[k % 2][:, mq * 512:(mq + 1) * 512],
                                    start=(k == 0), stop=(k == 31))
                        mm.then_inc(td, 1)
                S.add(td, 32)
                td_q = S.get(td)

                @block.vector
                def _(v, grp=grp, td_q=td_q):
                    v = VD(v)
                    v.wait_ge(td, td_q)
                    for j in range(2):
                        tc = grp * 2 + j
                        v.tensor_copy(sbB[:, j * 2048:j * 2048 + 512],
                                      bank(j * 3 + 0))
                        v.tensor_copy(sbB[:, j * 2048 + 512:j * 2048 + 1024],
                                      bank(j * 3 + 1))
                        qr3 = sbB[:, j * 2048 + 1024:j * 2048 + 1536]\
                            .rearrange("p (h r) -> p h r", r=32)
                        tt = v.tensor_tensor(
                            out=qr3,
                            in0=bank(j * 3 + 2).rearrange("p (h r) -> p h r", r=32),
                            in1=sct[:, tc * 32:(tc + 1) * 32].unsqueeze(1)
                            .broadcast_to([128, 16, 32]),
                            op=OP.mult)
                        if j == 1:
                            tt.then_inc(vd, 1)
                S.add(vd, 1)
                vd_q = S.get(vd)

                @block.gpsimd
                def _(g, grp=grp, vd_q=vd_q):
                    g.wait_ge(vd, vd_q)
                    for j in range(2):
                        tc = grp * 2 + j
                        for j2 in range(8):
                            base = j2 * 2048 + tc * 128
                            g.dma_start(
                                out=a2aq_in[base:base + 128, 0:64],
                                in_=sbB[:, j * 2048 + j2 * 64:
                                        j * 2048 + (j2 + 1) * 64]).then_inc(st, 16)
                            g.dma_start(
                                out=a2aq_in[base:base + 128, 64:96],
                                in_=sbB[:, j * 2048 + 1024 + j2 * 32:
                                        j * 2048 + 1024 + (j2 + 1) * 32]
                                ).then_inc(st, 16)
                            g.dma_start(
                                out=a2aq_in[base:base + 128, 96:160],
                                in_=sbB[:, j * 2048 + (j2 + 8) * 64:
                                        j * 2048 + (j2 + 9) * 64]).then_inc(st, 16)
                            g.dma_start(
                                out=a2aq_in[base:base + 128, 160:192],
                                in_=sbB[:, j * 2048 + 1024 + (j2 + 8) * 32:
                                        j * 2048 + 1024 + (j2 + 9) * 32]
                                ).then_inc(st, 16)
                S.add(st, 16 * 64)

            st_c = S.get(st)

            @block.gpsimd
            def _(g):
                g.wait_ge(st, st_c)
                g.collective_compute(
                    "AllGather", OP.bypass,
                    replica_groups=[list(range(NCORES))],
                    ins=[agkv_in.ap().opt()],
                    outs=[agkv_out.ap().opt()],
                ).then_inc(cc, 1)
                with allow_ici_all_to_all():
                    g.collective_compute(
                        "AllToAll", OP.bypass,
                        replica_groups=[list(range(NCORES))],
                        ins=[a2aq_in.ap().opt()],
                        outs=[a2aq_out.ap().opt()],
                    ).then_inc(cc, 1)
            cc_l1 = S.add(cc, 2)

            ldw_b = S.get(ld)

            @block.sync
            def _(s):
                s.wait_ge(cc, cc_l1)
                s.dma_start(out=sbC[:, 6656:6912], in_=wukv_cb[0:128, :]
                            ).then_inc(ld, 16)
                s.dma_start(out=sbC[:, 6912:7168], in_=wukv_cb[128:256, :]
                            ).then_inc(ld, 16)
            ld_wukv = S.add(ld, 32)

            for bb in range(4):
                bvd = S.get(vd)

                @block.vector
                def _(v, bb=bb, bvd=bvd):
                    v = VD(v)
                    v.wait_ge(ld, ld_wukv)
                    v.wait_ge(td, S.get(td))
                    v.memset(cols[:, 43:44], 0.0).then_inc(vd, 1)
                S.add(vd, 1)

                kv_vd_base = S.get(vd)
                for kb in range(32):
                    ld_pre2 = S.get(dl2)

                    @block.sync
                    def _(s, bb=bb, kb=kb):
                        if kb >= 2:
                            s.wait_ge(vd, kv_vd_base + kb - 1)
                        s.dma_start(
                            out=sbEb2[kb % 2][:],
                            in_=agkv_out[bb * 4096 + kb * 128:
                                         bb * 4096 + (kb + 1) * 128, :]
                            ).then_inc(dl2, 16)
                    S.add(dl2, 16)

                    @block.vector
                    def _(v, kb=kb, ld_pre2=ld_pre2):
                        v = VD(v)
                        v.wait_ge(dl2, ld_pre2 + 16)
                        v.tensor_copy(sbE[:, 512:800], sbEb2[kb % 2][:]
                                      ).then_inc(vd, 1)
                    S.add(vd, 1)
                    vq3 = S.get(vd)

                    emit_transposes(sbE[:, 512:768],
                                    lambda f, kb=kb: sbB[:, f * 4096 + kb * 128:
                                                         f * 4096 + (kb + 1) * 128],
                                    2, extra_tensor_wait=(vd, vq3))
                    t_b3 = S.get(td)
                    s_b3 = S.get(sd)

                    @block.tensor
                    def _(t, kb=kb, s_b3=s_b3):
                        t.wait_ge(sd, s_b3)
                        t.transpose(out=pb(0)[0:32, :],
                                    in_=sbE[:, 768:800],
                                    identity=idf[:]).then_inc(td, 1)
                    S.add(td, 1)

                    @block.scalar
                    def _(sc, kb=kb, t_b3=t_b3):
                        sc = SD(sc)
                        sc.wait_ge(td, t_b3 + 1)
                        sc.copy(out=sbC[0:32, kb * 128:(kb + 1) * 128],
                                in_=pb(0)[0:32, :]).then_inc(sd, 1)
                    S.add(sd, 1)

                for hh in range(2):
                    p = bb * 2 + hh
                    kv_td = S.get(td)
                    kv_vd = S.get(vd)
                    kv_sd = S.get(sd)

                    @block.tensor
                    def _(t, hh=hh, kv_sd=kv_sd, kv_vd=kv_vd):
                        t.wait_ge(sd, kv_sd)
                        t.wait_ge(vd, kv_vd)
                        for mb in range(8):
                            if mb >= 1:
                                t.wait_ge(sd, kv_sd + mb)
                            for c2 in range(2):
                                mm = t.matmul(
                                    bank(0)[0:64, :],
                                    sbC[:, 6656 + c2 * 256 + hh * 64:
                                        6656 + c2 * 256 + (hh + 1) * 64],
                                    sbB[:, c2 * 4096 + mb * 512:
                                        c2 * 4096 + (mb + 1) * 512],
                                    start=(c2 == 0), stop=(c2 == 1))
                            mm.then_inc(td, 1)
                        for kb in range(32):
                            if kb >= 1:
                                t.wait_ge(sd, kv_sd + 8 + kb)
                            for c2 in range(2):
                                mm = t.matmul(
                                    bank(1)[:, 0:64],
                                    sbB[:, c2 * 4096 + kb * 128:
                                        c2 * 4096 + (kb + 1) * 128],
                                    sbC[:, 6656 + c2 * 256 + 128 + hh * 64:
                                        6656 + c2 * 256 + 128 + (hh + 1) * 64],
                                    start=(c2 == 0), stop=(c2 == 1))
                            mm.then_inc(td, 1)
                    S.add(td, 40)
                    kv2_td = S.get(td)

                    @block.scalar
                    def _(sc, kv_td=kv_td):
                        sc = SD(sc)
                        for mb in range(8):
                            sc.wait_ge(td, kv_td + mb + 1)
                            sc.copy(out=sbA[0:64, mb * 512:(mb + 1) * 512],
                                    in_=bank(0)[0:64, :]).then_inc(sd, 1)
                        for kb in range(32):
                            sc.wait_ge(td, kv_td + 8 + kb + 1)
                            sc.copy(out=sbC[:, 4096 + kb * 65:4096 + kb * 65 + 64],
                                    in_=bank(1)[:, 0:64]).then_inc(sd, 1)
                    S.add(sd, 40)
                    kv2_sd = S.get(sd)


                    @block.vector
                    def _(v, hh=hh, p=p, kv2_sd=kv2_sd):
                        v = VD(v)
                        v.wait_ge(sd, kv2_sd)
                        v.memset(sbC[:, 4096:6176]
                                 .rearrange("p (k o) -> p k o", o=65)[:, :, 64:65],
                                 1.0).then_inc(vd, 1)
                    S.add(vd, 1)

                    @block.gpsimd
                    def _(g, kv2_sd=kv2_sd):
                        g.wait_ge(sd, kv2_sd)
                        g.dma_start(out=sbA[64:96, 0:4096],
                                    in_=sbC[:, 0:4096][0:32, :]).then_inc(gd, 16)
                    S.add(gd, 16)

                    q_vd_base = S.get(vd)
                    for qb in range(32):
                        ld_q = S.get(dl3)
                        r = 2 * bb + qb // 16
                        loc = (qb % 16) * 128

                        @block.sync
                        def _(s, qb=qb, r=r, loc=loc, hh=hh):
                            if qb >= 2:
                                s.wait_ge(vd, q_vd_base + qb - 1)
                            s.dma_start(out=sbEb2[qb % 2][:, 0:96],
                                        in_=a2aq_out[r * 2048 + loc:
                                                     r * 2048 + loc + 128,
                                                     hh * 96:(hh + 1) * 96]
                                        ).then_inc(dl3, 16)
                        S.add(dl3, 16)

                        @block.vector
                        def _(v, qb=qb, ld_q=ld_q):
                            v = VD(v)
                            v.wait_ge(dl3, ld_q + 16)
                            v.tensor_copy(sbE[:, 512:608], sbEb2[qb % 2][:, 0:96]
                                          ).then_inc(vd, 1)
                        S.add(vd, 1)
                        vq4 = S.get(vd)

                        @block.scalar
                        def _(sc, qb=qb, vq4=vq4):
                            sc = SD(sc)
                            sc.wait_ge(vd, vq4)
                            sc.activation(out=sbE[:, 608:704], in_=sbE[:, 512:608],
                                          func=AF.Square,
                                          accum_out=sbE[:, 704 + qb:705 + qb]
                                          ).then_inc(sd, 1)
                        S.add(sd, 1)
                        sq_sd = S.get(sd)

                        t_b4 = S.get(td)

                        @block.tensor
                        def _(t, qb=qb, sq_sd=sq_sd):
                            t.wait_ge(sd, sq_sd)
                            t.transpose(out=pb(qb % 2)[0:96, :],
                                        in_=sbE[:, 512:608],
                                        identity=idf[:]).then_inc(td, 1)
                        S.add(td, 1)

                        @block.scalar
                        def _(sc, qb=qb, t_b4=t_b4):
                            sc = SD(sc)
                            sc.wait_ge(td, t_b4 + 1)
                            sc.copy(out=sbA[0:96, 4096 + qb * 128:
                                            4096 + (qb + 1) * 128],
                                    in_=pb(qb % 2)[0:96, :]).then_inc(sd, 1)
                        S.add(sd, 1)

                    bb_vd = S.get(vd)
                    bb_td = S.get(td)
                    bb_sd = S.get(sd)

                    @block.vector
                    def _(v, bb_sd=bb_sd, gd_kr=S.get(gd)):
                        v = VD(v)
                        v.wait_ge(sd, bb_sd)
                        v.wait_ge(gd, gd_kr)
                        v.tensor_reduce(out=cols[:, 44:45], in_=sbE[:, 704:736],
                                        axis=AX.X, op=OP.max)
                        v.tensor_reduce(out=cols[0:96, 45:46], in_=sbA[0:96, 0:4096],
                                        axis=AX.X, op=OP.max,
                                        apply_absolute_value=True).then_inc(vd, 1)
                    S.add(vd, 1)
                    bv1 = S.get(vd)

                    @block.tensor
                    def _(t, bv1=bv1):
                        t.wait_ge(vd, bv1)
                        t.transpose(out=pb(0)[0:1, :], in_=cols[:, 44:45],
                                    identity=idf[:])
                        t.transpose(out=pb(1)[0:1, 0:96], in_=cols[0:96, 45:46],
                                    identity=idf[0:96, 0:96]).then_inc(td, 1)
                    S.add(td, 1)
                    bt1 = S.get(td)

                    @block.vector
                    def _(v, bt1=bt1):
                        v = VD(v)
                        v.wait_ge(td, bt1)
                        v.tensor_reduce(out=cols[0:1, 46:47], in_=pb(0)[0:1, :],
                                        axis=AX.X, op=OP.max)
                        v.tensor_reduce(out=cols[0:1, 47:48], in_=pb(1)[0:1, 0:96],
                                        axis=AX.X, op=OP.max).then_inc(vd, 1)
                    S.add(vd, 1)
                    bv2 = S.get(vd)

                    @block.scalar
                    def _(sc, bv2=bv2):
                        sc = SD(sc)
                        sc.wait_ge(vd, bv2)
                        sc.activation(out=cols[0:1, 46:47], in_=cols[0:1, 46:47],
                                      func=AF.Sqrt).then_inc(sd, 1)
                    S.add(sd, 1)
                    bs1 = S.get(sd)

                    @block.vector
                    def _(v, bs1=bs1):
                        v = VD(v)
                        v.wait_ge(sd, bs1)
                        v.tensor_tensor(out=cols[0:1, 48:49], in0=cols[0:1, 46:47],
                                        in1=cols[0:1, 47:48], op=OP.mult)
                        v.tensor_scalar(out=cols[0:1, 48:49], in0=cols[0:1, 48:49],
                                        scalar1=-1.0, scalar2=0.0,
                                        op0=OP.mult, op1=OP.bypass).then_inc(vd, 1)
                    S.add(vd, 1)
                    bv3 = S.get(vd)

                    @block.gpsimd
                    def _(g, bv3=bv3):
                        g.wait_ge(vd, bv3)
                        g.dma_start(out=cols[:, 49:50],
                                    in_=cols[0:1, 48:49].unsqueeze(1)
                                    .broadcast_to([1, 128, 1])).then_inc(gd, 16)
                    S.add(gd, 16)
                    gd_b = S.get(gd)

                    for tq in range(8):
                        nkb = 4 * tq + 4
                        t0 = S.get(td)
                        s0 = S.get(sd)
                        v0 = S.get(vd)

                        @block.tensor
                        def _(t, tq=tq, nkb=nkb, t0=t0, s0=s0, v0=v0,
                              gd_b=gd_b):
                            if tq == 0:
                                t.wait_ge(gd, gd_b)
                            for kb in range(nkb):
                                if kb >= 2:
                                    t.wait_ge(sd, s0 + kb - 1)
                                t.matmul(bank(4 + kb % 2),
                                         sbA[0:96, kb * 128:(kb + 1) * 128],
                                         sbA[0:96, 4096 + tq * 512:
                                             4096 + (tq + 1) * 512],
                                         start=True, stop=True).then_inc(td, 1)
                                if kb >= 1:
                                    kk = kb - 1
                                    if kk >= nkb - 4:
                                        t.wait_ge(vd, v0 + 1 + (kk - (nkb - 4)) + 1)
                                    else:
                                        t.wait_ge(sd, s0 + kk + 1)
                                    t.matmul(psB[0:65, 0:512],
                                             sbC[:, 4096 + kk * 65:
                                                 4096 + (kk + 1) * 65],
                                             sbDb2[kk % 2][:],
                                             start=(kk == 0), stop=False
                                             ).then_inc(td, 1)
                            kk = nkb - 1
                            t.wait_ge(vd, v0 + 1 + 4)
                            t.matmul(psB[0:65, 0:512],
                                     sbC[:, 4096 + kk * 65:4096 + (kk + 1) * 65],
                                     sbDb2[kk % 2][:],
                                     start=False, stop=True).then_inc(td, 1)
                        S.add(td, 2 * nkb)

                        @block.scalar
                        def _(sc, tq=tq, nkb=nkb, t0=t0):
                            sc = SD(sc)
                            for kb in range(nkb):
                                sc.wait_ge(td, t0 + max(2 * kb, 1))
                                sc.activation(out=sbDb2[kb % 2][:],
                                              in_=bank(4 + kb % 2),
                                              func=AF.Exp, scale=0.1020620726,
                                              bias=cols[:, 49:50]).then_inc(sd, 1)
                        S.add(sd, nkb)

                        @block.vector
                        def _(v, tq=tq, nkb=nkb, s0=s0):
                            v = VD(v)
                            v.memset(cols[:, 50:51], 0.0).then_inc(vd, 1)
                            for dd in range(4):
                                kb = nkb - 4 + dd
                                v.wait_ge(sd, s0 + kb + 1)
                                v.tensor_tensor(out=sbDb2[kb % 2][:],
                                                in0=sbDb2[kb % 2][:],
                                                in1=msk[:, dd * 512:(dd + 1) * 512],
                                                op=OP.mult).then_inc(vd, 1)
                        S.add(vd, 5)

                        av_td = S.get(td)

                        @block.scalar
                        def _(sc, av_td=av_td):
                            sc = SD(sc)
                            sc.wait_ge(td, av_td)
                            sc.copy(out=sbE[0:65, 0:512], in_=psB[0:65, 0:512]
                                    ).then_inc(sd, 1)
                        S.add(sd, 1)
                        yv_sd = S.get(sd)

                        for sub in range(4):
                            s_b5 = S.get(sd)
                            v_b5 = S.get(vd)

                            @block.tensor
                            def _(t, sub=sub, yv_sd=yv_sd, v_b5=v_b5):
                                t.wait_ge(sd, yv_sd)
                                if sub >= 2:
                                    t.wait_ge(vd, v_b5)
                                t.transpose(out=pya(sub % 2),
                                            in_=sbE[0:65, 0:512]
                                            [0:65, sub * 128:(sub + 1) * 128],
                                            identity=idf[0:65, 0:65]
                                            ).then_inc(td, 1)
                            S.add(td, 1)
                            tv = S.get(td)

                            @block.vector
                            def _(v, sub=sub, tv=tv, tq=tq, hh=hh, bb=bb):
                                v = VD(v)
                                v.wait_ge(td, tv)
                                v.reciprocal(out=cols[:, 51:52],
                                             in_=pya(sub % 2)[:, 64:65])
                                v.tensor_scalar(out=sbEb2[2][:, sub * 64:
                                                            (sub + 1) * 64],
                                                in0=pya(sub % 2)[:, 0:64],
                                                scalar1=cols[:, 51:52], scalar2=0.0,
                                                op0=OP.mult, op1=OP.bypass
                                                ).then_inc(vd, 1)
                            S.add(vd, 1)
                            yv = S.get(vd)

                            @block.gpsimd
                            def _(g, sub=sub, tq=tq, hh=hh, bb=bb, yv=yv):
                                g.wait_ge(vd, yv)
                                gt = tq * 512 + sub * 128
                                j2 = 2 * bb + (1 if gt >= 2048 else 0)
                                loc = gt % 2048
                                g.dma_start(
                                    out=a2ay_in[j2 * 2048 + loc:
                                                j2 * 2048 + loc + 128,
                                                hh * 64:(hh + 1) * 64],
                                    in_=sbEb2[2][:, sub * 64:(sub + 1) * 64]
                                    ).then_inc(st, 16)
                            S.add(st, 16)

            st_y = S.get(st)

            @block.gpsimd
            def _(g):
                g.wait_ge(st, st_y)
                with allow_ici_all_to_all():
                    g.collective_compute(
                        "AllToAll", OP.bypass,
                        replica_groups=[list(range(NCORES))],
                        ins=[a2ay_in.ap().opt()],
                        outs=[a2ay_out.ap().opt()],
                    ).then_inc(cc, 1)
            cc_y = S.add(cc, 1)

            for tc in range(TC):
                ld_pre = S.get(ld)
                vd_pre = S.get(vd)

                @block.sync
                def _(s, tc=tc, vd_pre=vd_pre):
                    if tc == 0:
                        s.wait_ge(cc, cc_y)
                        s.wait_ge(vd, vd_pre)
                    else:
                        s.wait_ge(vd, vd_pre)
                    for r in range(8):
                        s.dma_start(out=sbB[:, r * 128:(r + 1) * 128],
                                    in_=a2ay_out[r * 2048 + tc * 128:
                                                 r * 2048 + (tc + 1) * 128, :]
                                    ).then_inc(ld, 16)
                    for m in range(2):
                        for f in range(8):
                            s.dma_start(
                                out=wb1[:, (m * 8 + f) * 512:(m * 8 + f + 1) * 512],
                                in_=wo_ab[f * 128:(f + 1) * 128,
                                          m * 512:(m + 1) * 512]).then_inc(ld, 16)
                S.add(ld, 16 * 24)
                ld_all = S.get(ld)

                @block.vector
                def _(v, ld_all=ld_all):
                    v = VD(v)
                    v.wait_ge(ld, ld_all)
                    v.tensor_copy(sbD[:, 0:1024], sbB[:, 0:1024]).then_inc(vd, 1)
                S.add(vd, 1)
                vq5 = S.get(vd)

                emit_transposes(sbD[:, 0:1024],
                                lambda f: sbA[:, f * 128:(f + 1) * 128],
                                8, extra_tensor_wait=(vd, vq5))
                sd_yt = S.get(sd)

                @block.tensor
                def _(t, sd_yt=sd_yt, vd_pre=vd_pre, tc=tc):
                    t.wait_ge(sd, sd_yt)
                    for m in range(2):
                        for f in range(8):
                            mm = t.matmul(bank(m),
                                          sbA[:, f * 128:(f + 1) * 128],
                                          wb1[:, (m * 8 + f) * 512:
                                              (m * 8 + f + 1) * 512],
                                          start=(f == 0), stop=(f == 7))
                    mm.then_inc(td, 1)
                S.add(td, 1)
                td_y = S.get(td)

                @block.vector
                def _(v, td_y=td_y, tc=tc):
                    v = VD(v)
                    v.wait_ge(td, td_y)
                    for m in range(2):
                        xr = x_res[:, tc * D + m * 512:tc * D + (m + 1) * 512]
                        tt = v.tensor_tensor(out=xr, in0=bank(m), in1=xr,
                                             op=OP.add)
                        if m == 1:
                            tt.then_inc(vd, 1)
                S.add(vd, 1)


        if N_SUBLAYERS >= 1 and DO_FFN:
            emit_l0ffn()
        if N_SUBLAYERS >= 2:
            emit_l0attn()
        if N_SUBLAYERS >= 3:
            emit_l0ffn()
        if N_SUBLAYERS >= 4:
            l1_wq(wg1, wgq1, 2816, 1024)
            l1_wq(wu1, wuq1, 2816, 1024)
            l1_wq(wd1, wdq1, 1024, 2816)
            emit_l1ffn()
        if N_SUBLAYERS >= 5:
            emit_l1attn()
        if N_SUBLAYERS >= 6:
            emit_l1ffn()

        if DBG:
            dumps = [("d_agx", agx_in, [1024, 2048]),
                     ("d_sK", scanK, [512, 4096]), ("d_sQ", scanQ, [512, 4096]),
                     ("d_sBK", scanBK, [512, 4096]), ("d_sBV", scanBV, [512, 4096]),
                     ("d_a2a", a2a0_in, [1024, 2048]),
                     ("d_a2o", a2a0_out, [1024, 2048])]
            daps = [(nc.dram_tensor(nm, shp, BF16, kind="ExternalOutput").ap(), t)
                    for nm, t, shp in dumps]
            d_vd = S.get(vd); d_td = S.get(td); d_sd = S.get(sd); d_st0 = S.get(st)

            @block.gpsimd
            def _(g):
                g.wait_ge(vd, d_vd)
                g.wait_ge(td, d_td)
                g.wait_ge(sd, d_sd)
                g.wait_ge(ld, S.get(ld))
                g.wait_ge(st, d_st0)
                for w_ap, t in daps:
                    g.dma_start(out=w_ap, in_=t.ap()).then_inc(st, 16)
                g.wait_ge(st, d_st0 + 16 * len(daps))
            S.add(st, 16 * len(daps))

        fin_vd = S.get(vd)
        fin_td = S.get(td)
        fin_sd = S.get(sd)
        st_b = S.get(st)

        @block.gpsimd
        def _(g):
            g.wait_ge(vd, fin_vd)
            g.wait_ge(td, fin_td)
            g.wait_ge(sd, fin_sd)
            g.wait_ge(ld, S.get(ld))
            for tc in range(TC):
                g.dma_start(out=y_out[tc * 128:(tc + 1) * 128, :],
                            in_=x_res[:, tc * D:(tc + 1) * D]).then_inc(st, 16)
            g.wait_ge(st, st_b + 16 * TC)
        S.add(st, 16 * TC)

    return nc


def kernel(**inputs):
    x = np.ascontiguousarray(np.asarray(inputs["L0_attn_x"], dtype=np.float32))
    xf = x.reshape(NCORES * NT, D)
    ident_f = np.eye(128, dtype=np.float32)
    wg = np.ascontiguousarray(np.asarray(inputs["L0_ffn_w_gate"], np.float32))
    wu = np.ascontiguousarray(np.asarray(inputs["L0_ffn_w_up"], np.float32))
    wd = np.ascontiguousarray(np.asarray(inputs["L0_ffn_w_down"], np.float32))

    Wq = np.asarray(inputs["L0_attn_Wq"], np.float32)
    Wk = np.asarray(inputs["L0_attn_Wk"], np.float32)
    Wv = np.asarray(inputs["L0_attn_Wv"], np.float32)
    Wbeta = np.asarray(inputs["L0_attn_Wbeta"], np.float32)
    Wg_a = np.asarray(inputs["L0_attn_Wg"], np.float32)
    Wout_a = np.asarray(inputs["L0_attn_Wout"], np.float32)
    ln_g = np.asarray(inputs["L0_attn_ln_g"], np.float32)
    ln_b = np.asarray(inputs["L0_attn_ln_b"], np.float32)

    perm = np.empty(1024, np.int64)
    for r in range(8):
        for hh in range(2):
            for dd_ in range(64):
                perm[r * 128 + hh * 64 + dd_] = (r + 8 * hh) * 64 + dd_
    lng_r = np.ascontiguousarray(np.broadcast_to(ln_g[perm][None, :], (128, 1024)),
                                 dtype=np.float32).astype(np.float32)
    lnb_r = np.ascontiguousarray(np.broadcast_to(ln_b[perm][None, :], (128, 1024)),
                                 dtype=np.float32).astype(np.float32)
    l0wg_a = np.ascontiguousarray(Wg_a[:, perm])
    l0wo_a = np.ascontiguousarray(Wout_a[perm, :])
    eselS = np.zeros((128, 4096), dtype=BF)
    for _tau in range(32):
        eselS[2 * _tau, _tau * 128:_tau * 128 + 64] = 1
        eselS[2 * _tau + 1, _tau * 128 + 64:_tau * 128 + 128] = 1
    eselS[64:128] = eselS[0:64]
    esel = np.zeros((128, 128), dtype=BF)
    esel[0:64, 62] = 1
    esel[64:128, 63] = 1

    W_DQ = np.asarray(inputs["L1_attn_W_DQ"], np.float32)
    W_UQ = np.asarray(inputs["L1_attn_W_UQ"], np.float32)
    W_DKV = np.asarray(inputs["L1_attn_W_DKV"], np.float32)
    W_UK = np.asarray(inputs["L1_attn_W_UK"], np.float32)
    W_UV = np.asarray(inputs["L1_attn_W_UV"], np.float32)
    W_QR = np.asarray(inputs["L1_attn_W_QR"], np.float32)
    W_KR = np.asarray(inputs["L1_attn_W_KR"], np.float32)
    W_O = np.asarray(inputs["L1_attn_W_O"], np.float32)
    wkv_cat = np.ascontiguousarray(np.concatenate([W_DKV, W_KR], axis=1))
    wo_perm = np.ascontiguousarray(W_O[perm, :])
    tpos = np.arange(4096, dtype=np.float64)
    inv = 1.0 / 10000.0 ** (np.arange(0, 32, 2, dtype=np.float64) / 32)
    fr = tpos[:, None] * inv[None, :]
    emb = np.concatenate([fr, fr], -1)
    scale_all = (np.cos(emb) + np.sin(emb)).astype(np.float32)
    mask4 = np.zeros((128, 4, 512), dtype=BF)
    jj = np.arange(512)[None, :]
    pp = np.arange(128)[:, None]
    for dmi in range(4):
        mask4[:, dmi, :] = (jj >= pp + 128 * dmi).astype(BF)
    mask4 = np.ascontiguousarray(mask4.reshape(128, 2048))

    nc = build_nc()
    in_maps = []
    for c in range(NCORES):
        h0, h1 = c, c + 8
        pwc = np.zeros((1024, 392), np.float32)
        pwc[:, 0:64] = Wq[:, h0 * 64:(h0 + 1) * 64]
        pwc[:, 64:128] = Wq[:, h1 * 64:(h1 + 1) * 64]
        pwc[:, 128:192] = Wk[:, h0 * 64:(h0 + 1) * 64]
        pwc[:, 192:256] = Wk[:, h1 * 64:(h1 + 1) * 64]
        pwc[:, 256:320] = Wv[:, h0 * 64:(h0 + 1) * 64]
        pwc[:, 320:384] = Wv[:, h1 * 64:(h1 + 1) * 64]
        pwc[:, 384] = Wbeta[:, h0]
        pwc[:, 385] = Wbeta[:, h1]
        in_maps.append({
            "x": np.ascontiguousarray(xf[c * NT:(c + 1) * NT]),
            "ident_f": ident_f,
            "l0wg": wg, "l0wu": wu, "l0wd": wd,
            "l0pw": pwc, "l0wg_a": l0wg_a, "l0wo_a": l0wo_a,
            "lng_r": lng_r, "lnb_r": lnb_r, "esel": esel, "eselS": eselS,
            "l1wg": np.ascontiguousarray(np.asarray(inputs["L1_ffn_w_gate"], np.float32)),
            "l1wu": np.ascontiguousarray(np.asarray(inputs["L1_ffn_w_up"], np.float32)),
            "l1wd": np.ascontiguousarray(np.asarray(inputs["L1_ffn_w_down"], np.float32)),
            "l1wdq": W_DQ, "l1wuq": W_UQ, "l1wqr": W_QR,
            "l1wkv": wkv_cat,
            "l1wukv": np.ascontiguousarray(np.concatenate(
                [W_UK[:, h0 * 64:(h0 + 1) * 64], W_UK[:, h1 * 64:(h1 + 1) * 64],
                 W_UV[:, h0 * 64:(h0 + 1) * 64], W_UV[:, h1 * 64:(h1 + 1) * 64]],
                axis=1)),
            "l1wo": wo_perm,
            "scal_t": np.ascontiguousarray(
                scale_all[2048 * (c % 2):2048 * (c % 2) + 2048].reshape(
                    16, 128, 32).transpose(1, 0, 2).reshape(128, 512)),
            "mask4": mask4,
        })
    res = run_bass_kernel_spmd(nc, in_maps, core_ids=list(range(NCORES)))
    y = np.concatenate([res.results[c]["y"] for c in range(NCORES)], axis=0)
    return y.reshape(4, 4096, D).astype(np.float32)


# version 04
# speedup vs baseline: 1.2577x; 1.0227x over previous
import numpy as np
import ml_dtypes
import concourse.bass as bass
import concourse.mybir as mybir
from concourse.bass_utils import run_bass_kernel_spmd
from concourse.aot_env import allow_ici_all_to_all

F32 = mybir.dt.float32
BF16 = mybir.dt.bfloat16
AF = mybir.ActivationFunctionType
OP = mybir.AluOpType
AX = mybir.AxisListType
BF = ml_dtypes.bfloat16

NCORES = 8
NT = 2048
TC = 16
D = 1024
MAGIC = float(np.float32(1.5 * 2 ** 23))
LN_EPS = 1e-5

N_SUBLAYERS = 6
DO_WQ = True
DO_FFN = True
FFN_STAGES = 4
FFN_NGRP = 8
DBG = False



_V_OPS = {"tensor_tensor", "tensor_scalar", "tensor_reduce", "tensor_copy",
          "scalar_tensor_tensor", "reciprocal", "memset", "tensor_tensor_scan",
          "select", "tensor_add", "tensor_mul", "tensor_sub", "tensor_max",
          "iota", "tensor_tensor_reduce", "tensor_relu", "bn_stats", "bn_aggr"}


class VD:

    def __init__(self, v):
        self._v = v

    def __getattr__(self, name):
        f = getattr(self._v, name)
        if name in _V_OPS:
            def wrapped(*a, **k):
                r = f(*a, **k)
                self._v.drain()
                return r
            return wrapped
        return f



_S_OPS = {"activation", "copy", "add", "mul", "sign", "sqrt", "square", "memzero"}


class SD:

    def __init__(self, s):
        self._s = s

    def __getattr__(self, name):
        f = getattr(self._s, name)
        if name in _S_OPS:
            def wrapped(*a, **k):
                r = f(*a, **k)
                self._s.drain()
                return r
            return wrapped
        return f


class Sched:
    def __init__(self):
        self.c = {}

    def get(self, sem):
        return self.c.setdefault(sem.name, 0)

    def add(self, sem, n):
        self.c[sem.name] = self.get(sem) + n
        return self.c[sem.name]


def build_nc():
    nc = bass.Bass("TRN2", target_bir_lowering=False, detect_race_conditions=False)

    x_in = nc.dram_tensor("x", [NT, D], F32, kind="ExternalInput").ap()
    y_out = nc.dram_tensor("y", [NT, D], F32, kind="ExternalOutput").ap()
    ident_f = nc.dram_tensor("ident_f", [128, 128], F32, kind="ExternalInput").ap()

    wg0 = nc.dram_tensor("l0wg", [4096, 1024], F32, kind="ExternalInput").ap()
    wu0 = nc.dram_tensor("l0wu", [4096, 1024], F32, kind="ExternalInput").ap()
    wd0 = nc.dram_tensor("l0wd", [1024, 4096], F32, kind="ExternalInput").ap()

    wgi0 = nc.dram_tensor("wgi0", [1024, 4096], BF16)
    wui0 = nc.dram_tensor("wui0", [1024, 4096], BF16)
    wdi0 = nc.dram_tensor("wdi0", [4096, 1024], BF16)

    l0pw = nc.dram_tensor("l0pw", [1024, 392], F32, kind="ExternalInput").ap()
    l0wg_a = nc.dram_tensor("l0wg_a", [1024, 1024], F32, kind="ExternalInput").ap()
    l0wo_a = nc.dram_tensor("l0wo_a", [1024, 1024], F32, kind="ExternalInput").ap()
    lng_r = nc.dram_tensor("lng_r", [128, 1024], F32, kind="ExternalInput").ap()
    lnb_r = nc.dram_tensor("lnb_r", [128, 1024], F32, kind="ExternalInput").ap()
    esel_d = nc.dram_tensor("esel", [128, 128], BF16, kind="ExternalInput").ap()
    eselS_d = nc.dram_tensor("eselS", [128, 4096], BF16, kind="ExternalInput").ap()
    l0pw_b = nc.dram_tensor("l0pw_b", [1024, 392], BF16)
    l0wg_ab = nc.dram_tensor("l0wg_ab", [1024, 1024], BF16)
    l0wo_ab = nc.dram_tensor("l0wo_ab", [1024, 1024], BF16)
    agx_in = nc.dram_tensor("agx_in", [1024, 2048], BF16)
    agx_out = nc.dram_tensor("agx_out", [8192, 2048], BF16)
    scanK = nc.dram_tensor("scanK", [8 * 64, 4096], BF16)
    scanQ = nc.dram_tensor("scanQ", [8 * 64, 4096], BF16)
    scanBK = nc.dram_tensor("scanBK", [8 * 64, 4096], BF16)
    scanBV = nc.dram_tensor("scanBV", [8 * 64, 4096], BF16)
    a2a0_in = nc.dram_tensor("a2a0_in", [1024, 2048], BF16)
    a2a0_out = nc.dram_tensor("a2a0_out", [1024, 2048], BF16)

    wg1 = nc.dram_tensor("l1wg", [2816, 1024], F32, kind="ExternalInput").ap()
    wu1 = nc.dram_tensor("l1wu", [2816, 1024], F32, kind="ExternalInput").ap()
    wd1 = nc.dram_tensor("l1wd", [1024, 2816], F32, kind="ExternalInput").ap()
    wgq1 = nc.dram_tensor("wgq1", [1024, 2816], BF16)
    wuq1 = nc.dram_tensor("wuq1", [1024, 2816], BF16)
    wdq1 = nc.dram_tensor("wdq1", [2816, 1024], BF16)

    wdq_a = nc.dram_tensor("l1wdq", [1024, 4096], F32, kind="ExternalInput").ap()
    wuq_a = nc.dram_tensor("l1wuq", [4096, 1024], F32, kind="ExternalInput").ap()
    wqr_a = nc.dram_tensor("l1wqr", [4096, 512], F32, kind="ExternalInput").ap()
    wkv_a = nc.dram_tensor("l1wkv", [1024, 288], F32, kind="ExternalInput").ap()
    wukv_c = nc.dram_tensor("l1wukv", [256, 256], F32, kind="ExternalInput").ap()
    wo_a = nc.dram_tensor("l1wo", [1024, 1024], F32, kind="ExternalInput").ap()
    scal_t = nc.dram_tensor("scal_t", [128, 512], F32, kind="ExternalInput").ap()
    mask4_d = nc.dram_tensor("mask4", [128, 2048], BF16, kind="ExternalInput").ap()
    wdq_ab = nc.dram_tensor("wdq_ab", [1024, 4096], BF16)
    wuqr_ab = nc.dram_tensor("wuqr_ab", [4096, 1536], BF16)
    wkv_ab = nc.dram_tensor("wkv_ab", [1024, 288], BF16)
    wukv_cb = nc.dram_tensor("wukv_cb", [256, 256], BF16)
    wo_ab = nc.dram_tensor("wo_ab", [1024, 1024], BF16)
    agkv_in = nc.dram_tensor("agkv_in", [2048, 288], BF16)
    agkv_out = nc.dram_tensor("agkv_out", [16384, 288], BF16)
    cqT_scr = nc.dram_tensor("cqT_scr", [4096, 2048], BF16)
    a2aq_in = nc.dram_tensor("a2aq_in", [16384, 192], BF16)
    a2aq_out = nc.dram_tensor("a2aq_out", [16384, 192], BF16)
    a2ay_in = nc.dram_tensor("a2ay_in", [16384, 128], BF16)
    a2ay_out = nc.dram_tensor("a2ay_out", [16384, 128], BF16)

    from contextlib import ExitStack
    with ExitStack() as _es:
        x_res = _es.enter_context(nc.sbuf_tensor("x_res", [128, TC * D], F32))
        sbA = _es.enter_context(nc.sbuf_tensor("sbA", [128, 8192], BF16))
        sbB = _es.enter_context(nc.sbuf_tensor("sbB", [128, 8192], BF16))
        sbC = _es.enter_context(nc.sbuf_tensor("sbC", [128, 8192], BF16))
        sbD = _es.enter_context(nc.sbuf_tensor("sbD", [128, 4096], F32))
        sbE = _es.enter_context(nc.sbuf_tensor("sbE", [128, 2048], F32))
        wb0 = _es.enter_context(nc.sbuf_tensor("wb0", [128, 8192], BF16))
        wb1 = _es.enter_context(nc.sbuf_tensor("wb1", [128, 8192], BF16))
        cols = _es.enter_context(nc.sbuf_tensor("cols", [128, 64], F32))
        idf = _es.enter_context(nc.sbuf_tensor("idf", [128, 128], F32))
        esl = _es.enter_context(nc.sbuf_tensor("esl", [128, 128], BF16))
        eslS = _es.enter_context(nc.sbuf_tensor("eslS", [128, 4096], BF16))
        lngb = _es.enter_context(nc.sbuf_tensor("lngb", [128, 2048], F32))
        sct = _es.enter_context(nc.sbuf_tensor("sct", [128, 512], F32))
        msk = _es.enter_context(nc.sbuf_tensor("msk", [128, 2048], BF16))
        psA = _es.enter_context(nc.psum_tensor("psA", [128, 3072], F32))
        psB = _es.enter_context(nc.psum_tensor("psB", [128, 1024], F32))
        ld = _es.enter_context(nc.semaphore("ld"))
        gd = _es.enter_context(nc.semaphore("gd"))
        td = _es.enter_context(nc.semaphore("td"))
        vd = _es.enter_context(nc.semaphore("vd"))
        sd = _es.enter_context(nc.semaphore("sd"))
        st = _es.enter_context(nc.semaphore("st"))
        cc = _es.enter_context(nc.semaphore("cc"))
        wl0 = _es.enter_context(nc.semaphore("wl0"))
        wl1 = _es.enter_context(nc.semaphore("wl1"))
        dl0 = _es.enter_context(nc.semaphore("dl0"))
        dl1 = _es.enter_context(nc.semaphore("dl1"))
        dl2 = _es.enter_context(nc.semaphore("dl2"))
        dl3 = _es.enter_context(nc.semaphore("dl3"))
        r0 = _es.enter_context(nc.semaphore("r0"))
        r1 = _es.enter_context(nc.semaphore("r1"))
        block = _es.enter_context(nc.Block())
        S = Sched()
        wbufs = [wb0, wb1]

        def bank(i):
            return psA[:, i * 512:(i + 1) * 512]

        def pb(slot):
            return psB[:, slot * 512:slot * 512 + 128]

        @block.sync
        def _(s):
            for tc in range(TC):
                s.dma_start(out=x_res[:, tc * D:(tc + 1) * D],
                            in_=x_in[tc * 128:(tc + 1) * 128, :]).then_inc(ld, 16)
            s.dma_start(out=idf[:], in_=ident_f[:]).then_inc(ld, 16)
            s.dma_start(out=esl[:], in_=esel_d[:]).then_inc(ld, 16)
            s.dma_start(out=eslS[:], in_=eselS_d[:]).then_inc(ld, 16)
            s.dma_start(out=lngb[:, 0:1024], in_=lng_r[:]).then_inc(ld, 16)
            s.dma_start(out=lngb[:, 1024:2048], in_=lnb_r[:]).then_inc(ld, 16)
            s.dma_start(out=sct[:], in_=scal_t[:]).then_inc(ld, 16)
            s.dma_start(out=msk[:], in_=mask4_d[:]).then_inc(ld, 16)
        ld_x = S.add(ld, 16 * (TC + 7))

        def emit_transposes(src_f32, dst_fn, nchunks, extra_tensor_wait=None):
            t_b = S.get(td)
            s_b = S.get(sd)

            @block.tensor
            def _(t):
                if extra_tensor_wait is not None:
                    sem, val = extra_tensor_wait
                    t.wait_ge(sem, val)
                t.wait_ge(sd, s_b)
                for f in range(nchunks):
                    if f >= 2:
                        t.wait_ge(sd, s_b + (f - 1))
                    t.transpose(out=pb(f % 2),
                                in_=src_f32[:, f * 128:(f + 1) * 128],
                                identity=idf[:]).then_inc(td, 1)

            @block.scalar
            def _(sc):
                sc = SD(sc)
                for f in range(nchunks):
                    sc.wait_ge(td, t_b + f + 1)
                    sc.copy(out=dst_fn(f), in_=pb(f % 2)).then_inc(sd, 1)

            S.add(td, nchunks)
            S.add(sd, nchunks)

        def l0_wq(w_nat, w_stash, R, Cin, sw_i, isw_i):
            ntile = R // 128
            nch = Cin // 128
            fb = [wb0[:].bitcast(F32), wb1[:].bitcast(F32)]
            ld_b = S.get(ld)
            vd_b = S.get(vd)

            @block.sync
            def _(s):
                s.wait_ge(vd, vd_b)
                for it in range(ntile):
                    if it >= 1:
                        s.wait_ge(vd, vd_b + it)
                    s.dma_start(out=fb[it % 2][:, 0:Cin],
                                in_=w_nat[it * 128:(it + 1) * 128, :]).then_inc(ld, 16)
            S.add(ld, 16 * ntile)

            @block.vector
            def _(v):
                v = VD(v)
                for it in range(ntile):
                    v.wait_ge(ld, ld_b + 16 * (it + 1))
                    v.tensor_reduce(out=cols[:, 17:18], in_=fb[it % 2][:, 0:Cin],
                                    axis=AX.X, op=OP.add, apply_absolute_value=True)
                    if it == 0:
                        v.tensor_copy(cols[:, 16:17], cols[:, 17:18]
                                      ).then_inc(vd, 1)
                    else:
                        v.tensor_tensor(out=cols[:, 16:17], in0=cols[:, 16:17],
                                        in1=cols[:, 17:18], op=OP.add
                                        ).then_inc(vd, 1)
            S.add(vd, ntile)
            vd_p1 = S.get(vd)

            @block.tensor
            def _(t):
                t.wait_ge(vd, vd_p1)
                t.transpose(out=pb(0)[0:1, :], in_=cols[:, 16:17],
                            identity=idf[:]).then_inc(td, 1)
            td_tr = S.add(td, 1)

            @block.vector
            def _(v):
                v = VD(v)
                v.wait_ge(td, td_tr)
                v.tensor_reduce(out=cols[0:1, 18:19], in_=pb(0)[0:1, :],
                                axis=AX.X, op=OP.add)
                v.tensor_scalar(out=cols[0:1, 18:19], in0=cols[0:1, 18:19],
                                scalar1=1.0 / (R * Cin), scalar2=1e-5,
                                op0=OP.mult, op1=OP.max)
                v.reciprocal(out=cols[0:1, 19:20], in_=cols[0:1, 18:19]
                             ).then_inc(vd, 1)
            vd_sc = S.add(vd, 1)

            @block.gpsimd
            def _(g):
                g.wait_ge(vd, vd_sc)
                g.dma_start(out=cols[:, sw_i:sw_i + 1],
                            in_=cols[0:1, 19:20].unsqueeze(1)
                            .broadcast_to([1, 128, 1])).then_inc(gd, 16)
                g.dma_start(out=cols[:, isw_i:isw_i + 1],
                            in_=cols[0:1, 18:19].unsqueeze(1)
                            .broadcast_to([1, 128, 1])).then_inc(gd, 16)
            gd_sc = S.add(gd, 32)

            ld_b2 = S.get(ld)
            vd_b2 = S.get(vd)

            @block.sync
            def _(s):
                s.wait_ge(vd, vd_p1)
                for it in range(ntile):
                    if it >= 1:
                        s.wait_ge(vd, vd_b2 + it)
                    s.dma_start(out=fb[it % 2][:, 0:Cin],
                                in_=w_nat[it * 128:(it + 1) * 128, :]).then_inc(ld, 16)
            S.add(ld, 16 * ntile)

            for it in range(ntile):
                sd_prev = S.get(sd)

                @block.vector
                def _(v, it=it, st_prev=S.get(st)):
                    v = VD(v)
                    v.wait_ge(st, st_prev)
                    if it == 0:
                        v.wait_ge(gd, gd_sc)
                    v.wait_ge(ld, ld_b2 + 16 * (it + 1))
                    if it >= 1:
                        v.wait_ge(td, S.get(td))
                    v.tensor_scalar(out=sbD[:, 0:Cin], in0=fb[it % 2][:, 0:Cin],
                                    scalar1=cols[:, sw_i:sw_i + 1], scalar2=MAGIC,
                                    op0=OP.mult, op1=OP.add)
                    v.tensor_scalar(out=sbD[:, 0:Cin], in0=sbD[:, 0:Cin],
                                    scalar1=MAGIC, scalar2=1.0,
                                    op0=OP.subtract, op1=OP.min)
                    v.tensor_scalar(out=sbD[:, 0:Cin], in0=sbD[:, 0:Cin],
                                    scalar1=-1.0, scalar2=0.0,
                                    op0=OP.max, op1=OP.bypass).then_inc(vd, 1)
                vq = S.add(vd, 1)

                emit_transposes(sbD[:, 0:Cin],
                                lambda f: sbB[:, (f % 8) * 128:(f % 8 + 1) * 128],
                                nch, extra_tensor_wait=(vd, vq))
                sd_now = S.get(sd)

                @block.gpsimd
                def _(g, it=it, sd_prev=sd_prev):
                    for f in range(nch):
                        g.wait_ge(sd, sd_prev + f + 1)
                        g.dma_start(
                            out=w_stash[f * 128:(f + 1) * 128,
                                        it * 128:(it + 1) * 128],
                            in_=sbB[:, (f % 8) * 128:(f % 8 + 1) * 128]
                            ).then_inc(st, 16)
                S.add(st, 16 * nch)
                if nch > 8:
                    pass

            return

        def l0_wq_big(w_nat, w_stash, R, Cin, sw_i, isw_i):
            ntile = R // 128
            nch = Cin // 128
            fb = [wb0[:].bitcast(F32), wb1[:].bitcast(F32)]
            ld_b = S.get(ld)
            vd_b = S.get(vd)

            @block.sync
            def _(s):
                s.wait_ge(vd, vd_b)
                for it in range(ntile):
                    if it >= 1:
                        s.wait_ge(vd, vd_b + it)
                    s.dma_start(out=fb[it % 2][:, 0:Cin],
                                in_=w_nat[it * 128:(it + 1) * 128, :]).then_inc(ld, 16)
            S.add(ld, 16 * ntile)

            @block.vector
            def _(v):
                v = VD(v)
                for it in range(ntile):
                    v.wait_ge(ld, ld_b + 16 * (it + 1))
                    v.tensor_reduce(out=cols[:, 17:18], in_=fb[it % 2][:, 0:Cin],
                                    axis=AX.X, op=OP.add, apply_absolute_value=True)
                    if it == 0:
                        v.tensor_copy(cols[:, 16:17], cols[:, 17:18]
                                      ).then_inc(vd, 1)
                    else:
                        v.tensor_tensor(out=cols[:, 16:17], in0=cols[:, 16:17],
                                        in1=cols[:, 17:18], op=OP.add
                                        ).then_inc(vd, 1)
            S.add(vd, ntile)
            vd_p1 = S.get(vd)

            @block.tensor
            def _(t):
                t.wait_ge(vd, vd_p1)
                t.transpose(out=pb(0)[0:1, :], in_=cols[:, 16:17],
                            identity=idf[:]).then_inc(td, 1)
            td_tr = S.add(td, 1)

            @block.vector
            def _(v):
                v = VD(v)
                v.wait_ge(td, td_tr)
                v.tensor_reduce(out=cols[0:1, 18:19], in_=pb(0)[0:1, :],
                                axis=AX.X, op=OP.add)
                v.tensor_scalar(out=cols[0:1, 18:19], in0=cols[0:1, 18:19],
                                scalar1=1.0 / (R * Cin), scalar2=1e-5,
                                op0=OP.mult, op1=OP.max)
                v.reciprocal(out=cols[0:1, 19:20], in_=cols[0:1, 18:19]
                             ).then_inc(vd, 1)
            vd_sc = S.add(vd, 1)

            @block.gpsimd
            def _(g):
                g.wait_ge(vd, vd_sc)
                g.dma_start(out=cols[:, sw_i:sw_i + 1],
                            in_=cols[0:1, 19:20].unsqueeze(1)
                            .broadcast_to([1, 128, 1])).then_inc(gd, 16)
                g.dma_start(out=cols[:, isw_i:isw_i + 1],
                            in_=cols[0:1, 18:19].unsqueeze(1)
                            .broadcast_to([1, 128, 1])).then_inc(gd, 16)
            gd_sc = S.add(gd, 32)

            ld_b2 = S.get(ld)
            vd_b2 = S.get(vd)

            @block.sync
            def _(s):
                s.wait_ge(vd, vd_p1)
                for it in range(ntile):
                    if it >= 1:
                        s.wait_ge(vd, vd_b2 + it)
                    s.dma_start(out=fb[it % 2][:, 0:Cin],
                                in_=w_nat[it * 128:(it + 1) * 128, :]).then_inc(ld, 16)
            S.add(ld, 16 * ntile)

            for it in range(ntile):
                @block.vector
                def _(v, it=it, st_prev=S.get(st)):
                    v = VD(v)
                    v.wait_ge(st, st_prev)
                    if it == 0:
                        v.wait_ge(gd, gd_sc)
                    v.wait_ge(ld, ld_b2 + 16 * (it + 1))
                    if it >= 1:
                        v.wait_ge(td, S.get(td))
                    v.tensor_scalar(out=sbD[:, 0:Cin], in0=fb[it % 2][:, 0:Cin],
                                    scalar1=cols[:, sw_i:sw_i + 1], scalar2=MAGIC,
                                    op0=OP.mult, op1=OP.add)
                    v.tensor_scalar(out=sbD[:, 0:Cin], in0=sbD[:, 0:Cin],
                                    scalar1=MAGIC, scalar2=1.0,
                                    op0=OP.subtract, op1=OP.min)
                    v.tensor_scalar(out=sbD[:, 0:Cin], in0=sbD[:, 0:Cin],
                                    scalar1=-1.0, scalar2=0.0,
                                    op0=OP.max, op1=OP.bypass).then_inc(vd, 1)
                vq = S.add(vd, 1)

                for g0 in range(0, nch, 8):
                    gn = min(8, nch - g0)
                    st_before = S.get(st)
                    sd_before = S.get(sd)
                    emit_transposes(sbD[:, g0 * 128:(g0 + gn) * 128],
                                    lambda f: sbB[:, f * 128:(f + 1) * 128],
                                    gn, extra_tensor_wait=(vd, vq) if g0 == 0
                                    else (st, st_before))

                    @block.gpsimd
                    def _(g, sd_before=sd_before, it=it, g0=g0, gn=gn):
                        for f in range(gn):
                            g.wait_ge(sd, sd_before + f + 1)
                            g.dma_start(
                                out=w_stash[(g0 + f) * 128:(g0 + f + 1) * 128,
                                            it * 128:(it + 1) * 128],
                                in_=sbB[:, f * 128:(f + 1) * 128]).then_inc(st, 16)
                    S.add(st, 16 * gn)

        if DO_WQ:
            l0_wq(wg0, wgi0, 4096, 1024, 20, 21)
            l0_wq(wu0, wui0, 4096, 1024, 22, 23)
            l0_wq_big(wd0, wdi0, 1024, 4096, 24, 25)
        wq_st = S.get(st)

        def emit_l0ffn():
            G = 2
            xiTb = sbE[:].bitcast(BF16)
            entry_td = S.get(td)
            for grp in range(min(FFN_NGRP, TC // G)):
                grp_td = S.get(td)
                grp_vd = S.get(vd)

                for j in range(G):
                    tc = grp * G + j
                    vpre = S.get(vd)
                    spre = S.get(sd)
                    hb = sbD[:, 0:4096] if j == 0 else sbC[:].bitcast(F32)[:, 0:4096]
                    xq = hb[:, 0:1024]

                    @block.vector
                    def _(v, tc=tc, j=j, xq=xq, vpre=vpre, spre=spre, grp=grp):
                        v = VD(v)
                        if grp == 0 and j == 0:
                            v.wait_ge(ld, ld_x)
                            v.wait_ge(td, entry_td)
                        xr = x_res[:, tc * D:(tc + 1) * D]
                        v.tensor_reduce(out=cols[:, 0:1], in_=xr, axis=AX.X,
                                        op=OP.add)
                        v.scalar_tensor_tensor(out=xq, in0=xr, scalar=1.0,
                                               in1=xr, op0=OP.mult, op1=OP.mult,
                                               accum_out=cols[:, 1:2])
                        v.tensor_scalar(out=cols[:, 2:3], in0=cols[:, 0:1],
                                        scalar1=1.0 / D, scalar2=0.0,
                                        op0=OP.mult, op1=OP.bypass)
                        v.tensor_scalar(out=cols[:, 3:4], in0=cols[:, 1:2],
                                        scalar1=1.0 / D, scalar2=0.0,
                                        op0=OP.mult, op1=OP.bypass)
                        v.tensor_tensor(out=cols[:, 4:5], in0=cols[:, 2:3],
                                        in1=cols[:, 2:3], op=OP.mult)
                        v.scalar_tensor_tensor(out=cols[:, 4:5], in0=cols[:, 3:4],
                                               scalar=LN_EPS, in1=cols[:, 4:5],
                                               op0=OP.add, op1=OP.subtract
                                               ).then_inc(vd, 1)
                        v.wait_ge(sd, spre + 1)
                        v.reciprocal(out=cols[:, 5:6], in_=cols[:, 4:5])
                        v.scalar_tensor_tensor(out=cols[:, 6:7], in0=cols[:, 2:3],
                                               scalar=-1.0, in1=cols[:, 5:6],
                                               op0=OP.mult, op1=OP.mult)
                        v.tensor_scalar(out=xq, in0=xr,
                                        scalar1=cols[:, 5:6], scalar2=cols[:, 6:7],
                                        op0=OP.mult, op1=OP.add)
                        v.tensor_reduce(out=cols[:, 8:9], in_=xq, axis=AX.X,
                                        op=OP.max, apply_absolute_value=True)
                        v.tensor_scalar(out=cols[:, 8:9], in0=cols[:, 8:9],
                                        scalar1=1e-5, scalar2=0.0,
                                        op0=OP.max, op1=OP.bypass)
                        v.reciprocal(out=cols[:, 9:10], in_=cols[:, 8:9])
                        v.tensor_scalar(out=cols[:, 10 + 2 * j:11 + 2 * j],
                                        in0=cols[:, 9:10], scalar1=127.0,
                                        scalar2=0.0, op0=OP.mult, op1=OP.bypass)
                        v.tensor_scalar(out=cols[:, 11 + 2 * j:12 + 2 * j],
                                        in0=cols[:, 8:9], scalar1=1.0 / 127.0,
                                        scalar2=0.0, op0=OP.mult, op1=OP.bypass)
                        v.tensor_scalar(out=xq, in0=xq,
                                        scalar1=cols[:, 10 + 2 * j:11 + 2 * j],
                                        scalar2=MAGIC, op0=OP.mult, op1=OP.add)
                        v.tensor_scalar(out=xq, in0=xq, scalar1=MAGIC,
                                        scalar2=127.0, op0=OP.subtract, op1=OP.min)
                        v.tensor_scalar(out=xq, in0=xq, scalar1=-128.0,
                                        scalar2=0.0, op0=OP.max, op1=OP.bypass
                                        ).then_inc(vd, 1)
                    vq = S.add(vd, 2)

                    @block.scalar
                    def _(sc, vpre=vpre):
                        sc = SD(sc)
                        sc.wait_ge(vd, vpre + 1)
                        sc.activation(out=cols[:, 4:5], in_=cols[:, 4:5],
                                      func=AF.Sqrt).then_inc(sd, 1)
                    S.add(sd, 1)

                    emit_transposes(
                        xq,
                        lambda f, j=j: xiTb[:, (j * 8 + f) * 128:
                                            (j * 8 + f + 1) * 128],
                        8, extra_tensor_wait=(vd, vq))

                prep_vd = S.get(vd)
                prep_sd = S.get(sd)
                prep_td = S.get(td)
                if FFN_STAGES < 2:
                    continue

                ld_mb = S.get(ld)

                @block.sync
                def _(s, grp_td=grp_td, grp=grp, st_e=S.get(st)):
                    if grp == 0:
                        s.wait_ge(st, max(wq_st, st_e))
                        s.wait_ge(vd, grp_vd)
                        s.wait_ge(td, grp_td)
                    else:
                        s.wait_ge(td, grp_td)
                    for m in range(8):
                        if m >= 2:
                            s.wait_ge(td, prep_td + 32 * (m - 1))
                        wsem = wl0 if m % 2 == 0 else wl1
                        for k in range(8):
                            s.dma_start(
                                out=wbufs[m % 2][:, k * 512:(k + 1) * 512],
                                in_=wgi0[k * 128:(k + 1) * 128,
                                         m * 512:(m + 1) * 512]).then_inc(wsem, 16)
                            s.dma_start(
                                out=wbufs[m % 2][:, 4096 + k * 512:
                                                 4096 + (k + 1) * 512],
                                in_=wui0[k * 128:(k + 1) * 128,
                                         m * 512:(m + 1) * 512]).then_inc(wsem, 16)
                for m in range(8):
                    S.add(wl0 if m % 2 == 0 else wl1, 256)

                @block.tensor
                def _(t, ld_mb=ld_mb, prep_vd=prep_vd, prep_sd=prep_sd,
                      prep_td=prep_td, wl0_b=S.get(wl0) - 4 * 256,
                      wl1_b=S.get(wl1) - 4 * 256):
                    t.wait_ge(vd, prep_vd)
                    t.wait_ge(sd, prep_sd)
                    for m in range(8):
                        if m % 2 == 0:
                            t.wait_ge(wl0, wl0_b + 256 * (m // 2 + 1))
                        else:
                            t.wait_ge(wl1, wl1_b + 256 * (m // 2 + 1))
                        if m >= 1:
                            t.wait_ge(vd, prep_vd + 4 + 3 * m)
                            t.wait_ge(sd, prep_sd + 2 * m)
                        for j in range(G):
                            for k in range(8):
                                t.matmul(bank(j),
                                         xiTb[:, (j * 8 + k) * 128:
                                              (j * 8 + k + 1) * 128],
                                         wbufs[m % 2][:, k * 512:(k + 1) * 512],
                                         start=(k == 0), stop=(k == 7))
                            for k in range(8):
                                mm = t.matmul(bank(2 + j),
                                              xiTb[:, (j * 8 + k) * 128:
                                                   (j * 8 + k + 1) * 128],
                                              wbufs[m % 2][:, 4096 + k * 512:
                                                           4096 + (k + 1) * 512],
                                              start=(k == 0), stop=(k == 7))
                                if j == G - 1 and k == 7:
                                    mm.then_inc(td, 32)
                S.add(td, 32 * 8)

                @block.scalar
                def _(sc, prep_td=prep_td, prep_vd=prep_vd):
                    sc = SD(sc)
                    sc.wait_ge(vd, prep_vd + 4)
                    for m in range(8):
                        sc.wait_ge(td, prep_td + 32 * (m + 1))
                        for j in range(G):
                            sc.activation(
                                out=sbB[:, (j * 8 + m) * 512:
                                        (j * 8 + m + 1) * 512],
                                in_=bank(j), func=AF.Silu,
                                scale=cols[:, 26 + j:27 + j]).then_inc(sd, 1)
                S.add(sd, 16)

                @block.vector
                def _(v, prep_sd=prep_sd):
                    v = VD(v)
                    v.tensor_tensor(out=cols[:, 26:27], in0=cols[:, 11:12],
                                    in1=cols[:, 21:22], op=OP.mult)
                    v.tensor_tensor(out=cols[:, 27:28], in0=cols[:, 13:14],
                                    in1=cols[:, 21:22], op=OP.mult)
                    v.tensor_tensor(out=cols[:, 28:29], in0=cols[:, 11:12],
                                    in1=cols[:, 23:24], op=OP.mult)
                    v.tensor_tensor(out=cols[:, 29:30], in0=cols[:, 13:14],
                                    in1=cols[:, 23:24], op=OP.mult)
                    v.memset(cols[:, 32:34], 0.0).then_inc(vd, 4)
                    for m in range(8):
                        for j in range(G):
                            v.wait_ge(sd, prep_sd + 2 * m + j + 1)
                            hb = (sbD[:, 0:4096] if j == 0
                                  else sbC[:].bitcast(F32)[:, 0:4096])
                            v.scalar_tensor_tensor(
                                out=hb[:, m * 512:(m + 1) * 512],
                                in0=bank(2 + j), scalar=cols[:, 28 + j:29 + j],
                                in1=sbB[:, (j * 8 + m) * 512:
                                        (j * 8 + m + 1) * 512],
                                op0=OP.mult, op1=OP.mult)
                            v.tensor_reduce(out=cols[:, 34:35],
                                            in_=hb[:, m * 512:(m + 1) * 512],
                                            axis=AX.X, op=OP.max,
                                            apply_absolute_value=True)
                            tt = v.tensor_tensor(out=cols[:, 32 + j:33 + j],
                                                 in0=cols[:, 32 + j:33 + j],
                                                 in1=cols[:, 34:35], op=OP.max)
                            if j == G - 1:
                                tt.then_inc(vd, 3)
                S.add(vd, 4 + 24)

                if FFN_STAGES < 3:
                    continue
                for j in range(G):
                    hb = sbD[:, 0:4096] if j == 0 else sbC[:].bitcast(F32)[:, 0:4096]

                    @block.vector
                    def _(v, j=j, hb=hb):
                        v = VD(v)
                        v.tensor_scalar(out=cols[:, 35:36],
                                        in0=cols[:, 32 + j:33 + j],
                                        scalar1=1e-5, scalar2=0.0,
                                        op0=OP.max, op1=OP.bypass)
                        v.reciprocal(out=cols[:, 36:37], in_=cols[:, 35:36])
                        v.tensor_scalar(out=cols[:, 37:38], in0=cols[:, 36:37],
                                        scalar1=127.0, scalar2=0.0,
                                        op0=OP.mult, op1=OP.bypass)
                        v.tensor_scalar(out=cols[:, 30 + j:31 + j],
                                        in0=cols[:, 35:36], scalar1=1.0 / 127.0,
                                        scalar2=0.0, op0=OP.mult, op1=OP.bypass)
                        v.tensor_scalar(out=hb, in0=hb, scalar1=cols[:, 37:38],
                                        scalar2=MAGIC, op0=OP.mult, op1=OP.add)
                        v.tensor_scalar(out=hb, in0=hb, scalar1=MAGIC,
                                        scalar2=127.0, op0=OP.subtract, op1=OP.min)
                        v.tensor_scalar(out=hb, in0=hb, scalar1=-128.0,
                                        scalar2=0.0, op0=OP.max, op1=OP.bypass
                                        ).then_inc(vd, 1)
                    vq = S.add(vd, 1)

                    emit_transposes(
                        hb,
                        lambda f, j=j: sbA[:, (f * G + j) * 128:
                                           (f * G + j + 1) * 128],
                        32, extra_tensor_wait=(vd, vq))

                hq_sd = S.get(sd)
                hq_td = S.get(td)

                if FFN_STAGES < 4:
                    continue
                ld_db = S.get(ld)

                @block.sync
                def _(s, ld_db=ld_db, hq_td=hq_td, prep_td=prep_td):
                    s.wait_ge(td, prep_td + 32 * 8)
                    dsems = [dl0, dl1, dl2, dl3]
                    for k in range(32):
                        if k >= 4:
                            s.wait_ge(td, hq_td + 4 * (k - 3))
                        s.dma_start(out=wbufs[(k % 4) // 2][:, (k % 2) * 1024:
                                                            (k % 2 + 1) * 1024],
                                    in_=wdi0[k * 128:(k + 1) * 128, :]
                                    ).then_inc(dsems[k % 4], 16)
                for k in range(32):
                    S.add([dl0, dl1, dl2, dl3][k % 4], 16)

                @block.tensor
                def _(t, ld_db=ld_db, hq_sd=hq_sd,
                      dbs=[S.get(x) - 8 * 16 for x in [dl0, dl1, dl2, dl3]]):
                    t.wait_ge(sd, hq_sd)
                    dsems = [dl0, dl1, dl2, dl3]
                    for k in range(32):
                        t.wait_ge(dsems[k % 4], dbs[k % 4] + 16 * (k // 4 + 1))
                        wv = wbufs[(k % 4) // 2][:, (k % 2) * 1024:
                                                 (k % 2 + 1) * 1024]
                        for j in range(G):
                            for mo in range(2):
                                mm = t.matmul(bank(j * 2 + mo),
                                              sbA[:, (k * G + j) * 128:
                                                  (k * G + j + 1) * 128],
                                              wv[:, mo * 512:(mo + 1) * 512],
                                              start=(k == 0), stop=(k == 31))
                                if j == G - 1 and mo == 1:
                                    mm.then_inc(td, 4)
                S.add(td, 4 * 32)
                dn_td = S.get(td)

                @block.vector
                def _(v, dn_td=dn_td, grp=grp):
                    v = VD(v)
                    v.wait_ge(td, dn_td)
                    for j in range(G):
                        tc = grp * G + j
                        v.tensor_tensor(out=cols[:, 38:39],
                                        in0=cols[:, 30 + j:31 + j],
                                        in1=cols[:, 25:26], op=OP.mult)
                        v.tensor_scalar(out=cols[:, 38:39], in0=cols[:, 38:39],
                                        scalar1=0.5, scalar2=0.0,
                                        op0=OP.mult, op1=OP.bypass)
                        for mo in range(2):
                            xr = x_res[:, tc * D + mo * 512:
                                       tc * D + (mo + 1) * 512]
                            stt = v.scalar_tensor_tensor(
                                out=xr, in0=bank(j * 2 + mo),
                                scalar=cols[:, 38:39], in1=xr,
                                op0=OP.mult, op1=OP.add)
                            if j == G - 1 and mo == 1:
                                stt.then_inc(vd, 1)
                S.add(vd, 1)



        def cast_stash(w_in, w_out, R, C, oc0=0):
            ntile = R // 128
            fview = wb0[:].bitcast(F32)
            stage = wb1
            ld_b = S.get(ld)
            vd_b = S.get(vd)
            st_b = S.get(st)

            @block.sync
            def _(s):
                s.wait_ge(vd, vd_b)
                s.wait_ge(td, S.get(td))
                for it in range(ntile):
                    if it >= 1:
                        s.wait_ge(vd, vd_b + it)
                    s.dma_start(out=fview[:, 0:C],
                                in_=w_in[it * 128:(it + 1) * 128, :]).then_inc(ld, 16)
            S.add(ld, 16 * ntile)

            @block.vector
            def _(v):
                v = VD(v)
                for it in range(ntile):
                    v.wait_ge(ld, ld_b + 16 * (it + 1))
                    if it >= 1:
                        v.wait_ge(st, st_b + 16 * it)
                    v.tensor_copy(stage[:, 0:C], fview[:, 0:C]).then_inc(vd, 1)
            S.add(vd, ntile)

            @block.gpsimd
            def _(g):
                for it in range(ntile):
                    g.wait_ge(vd, vd_b + it + 1)
                    g.dma_start(out=w_out[it * 128:(it + 1) * 128, oc0:oc0 + C],
                                in_=stage[:, 0:C]).then_inc(st, 16)
            S.add(st, 16 * ntile)

        def emit_l0attn():
            cast_stash(l0pw, l0pw_b, 1024, 392)
            cast_stash(l0wg_a, l0wg_ab, 1024, 1024)
            cast_stash(l0wo_a, l0wo_ab, 1024, 1024)

            entry_td = S.get(td)
            xhat = sbD[:, 0:1024]
            stg = sbA
            for tc in range(TC):
                vpre = S.get(vd)
                spre = S.get(sd)
                st_pre = S.get(st)

                @block.vector
                def _(v, tc=tc, vpre=vpre, spre=spre, st_pre=st_pre):
                    v = VD(v)
                    if tc == 0:
                        v.wait_ge(td, entry_td)
                    else:
                        v.wait_ge(st, st_pre)
                    xr = x_res[:, tc * D:(tc + 1) * D]
                    v.tensor_reduce(out=cols[:, 0:1], in_=xr, axis=AX.X, op=OP.add)
                    v.scalar_tensor_tensor(out=xhat, in0=xr, scalar=1.0,
                                           in1=xr, op0=OP.mult, op1=OP.mult,
                                           accum_out=cols[:, 1:2])
                    v.tensor_scalar(out=cols[:, 2:3], in0=cols[:, 0:1],
                                    scalar1=1.0 / D, scalar2=0.0,
                                    op0=OP.mult, op1=OP.bypass)
                    v.tensor_scalar(out=cols[:, 3:4], in0=cols[:, 1:2],
                                    scalar1=1.0 / D, scalar2=0.0,
                                    op0=OP.mult, op1=OP.bypass)
                    v.tensor_tensor(out=cols[:, 4:5], in0=cols[:, 2:3],
                                    in1=cols[:, 2:3], op=OP.mult)
                    v.scalar_tensor_tensor(out=cols[:, 4:5], in0=cols[:, 3:4],
                                           scalar=LN_EPS, in1=cols[:, 4:5],
                                           op0=OP.add, op1=OP.subtract
                                           ).then_inc(vd, 1)
                    v.wait_ge(sd, spre + 1)
                    v.reciprocal(out=cols[:, 5:6], in_=cols[:, 4:5])
                    v.scalar_tensor_tensor(out=cols[:, 6:7], in0=cols[:, 2:3],
                                           scalar=-1.0, in1=cols[:, 5:6],
                                           op0=OP.mult, op1=OP.mult)
                    v.tensor_scalar(out=xhat, in0=xr,
                                    scalar1=cols[:, 5:6], scalar2=cols[:, 6:7],
                                    op0=OP.mult, op1=OP.add).then_inc(vd, 1)

                @block.scalar
                def _(sc, vpre=vpre):
                    sc = SD(sc)
                    sc.wait_ge(vd, vpre + 1)
                    sc.activation(out=cols[:, 4:5], in_=cols[:, 4:5],
                                  func=AF.Sqrt).then_inc(sd, 1)
                S.add(vd, 2)
                S.add(sd, 1)
                vq = S.get(vd)

                emit_transposes(xhat,
                                lambda f: stg[:, f * 128:(f + 1) * 128],
                                8, extra_tensor_wait=(vd, vq))
                sd_now = S.get(sd)

                @block.gpsimd
                def _(g, tc=tc, sd_now=sd_now):
                    g.wait_ge(sd, sd_now)
                    for f in range(8):
                        g.dma_start(
                            out=agx_in[f * 128:(f + 1) * 128,
                                       tc * 128:(tc + 1) * 128],
                            in_=stg[:, f * 128:(f + 1) * 128]).then_inc(st, 16)
                S.add(st, 16 * 8)

            st_ag = S.get(st)

            @block.gpsimd
            def _(g):
                g.wait_ge(st, st_ag)
                g.collective_compute(
                    "AllGather", OP.bypass,
                    replica_groups=[list(range(NCORES))],
                    ins=[agx_in.ap().opt()],
                    outs=[agx_out.ap().opt()],
                ).then_inc(cc, 1)
            gd_ag = S.add(cc, 1)

            pw = sbC
            ld_pw = S.get(ld)

            @block.sync
            def _(s):
                s.wait_ge(vd, S.get(vd))
                for f in range(8):
                    s.dma_start(out=pw[:, f * 392:(f + 1) * 392],
                                in_=l0pw_b[f * 128:(f + 1) * 128, :]
                                ).then_inc(ld, 16)
            S.add(ld, 16 * 8)
            ld_pw_done = S.get(ld)

            nbt = 4 * 32
            td_proj0 = S.get(td)
            for ib in range(nbt):
                b = ib // 32
                tb = ib % 32
                r = 2 * b + (1 if tb >= 16 else 0)
                col0 = 128 * (tb % 16)
                xs = sbB[:, (ib % 2) * 1024:(ib % 2) * 1024 + 1024]
                ld_pre = S.get(ld)
                td_pre = S.get(td)
                vd_pre = S.get(vd)
                sd_pre = S.get(sd)
                st_pre = S.get(st)

                @block.sync
                def _(s, ib=ib, r=r, col0=col0, xs=xs):
                    if ib == 0:
                        s.wait_ge(cc, gd_ag)
                        s.wait_ge(ld, ld_pw_done)
                    if ib >= 2:
                        s.wait_ge(td, td_proj0 + 5 * (ib - 2) + 1)
                    xsem = dl0 if ib % 2 == 0 else dl1
                    for f in range(8):
                        s.dma_start(out=xs[:, f * 128:(f + 1) * 128],
                                    in_=agx_out[1024 * r + f * 128:
                                                1024 * r + (f + 1) * 128,
                                                col0:col0 + 128]).then_inc(xsem, 16)
                S.add(dl0 if ib % 2 == 0 else dl1, 16 * 8)

                @block.tensor
                def _(t, ib=ib, xs=xs, vd_pre=vd_pre,
                      sd_pre=sd_pre, xv=S.get(dl0 if ib % 2 == 0 else dl1)):
                    t.wait_ge(dl0 if ib % 2 == 0 else dl1, xv)
                    if ib >= 1:
                        t.wait_ge(vd, vd_pre)
                        t.wait_ge(sd, sd_pre)
                    for f in range(8):
                        mm = t.matmul(psA[:, 0:392], xs[:, f * 128:(f + 1) * 128],
                                      pw[:, f * 392:(f + 1) * 392],
                                      start=(f == 0), stop=(f == 7))
                    mm.then_inc(td, 1)
                S.add(td, 1)
                td_mm = S.get(td)

                @block.scalar
                def _(sc, td_mm=td_mm, vd_pre=vd_pre):
                    sc = SD(sc)
                    sc.wait_ge(td, td_mm)
                    for i in range(4):
                        sc.activation(out=sbD[:, 512:576], in_=psA[:, i * 64:(i + 1) * 64],
                                      func=AF.Square, accum_out=cols[:, 8 + i:9 + i])
                    sc.activation(out=cols[:, 12:14], in_=psA[:, 384:386],
                                  func=AF.Sigmoid).then_inc(sd, 1)
                S.add(sd, 1)
                sd_sq = S.get(sd)

                @block.vector
                def _(v, sd_sq=sd_sq):
                    v = VD(v)
                    v.wait_ge(sd, sd_sq)
                    v.tensor_copy(cols[:, 16:20], cols[:, 8:12]).then_inc(vd, 1)
                S.add(vd, 1)
                vd_cp = S.get(vd)

                @block.scalar
                def _(sc, vd_cp=vd_cp):
                    sc = SD(sc)
                    sc.wait_ge(vd, vd_cp)
                    sc.activation(out=cols[:, 16:20], in_=cols[:, 16:20],
                                  func=AF.Sqrt).then_inc(sd, 1)
                S.add(sd, 1)
                sd_sqrt = S.get(sd)

                @block.vector
                def _(v, sd_sqrt=sd_sqrt, ib=ib, st_pre=st_pre):
                    v = VD(v)
                    v.wait_ge(sd, sd_sqrt)
                    if ib >= 1:
                        v.wait_ge(st, st_pre)
                    v.tensor_scalar(out=cols[:, 16:20], in0=cols[:, 16:20],
                                    scalar1=1e-12, scalar2=0.0,
                                    op0=OP.max, op1=OP.bypass)
                    v.reciprocal(out=cols[:, 16:20], in_=cols[:, 16:20])
                    v.tensor_scalar(out=cols[:, 14:16], in0=cols[:, 12:14],
                                    scalar1=-1.0, scalar2=0.0,
                                    op0=OP.mult, op1=OP.bypass)
                    for i in range(4):
                        v.tensor_scalar(out=sbD[:, i * 64:(i + 1) * 64],
                                        in0=psA[:, i * 64:(i + 1) * 64],
                                        scalar1=cols[:, 16 + i:17 + i], scalar2=0.0,
                                        op0=OP.mult, op1=OP.bypass)
                    for hh in range(2):
                        v.tensor_scalar(out=sbD[:, (4 + hh) * 64:(5 + hh) * 64],
                                        in0=sbD[:, (2 + hh) * 64:(3 + hh) * 64],
                                        scalar1=cols[:, 14 + hh:15 + hh], scalar2=0.0,
                                        op0=OP.mult, op1=OP.bypass)
                    for hh in range(2):
                        tsv = v.tensor_scalar(out=sbD[:, (6 + hh) * 64:(7 + hh) * 64],
                                              in0=psA[:, (4 + hh) * 64:(5 + hh) * 64],
                                              scalar1=cols[:, 12 + hh:13 + hh],
                                              scalar2=0.0,
                                              op0=OP.mult, op1=OP.bypass)
                        if hh == 1:
                            tsv.then_inc(vd, 1)
                S.add(vd, 1)
                vq = S.get(vd)

                emit_transposes(sbD[:, 0:512],
                                lambda f: stg[:, 1024 + f * 128:1024 + (f + 1) * 128],
                                4, extra_tensor_wait=(vd, vq))
                sd_tr = S.get(sd)

                @block.gpsimd
                def _(g, b=b, tb=tb, sd_tr=sd_tr):
                    g.wait_ge(sd, sd_tr)
                    for f, dst in enumerate([scanQ, scanK, scanBK, scanBV]):
                        for hh in range(2):
                            p = b * 2 + hh
                            g.dma_start(
                                out=dst[p * 64:(p + 1) * 64,
                                        tb * 128:(tb + 1) * 128],
                                in_=stg[hh * 64:(hh + 1) * 64,
                                        1024 + f * 128:1024 + (f + 1) * 128]
                            ).then_inc(st, 16)
                S.add(st, 16 * 8)

            proj_st = S.get(st)
            proj_vd = S.get(vd)
            proj_sd = S.get(sd)
            proj_td = S.get(td)

            sbDb = sbD[:].bitcast(BF16)
            sbEb = sbE[:].bitcast(BF16)
            zcol = cols[:, 60:61]
            onec = cols[:, 61:62]

            @block.vector
            def _(v):
                v = VD(v)
                v.wait_ge(st, proj_st)
                v.wait_ge(td, proj_td)
                v.wait_ge(sd, proj_sd)
                v.memset(zcol, 0.0)
                v.memset(onec, 1.0).then_inc(vd, 1)
            S.add(vd, 1)
            scan_v0 = S.get(vd)

            for p in range(8):
                ld_p = S.get(ld)

                @block.sync
                def _(s, p=p, ld_p=ld_p):
                    if p == 0:
                        s.wait_ge(vd, scan_v0)
                    else:
                        s.wait_ge(vd, S.get(vd))
                    for half in range(2):
                        s.dma_start(out=sbA[half * 64:(half + 1) * 64, 0:4096],
                                    in_=scanK[p * 64:(p + 1) * 64, :]
                                    ).then_inc(ld, 16)
                        s.dma_start(out=sbA[half * 64:(half + 1) * 64, 4096:8192],
                                    in_=scanQ[p * 64:(p + 1) * 64, :]
                                    ).then_inc(ld, 16)
                    s.dma_start(out=sbB[0:64, 0:4096],
                                in_=scanBK[p * 64:(p + 1) * 64, :]).then_inc(ld, 16)
                    s.dma_start(out=sbB[64:128, 0:4096],
                                in_=scanBV[p * 64:(p + 1) * 64, :]).then_inc(ld, 16)
                S.add(ld, 16 * 6)
                ld_pair = S.get(ld)

                pe_sd = S.get(sd)
                vbases = []
                evac_end = [None] * 32
                last_evac_end = pe_sd

                def _bkr(tau):
                    return sbB[:, 4096:8192] if tau % 2 == 0 else wb1[:, 0:4096]

                def _bvr(tau):
                    return sbC[:, 0:4096] if tau % 2 == 0 else wb1[:, 4096:8192]

                def emit_repl(tau, first, p=p, ld_pair=ld_pair):
                    nonlocal last_evac_end
                    td_r0 = S.get(td)
                    evac_base = S.get(sd)
                    vd_guard = (vbases[tau - 2] + 3) if tau >= 2 else 0
                    sd_guard = last_evac_end

                    @block.tensor
                    def _(t, tau=tau, first=first, td_r0=td_r0,
                          evac_base=evac_base, vd_guard=vd_guard,
                          sd_guard=sd_guard, ld_pair=ld_pair):
                        if first:
                            t.wait_ge(ld, ld_pair)
                        if vd_guard:
                            t.wait_ge(vd, vd_guard)
                        t.wait_ge(sd, sd_guard)
                        for c in range(8):
                            if c >= 2:
                                t.wait_ge(sd, evac_base + 2 * (c - 1))
                            t.matmul(bank(4 + c % 2),
                                     eslS[0:64, tau * 128:(tau + 1) * 128],
                                     sbB[0:64, c * 512:(c + 1) * 512],
                                     start=True, stop=True).then_inc(td, 1)
                            t.matmul(psB[:, (c % 2) * 512:(c % 2) * 512 + 512],
                                     eslS[64:128, tau * 128:(tau + 1) * 128],
                                     sbB[64:128, c * 512:(c + 1) * 512],
                                     start=True, stop=True).then_inc(td, 1)
                    S.add(td, 16)

                    @block.scalar
                    def _(sc, tau=tau, td_r0=td_r0):
                        sc = SD(sc)
                        bkr, bvr = _bkr(tau), _bvr(tau)
                        for c in range(8):
                            sc.wait_ge(td, td_r0 + 2 * c + 1)
                            sc._s.copy(out=bkr[:, c * 512:(c + 1) * 512],
                                       in_=bank(4 + c % 2)).then_inc(sd, 1)
                            sc.wait_ge(td, td_r0 + 2 * c + 2)
                            sc._s.copy(out=bvr[:, c * 512:(c + 1) * 512],
                                       in_=psB[:, (c % 2) * 512:
                                               (c % 2) * 512 + 512]
                                       ).then_inc(sd, 1)
                    S.add(sd, 16)
                    last_evac_end = S.get(sd)
                    evac_end[tau] = last_evac_end

                emit_repl(0, True)
                emit_repl(1, False)

                for tau in range(32):
                    vd_pre = S.get(vd)
                    sd_pre = S.get(sd)
                    vbases.append(vd_pre)

                    sq_guard = S.get(td)

                    @block.vector
                    def _(v, p=p, tau=tau, vd_pre=vd_pre, sd_pre=sd_pre,
                          ev_end=evac_end[tau], sq_guard=sq_guard):
                        v = VD(v)
                        bkr, bvr = _bkr(tau), _bvr(tau)
                        v.wait_ge(sd, ev_end)
                        v._v.tensor_tensor(out=sbD[:, 0:2048],
                                           in0=sbA[:, 0:2048],
                                           in1=bkr[:, 0:2048],
                                           op=OP.mult).then_inc(vd, 1)
                        v._v.tensor_tensor(out=sbD[:, 2048:4096],
                                           in0=sbA[:, 2048:4096],
                                           in1=bkr[:, 2048:4096],
                                           op=OP.mult).then_inc(vd, 1)
                        v.tensor_tensor(out=sbC[:, 4096:8192], in0=sbA[:, 0:4096],
                                        in1=bvr[:], op=OP.mult).then_inc(vd, 1)
                        v.wait_ge(sd, sd_pre + 2)
                        v.tensor_tensor_scan(out=sbEb[:, 0:4096],
                                             data0=sbD[:, 0:4096],
                                             data1=sbC[:, 4096:8192],
                                             initial=zcol,
                                             op0=OP.mult, op1=OP.add
                                             ).then_inc(vd, 1)
                        v.wait_ge(td, sq_guard)
                        v._v.tensor_tensor(out=wb0[:, 4096:8192],
                                           in0=sbEb[:, 0:4096],
                                           in1=sbA[:, 4096:8192], op=OP.mult
                                           ).then_inc(vd, 1)
                    S.add(vd, 5)

                    @block.scalar
                    def _(sc, vd_pre=vd_pre):
                        sc = SD(sc)
                        sc.wait_ge(vd, vd_pre + 1)
                        sc._s.activation(out=sbD[:, 0:2048], in_=sbD[:, 0:2048],
                                         func=AF.Identity, bias=onec,
                                         scale=1.0).then_inc(sd, 1)
                        sc.wait_ge(vd, vd_pre + 2)
                        sc._s.activation(out=sbD[:, 2048:4096],
                                         in_=sbD[:, 2048:4096],
                                         func=AF.Identity, bias=onec,
                                         scale=1.0).then_inc(sd, 1)
                    S.add(sd, 2)

                    if tau + 2 <= 31:
                        emit_repl(tau + 2, False)

                    @block.tensor
                    def _(t, tau=tau, vd_pre=vd_pre, pe_sd=pe_sd):
                        t.wait_ge(vd, vd_pre + 5)
                        if tau == 0:
                            t.wait_ge(sd, pe_sd)
                        for m in range(8):
                            pbank = psA[(m // 4) * 64:(m // 4) * 64 + 64,
                                        (m % 4) * 512:(m % 4 + 1) * 512]
                            mm = t.matmul(pbank,
                                          esl[:, 62 - 2 * tau:126 - 2 * tau],
                                          wb0[:, 4096 + m * 512:4096 + (m + 1) * 512],
                                          start=(tau == 0), stop=(tau == 31),
                                          skip_group_check=True)
                            if m == 7:
                                mm.then_inc(td, 1)
                    S.add(td, 1)

                td_pair = S.get(td)

                @block.scalar
                def _(sc, td_pair=td_pair):
                    sc = SD(sc)
                    sc.wait_ge(td, td_pair)
                    for m in range(8):
                        pbank = psA[(m // 4) * 64:(m // 4) * 64 + 64,
                                    (m % 4) * 512:(m % 4 + 1) * 512]
                        sc._s.copy(out=wb0[(m // 4) * 64:(m // 4) * 64 + 64,
                                           (m % 4) * 512:(m % 4 + 1) * 512],
                                   in_=pbank).then_inc(sd, 1)
                S.add(sd, 8)

                sd_pair = S.get(sd)
                b = p // 2
                hh = p % 2

                @block.gpsimd
                def _(g, b=b, hh=hh, sd_pair=sd_pair):
                    g.wait_ge(sd, sd_pair)
                    for jh in range(2):
                        j = 2 * b + jh
                        g.dma_start(
                            out=a2a0_in[j * 128 + hh * 64:j * 128 + (hh + 1) * 64, :],
                            in_=wb0[jh * 64:(jh + 1) * 64, 0:2048]
                            ).then_inc(st, 16)
                S.add(st, 16 * 2)


            st_a2a = S.get(st)

            @block.gpsimd
            def _(g):
                g.wait_ge(st, st_a2a)
                with allow_ici_all_to_all():
                    g.collective_compute(
                        "AllToAll", OP.bypass,
                        replica_groups=[list(range(NCORES))],
                        ins=[a2a0_in.ap().opt()],
                        outs=[a2a0_out.ap().opt()],
                    ).then_inc(cc, 1)
            gd_a2a = S.add(cc, 1)

            sbEf = sbE
            for tc in range(TC):
                ld_pre = S.get(ld)
                vd_pre = S.get(vd)
                sd_pre = S.get(sd)
                td_pre = S.get(td)

                @block.sync
                def _(s, tc=tc, ld_pre=ld_pre, vd_pre=vd_pre, td_pre=td_pre):
                    if tc == 0:
                        s.wait_ge(cc, gd_a2a)
                    else:
                        s.wait_ge(vd, vd_pre)
                    for f in range(8):
                        s.dma_start(out=sbB[:, f * 128:(f + 1) * 128],
                                    in_=a2a0_out[f * 128:(f + 1) * 128,
                                                 tc * 128:(tc + 1) * 128]
                                    ).then_inc(ld, 16)
                    for f in range(8):
                        s.dma_start(out=sbB[:, 1024 + f * 128:1024 + (f + 1) * 128],
                                    in_=agx_in[f * 128:(f + 1) * 128,
                                               tc * 128:(tc + 1) * 128]
                                    ).then_inc(ld, 16)
                    for m in range(2):
                        for f in range(8):
                            s.dma_start(
                                out=wb1[:, (m * 8 + f) * 512:(m * 8 + f + 1) * 512],
                                in_=l0wg_ab[f * 128:(f + 1) * 128,
                                            m * 512:(m + 1) * 512]).then_inc(ld, 16)
                S.add(ld, 16 * 32)
                ld_ao = ld_pre + 16 * 8
                ld_xh = ld_pre + 16 * 16
                ld_wg = ld_pre + 16 * 32

                @block.vector
                def _(v, tc=tc, ld_pre=ld_pre, sd_pre=sd_pre, td_pre=td_pre):
                    v = VD(v)
                    v.wait_ge(ld, ld_pre + 16 * 32)
                    v.tensor_copy(sbEf[:, 0:1024], sbB[:, 0:1024]).then_inc(vd, 1)
                S.add(vd, 1)
                vq = S.get(vd)

                emit_transposes(sbEf[:, 0:1024],
                                lambda f: sbD[:, f * 128:(f + 1) * 128],
                                8, extra_tensor_wait=(vd, vq))
                sd_ao = S.get(sd)

                @block.vector
                def _(v, tc=tc, sd_ao=sd_ao, td_pre=td_pre):
                    v = VD(v)
                    v.wait_ge(sd, sd_ao)
                    ao = sbD[:, 0:1024]
                    v.tensor_reduce(out=cols[:, 0:1], in_=ao, axis=AX.X, op=OP.add)
                    v.scalar_tensor_tensor(out=sbD[:, 1024:2048], in0=ao, scalar=1.0,
                                           in1=ao, op0=OP.mult, op1=OP.mult,
                                           accum_out=cols[:, 1:2])
                    v.tensor_scalar(out=cols[:, 2:3], in0=cols[:, 0:1],
                                    scalar1=1.0 / D, scalar2=0.0,
                                    op0=OP.mult, op1=OP.bypass)
                    v.tensor_scalar(out=cols[:, 3:4], in0=cols[:, 1:2],
                                    scalar1=1.0 / D, scalar2=0.0,
                                    op0=OP.mult, op1=OP.bypass)
                    v.tensor_tensor(out=cols[:, 4:5], in0=cols[:, 2:3],
                                    in1=cols[:, 2:3], op=OP.mult)
                    v.scalar_tensor_tensor(out=cols[:, 4:5], in0=cols[:, 3:4],
                                           scalar=LN_EPS, in1=cols[:, 4:5],
                                           op0=OP.add, op1=OP.subtract
                                           ).then_inc(vd, 1)
                S.add(vd, 1)
                vd_ln = S.get(vd)

                @block.scalar
                def _(sc, vd_ln=vd_ln):
                    sc = SD(sc)
                    sc.wait_ge(vd, vd_ln)
                    sc.activation(out=cols[:, 4:5], in_=cols[:, 4:5],
                                  func=AF.Sqrt).then_inc(sd, 1)
                S.add(sd, 1)
                sd_sq2 = S.get(sd)

                @block.vector
                def _(v, sd_sq2=sd_sq2):
                    v = VD(v)
                    v.wait_ge(sd, sd_sq2)
                    ao = sbD[:, 0:1024]
                    v.reciprocal(out=cols[:, 5:6], in_=cols[:, 4:5])
                    v.scalar_tensor_tensor(out=cols[:, 6:7], in0=cols[:, 2:3],
                                           scalar=-1.0, in1=cols[:, 5:6],
                                           op0=OP.mult, op1=OP.mult)
                    v.tensor_scalar(out=ao, in0=ao,
                                    scalar1=cols[:, 5:6], scalar2=cols[:, 6:7],
                                    op0=OP.mult, op1=OP.add)
                    v.tensor_tensor(out=ao, in0=ao, in1=lngb[:, 0:1024],
                                    op=OP.mult)
                    v.tensor_tensor(out=ao, in0=ao, in1=lngb[:, 1024:2048],
                                    op=OP.add).then_inc(vd, 1)
                S.add(vd, 1)
                vd_ao = S.get(vd)

                @block.tensor
                def _(t, ld_pre=ld_pre, vd_pre=vd_pre, sd_pre=sd_pre, tc=tc):
                    t.wait_ge(ld, ld_pre + 16 * 32)
                    if tc >= 1:
                        t.wait_ge(vd, vd_pre)
                    for m in range(2):
                        for f in range(8):
                            mm = t.matmul(psA[:, m * 512:(m + 1) * 512],
                                          sbB[:, 1024 + f * 128:1024 + (f + 1) * 128],
                                          wb1[:, (m * 8 + f) * 512:(m * 8 + f + 1) * 512],
                                          start=(f == 0), stop=(f == 7))
                    mm.then_inc(td, 1)
                S.add(td, 1)
                td_g = S.get(td)

                @block.scalar
                def _(sc, td_g=td_g):
                    sc = SD(sc)
                    sc.wait_ge(td, td_g)
                    for m in range(2):
                        sc.activation(out=sbA[:, 1024 + m * 512:1024 + (m + 1) * 512],
                                      in_=psA[:, m * 512:(m + 1) * 512],
                                      func=AF.Silu).then_inc(sd, 1)
                S.add(sd, 2)
                sd_g = S.get(sd)

                @block.vector
                def _(v, sd_g=sd_g, vd_ao=vd_ao):
                    v = VD(v)
                    v.wait_ge(sd, sd_g)
                    v.tensor_tensor(out=sbD[:, 1024:2048], in0=sbD[:, 0:1024],
                                    in1=sbA[:, 1024:2048], op=OP.mult
                                    ).then_inc(vd, 1)
                S.add(vd, 1)
                vq2 = S.get(vd)

                emit_transposes(sbD[:, 1024:2048],
                                lambda f: sbA[:, f * 128:(f + 1) * 128],
                                8, extra_tensor_wait=(vd, vq2))
                sd_pt = S.get(sd)

                ld_w2 = S.get(ld)

                @block.sync
                def _(s, ld_w2=ld_w2, sd_pt=sd_pt, td_pre=td_pre):
                    s.wait_ge(sd, sd_pt)
                    for m in range(2):
                        for f in range(8):
                            s.dma_start(
                                out=wb1[:, (m * 8 + f) * 512:(m * 8 + f + 1) * 512],
                                in_=l0wo_ab[f * 128:(f + 1) * 128,
                                            m * 512:(m + 1) * 512]).then_inc(ld, 16)
                S.add(ld, 16 * 16)
                ld_wo = S.get(ld)

                @block.tensor
                def _(t, ld_wo=ld_wo, sd_pt=sd_pt):
                    t.wait_ge(ld, ld_wo)
                    t.wait_ge(sd, sd_pt)
                    for m in range(2):
                        for f in range(8):
                            mm = t.matmul(psA[:, 1024 + m * 512:1024 + (m + 1) * 512],
                                          sbA[:, f * 128:(f + 1) * 128],
                                          wb1[:, (m * 8 + f) * 512:(m * 8 + f + 1) * 512],
                                          start=(f == 0), stop=(f == 7))
                    mm.then_inc(td, 1)
                S.add(td, 1)
                td_o = S.get(td)

                @block.vector
                def _(v, td_o=td_o, tc=tc):
                    v = VD(v)
                    v.wait_ge(td, td_o)
                    for m in range(2):
                        xr = x_res[:, tc * D + m * 512:tc * D + (m + 1) * 512]
                        v.tensor_tensor(out=xr, in0=psA[:, 1024 + m * 512:
                                                        1024 + (m + 1) * 512],
                                        in1=xr, op=OP.add).then_inc(vd, 1)
                S.add(vd, 2)

        MAGICF = MAGIC

        def emit_fp4(v, x_ap, dst, C, u, t1, t2, sc0):
            nb = C // 16
            bm = sbE[:, sc0:sc0 + nb]
            sc_ = sbE[:, sc0 + nb:sc0 + 2 * nb]
            f1 = sbE[:, sc0 + 2 * nb:sc0 + 3 * nb]
            sh = sbE[:, sc0 + 3 * nb:sc0 + 4 * nb]
            x3 = x_ap.rearrange("p (b n) -> p b n", n=16)
            v.tensor_reduce(out=bm, in_=x3, axis=AX.X, op=OP.max,
                            apply_absolute_value=True)
            v.tensor_scalar(out=sc_, in0=bm, scalar1=1.0 / 6.0, scalar2=1e-12,
                            op0=OP.mult, op1=OP.max)
            v.reciprocal(out=f1, in_=sc_)
            v.tensor_scalar(out=sh, in0=sc_, scalar1=0.5, scalar2=0.0,
                            op0=OP.mult, op1=OP.bypass)
            v.tensor_scalar(out=f1, in0=f1, scalar1=2.0, scalar2=0.0,
                            op0=OP.mult, op1=OP.bypass)
            ub = u.rearrange("p (b n) -> p b n", n=16)
            v.tensor_tensor(out=ub, in0=x3,
                            in1=f1[:].unsqueeze(2).broadcast_to([128, nb, 16]),
                            op=OP.mult)
            v.tensor_scalar(out=t1, in0=u, scalar1=MAGICF, scalar2=MAGICF,
                            op0=OP.add, op1=OP.subtract)
            v.tensor_scalar(out=dst, in0=t1, scalar1=4.0, scalar2=-4.0,
                            op0=OP.min, op1=OP.max)
            v.tensor_scalar(out=t1, in0=u, scalar1=0.5, scalar2=MAGICF,
                            op0=OP.mult, op1=OP.add)
            v.tensor_scalar(out=t1, in0=t1, scalar1=MAGICF, scalar2=2.0,
                            op0=OP.subtract, op1=OP.mult)
            v.tensor_scalar(out=t2, in0=t1, scalar1=8.0, scalar2=-8.0,
                            op0=OP.min, op1=OP.max)
            v.tensor_tensor(out=dst, in0=dst, in1=t2, op=OP.add)
            v.tensor_scalar(out=t2, in0=t1, scalar1=4.0, scalar2=-4.0,
                            op0=OP.min, op1=OP.max)
            v.tensor_tensor(out=dst, in0=dst, in1=t2, op=OP.subtract)
            v.tensor_scalar(out=t1, in0=u, scalar1=0.25, scalar2=MAGICF,
                            op0=OP.mult, op1=OP.add)
            v.tensor_scalar(out=t1, in0=t1, scalar1=MAGICF, scalar2=4.0,
                            op0=OP.subtract, op1=OP.mult)
            v.tensor_scalar(out=t2, in0=t1, scalar1=12.0, scalar2=-12.0,
                            op0=OP.min, op1=OP.max)
            v.tensor_tensor(out=dst, in0=dst, in1=t2, op=OP.add)
            v.tensor_scalar(out=t2, in0=t1, scalar1=8.0, scalar2=-8.0,
                            op0=OP.min, op1=OP.max)
            v.tensor_tensor(out=dst, in0=dst, in1=t2, op=OP.subtract)
            db = dst.rearrange("p (b n) -> p b n", n=16)
            v.tensor_tensor(out=db, in0=db,
                            in1=sh[:].unsqueeze(2).broadcast_to([128, nb, 16]),
                            op=OP.mult)

        def l1_wq(w_nat, w_stash, R, C):
            ntile = R // 128
            nch = C // 128
            fw = wb0[:].bitcast(F32)
            u_ = sbD[:, 0:C]
            t1_ = wb1[:].bitcast(F32)[:, 0:C]
            t2_ = sbA[:].bitcast(F32)[:, 0:C]
            dst_ = sbC[:].bitcast(F32)[:, 0:C]
            ld_b = S.get(ld)
            vd_b = S.get(vd)

            @block.sync
            def _(s):
                s.wait_ge(vd, vd_b)
                s.wait_ge(td, S.get(td))
                s.wait_ge(st, S.get(st))
                for it in range(ntile):
                    if it >= 1:
                        s.wait_ge(vd, vd_b + it)
                    s.dma_start(out=fw[:, 0:C],
                                in_=w_nat[it * 128:(it + 1) * 128, :]).then_inc(ld, 16)
            S.add(ld, 16 * ntile)

            for it in range(ntile):
                st_pre = S.get(st)

                @block.vector
                def _(v, it=it, st_pre=st_pre):
                    v = VD(v)
                    v.wait_ge(ld, ld_b + 16 * (it + 1))
                    if it >= 1:
                        v.wait_ge(td, S.get(td))
                        v.wait_ge(st, st_pre)
                    emit_fp4(v, fw[:, 0:C], dst_, C, u_, t1_, t2_, 1024)
                    v.memset(cols[:, 40:41], 0.0).then_inc(vd, 1)
                S.add(vd, 1)
                vq = S.get(vd)

                for g0 in range(0, nch, 8):
                    gn = min(8, nch - g0)
                    sd_b2 = S.get(sd)
                    emit_transposes(dst_[:, g0 * 128:(g0 + gn) * 128],
                                    lambda f: sbB[:, f * 128:(f + 1) * 128],
                                    gn, extra_tensor_wait=(vd, vq) if g0 == 0
                                    else (st, S.get(st)))

                    @block.gpsimd
                    def _(g, it=it, g0=g0, gn=gn, sd_b2=sd_b2):
                        for f in range(gn):
                            g.wait_ge(sd, sd_b2 + f + 1)
                            g.dma_start(
                                out=w_stash[(g0 + f) * 128:(g0 + f + 1) * 128,
                                            it * 128:(it + 1) * 128],
                                in_=sbB[:, f * 128:(f + 1) * 128]).then_inc(st, 16)
                    S.add(st, 16 * gn)

        def emit_l1ffn():
            G = 2
            INNER = 2816
            MSL = [512, 512, 512, 512, 512, 256]
            xqTb = sbE[:].bitcast(BF16)
            for grp in range(TC // G):
                grp_td = S.get(td)
                grp_vd = S.get(vd)
                grp_st = S.get(st)

                for j in range(G):
                    tc = grp * G + j
                    vpre = S.get(vd)
                    spre = S.get(sd)
                    xhbuf = sbD[:, 0:1024]
                    xqf = sbC[:].bitcast(F32)[:, 0:1024] if j == 0 else                         sbC[:].bitcast(F32)[:, 1024:2048]

                    @block.vector
                    def _(v, tc=tc, j=j, vpre=vpre, spre=spre, grp=grp,
                          xqf=xqf, xhbuf=xhbuf):
                        v = VD(v)
                        if grp == 0 and j == 0:
                            v.wait_ge(td, grp_td)
                            v.wait_ge(st, grp_st)
                        xr = x_res[:, tc * D:(tc + 1) * D]
                        v.tensor_reduce(out=cols[:, 0:1], in_=xr, axis=AX.X,
                                        op=OP.add)
                        v.scalar_tensor_tensor(out=xhbuf, in0=xr, scalar=1.0,
                                               in1=xr, op0=OP.mult, op1=OP.mult,
                                               accum_out=cols[:, 1:2])
                        v.tensor_scalar(out=cols[:, 2:3], in0=cols[:, 0:1],
                                        scalar1=1.0 / D, scalar2=0.0,
                                        op0=OP.mult, op1=OP.bypass)
                        v.tensor_scalar(out=cols[:, 3:4], in0=cols[:, 1:2],
                                        scalar1=1.0 / D, scalar2=0.0,
                                        op0=OP.mult, op1=OP.bypass)
                        v.tensor_tensor(out=cols[:, 4:5], in0=cols[:, 2:3],
                                        in1=cols[:, 2:3], op=OP.mult)
                        v.scalar_tensor_tensor(out=cols[:, 4:5], in0=cols[:, 3:4],
                                               scalar=LN_EPS, in1=cols[:, 4:5],
                                               op0=OP.add, op1=OP.subtract
                                               ).then_inc(vd, 1)
                        v.wait_ge(sd, spre + 1)
                        v.reciprocal(out=cols[:, 5:6], in_=cols[:, 4:5])
                        v.scalar_tensor_tensor(out=cols[:, 6:7], in0=cols[:, 2:3],
                                               scalar=-1.0, in1=cols[:, 5:6],
                                               op0=OP.mult, op1=OP.mult)
                        v.tensor_scalar(out=xhbuf, in0=xr,
                                        scalar1=cols[:, 5:6], scalar2=cols[:, 6:7],
                                        op0=OP.mult, op1=OP.add)
                        emit_fp4(v, xhbuf, xqf, 1024,
                                 sbD[:, 1024:2048], sbD[:, 2048:3072],
                                 sbD[:, 3072:4096], 1024)
                        v.memset(cols[:, 41:42], 0.0).then_inc(vd, 1)

                    @block.scalar
                    def _(sc, vpre=vpre):
                        sc = SD(sc)
                        sc.wait_ge(vd, vpre + 1)
                        sc.activation(out=cols[:, 4:5], in_=cols[:, 4:5],
                                      func=AF.Sqrt).then_inc(sd, 1)
                    S.add(vd, 2)
                    S.add(sd, 1)
                    vq = S.get(vd)

                    emit_transposes(
                        xqf,
                        lambda f, j=j: xqTb[:, (j * 8 + f) * 128:
                                            (j * 8 + f + 1) * 128],
                        8, extra_tensor_wait=(vd, vq))

                prep_vd = S.get(vd)
                prep_sd = S.get(sd)
                prep_td = S.get(td)

                @block.sync
                def _(s, grp_td=grp_td, grp=grp, prep_td=prep_td,
                      prep_vd=prep_vd):
                    s.wait_ge(td, grp_td)
                    for m in range(6):
                        if m >= 2:
                            s.wait_ge(td, prep_td + (m - 1))
                        wsem = wl0 if m % 2 == 0 else wl1
                        msl = MSL[m]
                        for k in range(8):
                            s.dma_start(
                                out=wbufs[m % 2][:, k * 512:k * 512 + msl],
                                in_=wgq1[k * 128:(k + 1) * 128,
                                         m * 512:m * 512 + msl]).then_inc(wsem, 16)
                            s.dma_start(
                                out=wbufs[m % 2][:, 4096 + k * 512:
                                                 4096 + k * 512 + msl],
                                in_=wuq1[k * 128:(k + 1) * 128,
                                         m * 512:m * 512 + msl]).then_inc(wsem, 16)
                for m in range(6):
                    S.add(wl0 if m % 2 == 0 else wl1, 256)

                @block.tensor
                def _(t, prep_vd=prep_vd, prep_sd=prep_sd, prep_td=prep_td,
                      wlb=[S.get(wl0) - 3 * 256, S.get(wl1) - 3 * 256]):
                    t.wait_ge(vd, prep_vd)
                    t.wait_ge(sd, prep_sd)
                    for m in range(6):
                        t.wait_ge(wl0 if m % 2 == 0 else wl1,
                                  wlb[m % 2] + 256 * (m // 2 + 1))
                        if m >= 1:
                            t.wait_ge(vd, prep_vd + m)
                            t.wait_ge(sd, prep_sd + 2 * m)
                        msl = MSL[m]
                        for j in range(G):
                            for k in range(8):
                                t.matmul(bank(j)[:, 0:msl],
                                         xqTb[:, (j * 8 + k) * 128:
                                              (j * 8 + k + 1) * 128],
                                         wbufs[m % 2][:, k * 512:k * 512 + msl],
                                         start=(k == 0), stop=(k == 7))
                            for k in range(8):
                                mm = t.matmul(bank(2 + j)[:, 0:msl],
                                              xqTb[:, (j * 8 + k) * 128:
                                                   (j * 8 + k + 1) * 128],
                                              wbufs[m % 2][:, 4096 + k * 512:
                                                           4096 + k * 512 + msl],
                                              start=(k == 0), stop=(k == 7))
                                if j == G - 1 and k == 7:
                                    mm.then_inc(td, 1)
                S.add(td, 6)

                @block.scalar
                def _(sc, prep_td=prep_td):
                    sc = SD(sc)
                    for m in range(6):
                        sc.wait_ge(td, prep_td + (m + 1))
                        msl = MSL[m]
                        for j in range(G):
                            sc.activation(
                                out=sbB[:, (j * 6 + m) * 512:
                                        (j * 6 + m) * 512 + msl],
                                in_=bank(j)[:, 0:msl],
                                func=AF.Silu).then_inc(sd, 1)
                S.add(sd, 12)

                @block.vector
                def _(v, prep_sd=prep_sd):
                    v = VD(v)
                    for m in range(6):
                        msl = MSL[m]
                        for j in range(G):
                            v.wait_ge(sd, prep_sd + 2 * m + j + 1)
                            hb = (sbD[:, 0:2816] if j == 0
                                  else sbC[:].bitcast(F32)[:, 0:2816])
                            tt = v.tensor_tensor(
                                out=hb[:, m * 512:m * 512 + msl],
                                in0=bank(2 + j)[:, 0:msl],
                                in1=sbB[:, (j * 6 + m) * 512:
                                        (j * 6 + m) * 512 + msl],
                                op=OP.mult)
                            if j == G - 1:
                                tt.then_inc(vd, 1)
                S.add(vd, 6)

                for j in range(G):
                    hb = sbD[:, 0:2816] if j == 0 else sbC[:].bitcast(F32)[:, 0:2816]
                    hq = wb0[:].bitcast(F32)[:, 0:2816]
                    t1j = (sbA[:].bitcast(F32)[:, 0:2816] if j == 0
                           else sbD[:, 0:2816])
                    td_now = S.get(td)

                    @block.vector
                    def _(v, j=j, hb=hb, hq=hq, t1j=t1j, td_now=td_now):
                        v = VD(v)
                        if j == 1:
                            v.wait_ge(td, td_now)
                        emit_fp4(v, hb, hq, 2816,
                                 wb1[:].bitcast(F32)[:, 0:2816],
                                 t1j,
                                 sbB[:].bitcast(F32)[:, 0:2816], 1024)
                        v.memset(cols[:, 42:43], 0.0).then_inc(vd, 1)
                    S.add(vd, 1)
                    vq = S.get(vd)

                    emit_transposes(
                        hq,
                        lambda f, j=j: sbA[:, (f * G + j) * 128:
                                           (f * G + j + 1) * 128],
                        22, extra_tensor_wait=(vd, vq))

                hq_sd = S.get(sd)
                hq_td = S.get(td)

                @block.sync
                def _(s, hq_td=hq_td, prep_td=prep_td):
                    s.wait_ge(td, hq_td)
                    dsems = [dl0, dl1, dl2, dl3]
                    for k in range(22):
                        if k >= 4:
                            s.wait_ge(td, hq_td + (k - 3))
                        s.dma_start(out=wbufs[(k % 4) // 2][:, (k % 2) * 1024:
                                                            (k % 2 + 1) * 1024],
                                    in_=wdq1[k * 128:(k + 1) * 128, :]
                                    ).then_inc(dsems[k % 4], 16)
                for k in range(22):
                    S.add([dl0, dl1, dl2, dl3][k % 4], 16)

                @block.tensor
                def _(t, hq_sd=hq_sd,
                      dbs=[S.get(x) for x in [dl0, dl1, dl2, dl3]],
                      dcnt=[(22 + 3 - i) // 4 for i in range(4)]):
                    t.wait_ge(sd, hq_sd)
                    dsems = [dl0, dl1, dl2, dl3]
                    for k in range(22):
                        t.wait_ge(dsems[k % 4],
                                  dbs[k % 4] - 16 * dcnt[k % 4] + 16 * (k // 4 + 1))
                        wv = wbufs[(k % 4) // 2][:, (k % 2) * 1024:
                                                 (k % 2 + 1) * 1024]
                        for j in range(G):
                            for mo in range(2):
                                mm = t.matmul(bank(j * 2 + mo),
                                              sbA[:, (k * G + j) * 128:
                                                  (k * G + j + 1) * 128],
                                              wv[:, mo * 512:(mo + 1) * 512],
                                              start=(k == 0), stop=(k == 21))
                                if j == G - 1 and mo == 1:
                                    mm.then_inc(td, 1)
                S.add(td, 22)
                dn_td = S.get(td)

                @block.vector
                def _(v, dn_td=dn_td, grp=grp):
                    v = VD(v)
                    v.wait_ge(td, dn_td)
                    for j in range(G):
                        tc = grp * G + j
                        for mo in range(2):
                            xr = x_res[:, tc * D + mo * 512:
                                       tc * D + (mo + 1) * 512]
                            stt = v.scalar_tensor_tensor(
                                out=xr, in0=bank(j * 2 + mo),
                                scalar=0.5, in1=xr,
                                op0=OP.mult, op1=OP.add)
                            if j == G - 1 and mo == 1:
                                stt.then_inc(vd, 1)
                S.add(vd, 1)


        def emit_l1attn():
            sbDb_ = sbD[:].bitcast(BF16)
            sbDb2 = [sbDb_[:, 0:512], sbDb_[:, 512:1024]]
            sbEb_ = sbE[:].bitcast(BF16)
            sbEb2 = [sbEb_[:, 2048:2336], sbEb_[:, 2336:2624],
                     sbEb_[:, 2624:2880]]

            def pya(i):
                return psA[:, 1024 + i * 512:1024 + i * 512 + 65]

            cast_stash(wdq_a, wdq_ab, 1024, 4096)
            cast_stash(wuq_a, wuqr_ab, 4096, 1024, oc0=0)
            cast_stash(wqr_a, wuqr_ab, 4096, 512, oc0=1024)
            cast_stash(wkv_a, wkv_ab, 1024, 288)
            cast_stash(wukv_c, wukv_cb, 256, 256)
            cast_stash(wo_a, wo_ab, 1024, 1024)

            entry_td = S.get(td)
            for tc in range(TC):
                vpre = S.get(vd)
                spre = S.get(sd)
                st_pre = S.get(st)
                xhat = sbD[:, 0:1024]

                @block.vector
                def _(v, tc=tc, vpre=vpre, spre=spre, st_pre=st_pre):
                    v = VD(v)
                    if tc == 0:
                        v.wait_ge(td, entry_td)
                        v.wait_ge(st, st_pre)
                    xr = x_res[:, tc * D:(tc + 1) * D]
                    v.tensor_reduce(out=cols[:, 0:1], in_=xr, axis=AX.X, op=OP.add)
                    v.scalar_tensor_tensor(out=xhat, in0=xr, scalar=1.0,
                                           in1=xr, op0=OP.mult, op1=OP.mult,
                                           accum_out=cols[:, 1:2])
                    v.tensor_scalar(out=cols[:, 2:3], in0=cols[:, 0:1],
                                    scalar1=1.0 / D, scalar2=0.0,
                                    op0=OP.mult, op1=OP.bypass)
                    v.tensor_scalar(out=cols[:, 3:4], in0=cols[:, 1:2],
                                    scalar1=1.0 / D, scalar2=0.0,
                                    op0=OP.mult, op1=OP.bypass)
                    v.tensor_tensor(out=cols[:, 4:5], in0=cols[:, 2:3],
                                    in1=cols[:, 2:3], op=OP.mult)
                    v.scalar_tensor_tensor(out=cols[:, 4:5], in0=cols[:, 3:4],
                                           scalar=LN_EPS, in1=cols[:, 4:5],
                                           op0=OP.add, op1=OP.subtract
                                           ).then_inc(vd, 1)
                    v.wait_ge(sd, spre + 1)
                    v.reciprocal(out=cols[:, 5:6], in_=cols[:, 4:5])
                    v.scalar_tensor_tensor(out=cols[:, 6:7], in0=cols[:, 2:3],
                                           scalar=-1.0, in1=cols[:, 5:6],
                                           op0=OP.mult, op1=OP.mult)
                    v.tensor_scalar(out=xhat, in0=xr,
                                    scalar1=cols[:, 5:6], scalar2=cols[:, 6:7],
                                    op0=OP.mult, op1=OP.add).then_inc(vd, 1)

                @block.scalar
                def _(sc, vpre=vpre):
                    sc = SD(sc)
                    sc.wait_ge(vd, vpre + 1)
                    sc.activation(out=cols[:, 4:5], in_=cols[:, 4:5],
                                  func=AF.Sqrt).then_inc(sd, 1)
                S.add(vd, 2)
                S.add(sd, 1)
                vq = S.get(vd)

                emit_transposes(xhat,
                                lambda f: sbA[:, f * 128:(f + 1) * 128],
                                8, extra_tensor_wait=(vd, vq))
                xt_sd = S.get(sd)

                ldq_b = [S.get(wl0), S.get(wl1)]

                @block.sync
                def _(s, tc=tc, xt_sd=xt_sd, vpre=vpre):
                    if tc >= 1:
                        s.wait_ge(td, S.get(td))
                    for m in range(8):
                        wsem = wl0 if m % 2 == 0 else wl1
                        for k in range(8):
                            s.dma_start(
                                out=wbufs[m % 2][:, k * 512:(k + 1) * 512],
                                in_=wdq_ab[k * 128:(k + 1) * 128,
                                           m * 512:(m + 1) * 512]).then_inc(wsem, 16)
                    for k in range(8):
                        s.dma_start(out=wb1[:, 4096 + k * 288:4096 + (k + 1) * 288],
                                    in_=wkv_ab[k * 128:(k + 1) * 128, :]
                                    ).then_inc(wl1, 16)
                for m in range(8):
                    S.add(wl0 if m % 2 == 0 else wl1, 128)
                S.add(wl1, 128)

                @block.tensor
                def _(t, tc=tc, xt_sd=xt_sd, ldq_b=ldq_b, vpre=vpre):
                    t.wait_ge(sd, xt_sd)
                    for m in range(8):
                        t.wait_ge(wl0 if m % 2 == 0 else wl1,
                                  ldq_b[m % 2] + 128 * (m // 2 + 1))
                        if m >= 2:
                            t.wait_ge(vd, vpre + 2 + (m - 1))
                        for k in range(8):
                            mm = t.matmul(bank(m % 2),
                                          sbA[:, k * 128:(k + 1) * 128],
                                          wbufs[m % 2][:, k * 512:(k + 1) * 512],
                                          start=(k == 0), stop=(k == 7))
                        mm.then_inc(td, 1)
                    t.wait_ge(wl1, ldq_b[1] + 128 * 4 + 128)
                    for k in range(8):
                        mm = t.matmul(bank(2)[:, 0:288],
                                      sbA[:, k * 128:(k + 1) * 128],
                                      wb1[:, 4096 + k * 288:4096 + (k + 1) * 288],
                                      start=(k == 0), stop=(k == 7))
                    mm.then_inc(td, 1)
                S.add(td, 9)
                td_cq = S.get(td)

                @block.vector
                def _(v, tc=tc, vpre=vpre, td_b=td_cq - 9):
                    v = VD(v)
                    for m in range(8):
                        v.wait_ge(td, td_b + m + 1)
                        v.tensor_copy(sbD[:, m * 512:(m + 1) * 512],
                                      bank(m % 2)).then_inc(vd, 1)
                    v.wait_ge(td, td_b + 9)
                    v.tensor_copy(sbB[:, 0:256], bank(2)[:, 0:256])
                    v.tensor_tensor(out=sbB[:, 256:288], in0=bank(2)[:, 256:288],
                                    in1=sct[:, tc * 32:(tc + 1) * 32], op=OP.mult
                                    ).then_inc(vd, 1)
                S.add(vd, 9)
                vd_kv = S.get(vd)

                @block.gpsimd
                def _(g, tc=tc, vd_kv=vd_kv):
                    g.wait_ge(vd, vd_kv)
                    g.dma_start(out=agkv_in[tc * 128:(tc + 1) * 128, :],
                                in_=sbB[:, 0:288]).then_inc(st, 16)
                S.add(st, 16)
                vq2 = S.get(vd)

                emit_transposes(sbD[:, 0:4096],
                                lambda f: sbA[:, 1024 + f * 128:1024 + (f + 1) * 128],
                                32, extra_tensor_wait=(vd, vq2))
                cqt_sd = S.get(sd)

                @block.gpsimd
                def _(g, tc=tc, cqt_sd=cqt_sd):
                    g.wait_ge(sd, cqt_sd)
                    for f in range(32):
                        g.dma_start(
                            out=cqT_scr[f * 128:(f + 1) * 128,
                                        tc * 128:(tc + 1) * 128],
                            in_=sbA[:, 1024 + f * 128:1024 + (f + 1) * 128]
                            ).then_inc(st, 16)
                S.add(st, 16 * 32)

            a1_st = S.get(st)
            for grp in range(8):
                g_td = S.get(td)
                g_vd = S.get(vd)
                ldb = [S.get(wl0), S.get(wl1)]
                ldc = [S.get(dl0), S.get(dl1)]

                @block.sync
                def _(s, grp=grp, g_td=g_td, a1_st=a1_st):
                    if grp == 0:
                        s.wait_ge(st, a1_st)
                    else:
                        s.wait_ge(td, g_td)
                    for k in range(32):
                        wsem = wl0 if k % 2 == 0 else wl1
                        xsem = dl0 if k % 2 == 0 else dl1
                        if k >= 2:
                            s.wait_ge(td, g_td + (k - 1) * 1)
                        s.dma_start(out=wbufs[k % 2][:, 0:1536],
                                    in_=wuqr_ab[k * 128:(k + 1) * 128, :]
                                    ).then_inc(wsem, 16)
                        for j in range(2):
                            s.dma_start(
                                out=wbufs[k % 2][:, 2048 + j * 128:
                                                 2048 + (j + 1) * 128],
                                in_=cqT_scr[k * 128:(k + 1) * 128,
                                            (grp * 2 + j) * 128:
                                            (grp * 2 + j + 1) * 128]
                                ).then_inc(xsem, 16)
                for k in range(32):
                    S.add(wl0 if k % 2 == 0 else wl1, 16)
                    S.add(dl0 if k % 2 == 0 else dl1, 32)

                @block.tensor
                def _(t, grp=grp, ldb=ldb, ldc=ldc, g_vd=g_vd):
                    t.wait_ge(vd, g_vd)
                    for k in range(32):
                        t.wait_ge(wl0 if k % 2 == 0 else wl1,
                                  ldb[k % 2] + 16 * (k // 2 + 1))
                        t.wait_ge(dl0 if k % 2 == 0 else dl1,
                                  ldc[k % 2] + 32 * (k // 2 + 1))
                        for j in range(2):
                            for mq in range(3):
                                mm = t.matmul(
                                    bank(j * 3 + mq),
                                    wbufs[k % 2][:, 2048 + j * 128:
                                                 2048 + (j + 1) * 128],
                                    wbufs[k % 2][:, mq * 512:(mq + 1) * 512],
                                    start=(k == 0), stop=(k == 31))
                        mm.then_inc(td, 1)
                S.add(td, 32)
                td_q = S.get(td)

                @block.vector
                def _(v, grp=grp, td_q=td_q):
                    v = VD(v)
                    v.wait_ge(td, td_q)
                    for j in range(2):
                        tc = grp * 2 + j
                        v.tensor_copy(sbB[:, j * 2048:j * 2048 + 512],
                                      bank(j * 3 + 0))
                        v.tensor_copy(sbB[:, j * 2048 + 512:j * 2048 + 1024],
                                      bank(j * 3 + 1))
                        qr3 = sbB[:, j * 2048 + 1024:j * 2048 + 1536]\
                            .rearrange("p (h r) -> p h r", r=32)
                        tt = v.tensor_tensor(
                            out=qr3,
                            in0=bank(j * 3 + 2).rearrange("p (h r) -> p h r", r=32),
                            in1=sct[:, tc * 32:(tc + 1) * 32].unsqueeze(1)
                            .broadcast_to([128, 16, 32]),
                            op=OP.mult)
                        if j == 1:
                            tt.then_inc(vd, 1)
                S.add(vd, 1)
                vd_q = S.get(vd)

                @block.gpsimd
                def _(g, grp=grp, vd_q=vd_q):
                    g.wait_ge(vd, vd_q)
                    for j in range(2):
                        tc = grp * 2 + j
                        for j2 in range(8):
                            base = j2 * 2048 + tc * 128
                            g.dma_start(
                                out=a2aq_in[base:base + 128, 0:64],
                                in_=sbB[:, j * 2048 + j2 * 64:
                                        j * 2048 + (j2 + 1) * 64]).then_inc(st, 16)
                            g.dma_start(
                                out=a2aq_in[base:base + 128, 64:96],
                                in_=sbB[:, j * 2048 + 1024 + j2 * 32:
                                        j * 2048 + 1024 + (j2 + 1) * 32]
                                ).then_inc(st, 16)
                            g.dma_start(
                                out=a2aq_in[base:base + 128, 96:160],
                                in_=sbB[:, j * 2048 + (j2 + 8) * 64:
                                        j * 2048 + (j2 + 9) * 64]).then_inc(st, 16)
                            g.dma_start(
                                out=a2aq_in[base:base + 128, 160:192],
                                in_=sbB[:, j * 2048 + 1024 + (j2 + 8) * 32:
                                        j * 2048 + 1024 + (j2 + 9) * 32]
                                ).then_inc(st, 16)
                S.add(st, 16 * 64)

            st_c = S.get(st)

            @block.gpsimd
            def _(g):
                g.wait_ge(st, st_c)
                g.collective_compute(
                    "AllGather", OP.bypass,
                    replica_groups=[list(range(NCORES))],
                    ins=[agkv_in.ap().opt()],
                    outs=[agkv_out.ap().opt()],
                ).then_inc(cc, 1)
                with allow_ici_all_to_all():
                    g.collective_compute(
                        "AllToAll", OP.bypass,
                        replica_groups=[list(range(NCORES))],
                        ins=[a2aq_in.ap().opt()],
                        outs=[a2aq_out.ap().opt()],
                    ).then_inc(cc, 1)
            cc_l1 = S.add(cc, 2)

            ldw_b = S.get(ld)

            @block.sync
            def _(s):
                s.wait_ge(cc, cc_l1)
                s.dma_start(out=sbC[:, 6656:6912], in_=wukv_cb[0:128, :]
                            ).then_inc(ld, 16)
                s.dma_start(out=sbC[:, 6912:7168], in_=wukv_cb[128:256, :]
                            ).then_inc(ld, 16)
            ld_wukv = S.add(ld, 32)

            for bb in range(4):
                bvd = S.get(vd)

                @block.vector
                def _(v, bb=bb, bvd=bvd):
                    v = VD(v)
                    v.wait_ge(ld, ld_wukv)
                    v.wait_ge(td, S.get(td))
                    v.memset(cols[:, 43:44], 0.0).then_inc(vd, 1)
                S.add(vd, 1)

                kv_vd_base = S.get(vd)
                for kb in range(32):
                    ld_pre2 = S.get(dl2)

                    @block.sync
                    def _(s, bb=bb, kb=kb):
                        if kb >= 2:
                            s.wait_ge(vd, kv_vd_base + kb - 1)
                        s.dma_start(
                            out=sbEb2[kb % 2][:],
                            in_=agkv_out[bb * 4096 + kb * 128:
                                         bb * 4096 + (kb + 1) * 128, :]
                            ).then_inc(dl2, 16)
                    S.add(dl2, 16)

                    @block.vector
                    def _(v, kb=kb, ld_pre2=ld_pre2):
                        v = VD(v)
                        v.wait_ge(dl2, ld_pre2 + 16)
                        v.tensor_copy(sbE[:, 512:800], sbEb2[kb % 2][:]
                                      ).then_inc(vd, 1)
                    S.add(vd, 1)
                    vq3 = S.get(vd)

                    emit_transposes(sbE[:, 512:768],
                                    lambda f, kb=kb: sbB[:, f * 4096 + kb * 128:
                                                         f * 4096 + (kb + 1) * 128],
                                    2, extra_tensor_wait=(vd, vq3))
                    t_b3 = S.get(td)
                    s_b3 = S.get(sd)

                    @block.tensor
                    def _(t, kb=kb, s_b3=s_b3):
                        t.wait_ge(sd, s_b3)
                        t.transpose(out=pb(0)[0:32, :],
                                    in_=sbE[:, 768:800],
                                    identity=idf[:]).then_inc(td, 1)
                    S.add(td, 1)

                    @block.scalar
                    def _(sc, kb=kb, t_b3=t_b3):
                        sc = SD(sc)
                        sc.wait_ge(td, t_b3 + 1)
                        sc.copy(out=sbC[0:32, kb * 128:(kb + 1) * 128],
                                in_=pb(0)[0:32, :]).then_inc(sd, 1)
                    S.add(sd, 1)

                for hh in range(2):
                    p = bb * 2 + hh
                    kv_td = S.get(td)
                    kv_vd = S.get(vd)
                    kv_sd = S.get(sd)

                    @block.tensor
                    def _(t, hh=hh, kv_sd=kv_sd, kv_vd=kv_vd):
                        t.wait_ge(sd, kv_sd)
                        t.wait_ge(vd, kv_vd)
                        for mb in range(8):
                            if mb >= 1:
                                t.wait_ge(sd, kv_sd + mb)
                            for c2 in range(2):
                                mm = t.matmul(
                                    bank(0)[0:64, :],
                                    sbC[:, 6656 + c2 * 256 + hh * 64:
                                        6656 + c2 * 256 + (hh + 1) * 64],
                                    sbB[:, c2 * 4096 + mb * 512:
                                        c2 * 4096 + (mb + 1) * 512],
                                    start=(c2 == 0), stop=(c2 == 1))
                            mm.then_inc(td, 1)
                        for kb in range(32):
                            if kb >= 1:
                                t.wait_ge(sd, kv_sd + 8 + kb)
                            for c2 in range(2):
                                mm = t.matmul(
                                    bank(1)[:, 0:64],
                                    sbB[:, c2 * 4096 + kb * 128:
                                        c2 * 4096 + (kb + 1) * 128],
                                    sbC[:, 6656 + c2 * 256 + 128 + hh * 64:
                                        6656 + c2 * 256 + 128 + (hh + 1) * 64],
                                    start=(c2 == 0), stop=(c2 == 1))
                            mm.then_inc(td, 1)
                    S.add(td, 40)
                    kv2_td = S.get(td)

                    @block.scalar
                    def _(sc, kv_td=kv_td):
                        sc = SD(sc)
                        for mb in range(8):
                            sc.wait_ge(td, kv_td + mb + 1)
                            sc.copy(out=sbA[0:64, mb * 512:(mb + 1) * 512],
                                    in_=bank(0)[0:64, :]).then_inc(sd, 1)
                        for kb in range(32):
                            sc.wait_ge(td, kv_td + 8 + kb + 1)
                            sc.copy(out=sbC[:, 4096 + kb * 65:4096 + kb * 65 + 64],
                                    in_=bank(1)[:, 0:64]).then_inc(sd, 1)
                    S.add(sd, 40)
                    kv2_sd = S.get(sd)


                    @block.vector
                    def _(v, hh=hh, p=p, kv2_sd=kv2_sd):
                        v = VD(v)
                        v.wait_ge(sd, kv2_sd)
                        v.memset(sbC[:, 4096:6176]
                                 .rearrange("p (k o) -> p k o", o=65)[:, :, 64:65],
                                 1.0).then_inc(vd, 1)
                    S.add(vd, 1)

                    @block.gpsimd
                    def _(g, kv2_sd=kv2_sd):
                        g.wait_ge(sd, kv2_sd)
                        g.dma_start(out=sbA[64:96, 0:4096],
                                    in_=sbC[:, 0:4096][0:32, :]).then_inc(gd, 16)
                    S.add(gd, 16)

                    q_vd_base = S.get(vd)
                    for qb in range(32):
                        ld_q = S.get(dl3)
                        r = 2 * bb + qb // 16
                        loc = (qb % 16) * 128

                        @block.sync
                        def _(s, qb=qb, r=r, loc=loc, hh=hh):
                            if qb >= 2:
                                s.wait_ge(vd, q_vd_base + qb - 1)
                            s.dma_start(out=sbEb2[qb % 2][:, 0:96],
                                        in_=a2aq_out[r * 2048 + loc:
                                                     r * 2048 + loc + 128,
                                                     hh * 96:(hh + 1) * 96]
                                        ).then_inc(dl3, 16)
                        S.add(dl3, 16)

                        @block.vector
                        def _(v, qb=qb, ld_q=ld_q):
                            v = VD(v)
                            v.wait_ge(dl3, ld_q + 16)
                            v.tensor_copy(sbE[:, 512:608], sbEb2[qb % 2][:, 0:96]
                                          ).then_inc(vd, 1)
                        S.add(vd, 1)
                        vq4 = S.get(vd)

                        @block.scalar
                        def _(sc, qb=qb, vq4=vq4):
                            sc = SD(sc)
                            sc.wait_ge(vd, vq4)
                            sc.activation(out=sbE[:, 608:704], in_=sbE[:, 512:608],
                                          func=AF.Square,
                                          accum_out=sbE[:, 704 + qb:705 + qb]
                                          ).then_inc(sd, 1)
                        S.add(sd, 1)
                        sq_sd = S.get(sd)

                        t_b4 = S.get(td)

                        @block.tensor
                        def _(t, qb=qb, sq_sd=sq_sd):
                            t.wait_ge(sd, sq_sd)
                            t.transpose(out=pb(qb % 2)[0:96, :],
                                        in_=sbE[:, 512:608],
                                        identity=idf[:]).then_inc(td, 1)
                        S.add(td, 1)

                        @block.scalar
                        def _(sc, qb=qb, t_b4=t_b4):
                            sc = SD(sc)
                            sc.wait_ge(td, t_b4 + 1)
                            sc.copy(out=sbA[0:96, 4096 + qb * 128:
                                            4096 + (qb + 1) * 128],
                                    in_=pb(qb % 2)[0:96, :]).then_inc(sd, 1)
                        S.add(sd, 1)

                    bb_vd = S.get(vd)
                    bb_td = S.get(td)
                    bb_sd = S.get(sd)

                    @block.vector
                    def _(v, bb_sd=bb_sd, gd_kr=S.get(gd)):
                        v = VD(v)
                        v.wait_ge(sd, bb_sd)
                        v.wait_ge(gd, gd_kr)
                        v.tensor_reduce(out=cols[:, 44:45], in_=sbE[:, 704:736],
                                        axis=AX.X, op=OP.max)
                        v.tensor_reduce(out=cols[0:96, 45:46], in_=sbA[0:96, 0:4096],
                                        axis=AX.X, op=OP.max,
                                        apply_absolute_value=True).then_inc(vd, 1)
                    S.add(vd, 1)
                    bv1 = S.get(vd)

                    @block.tensor
                    def _(t, bv1=bv1):
                        t.wait_ge(vd, bv1)
                        t.transpose(out=pb(0)[0:1, :], in_=cols[:, 44:45],
                                    identity=idf[:])
                        t.transpose(out=pb(1)[0:1, 0:96], in_=cols[0:96, 45:46],
                                    identity=idf[0:96, 0:96]).then_inc(td, 1)
                    S.add(td, 1)
                    bt1 = S.get(td)

                    @block.vector
                    def _(v, bt1=bt1):
                        v = VD(v)
                        v.wait_ge(td, bt1)
                        v.tensor_reduce(out=cols[0:1, 46:47], in_=pb(0)[0:1, :],
                                        axis=AX.X, op=OP.max)
                        v.tensor_reduce(out=cols[0:1, 47:48], in_=pb(1)[0:1, 0:96],
                                        axis=AX.X, op=OP.max).then_inc(vd, 1)
                    S.add(vd, 1)
                    bv2 = S.get(vd)

                    @block.scalar
                    def _(sc, bv2=bv2):
                        sc = SD(sc)
                        sc.wait_ge(vd, bv2)
                        sc.activation(out=cols[0:1, 46:47], in_=cols[0:1, 46:47],
                                      func=AF.Sqrt).then_inc(sd, 1)
                    S.add(sd, 1)
                    bs1 = S.get(sd)

                    @block.vector
                    def _(v, bs1=bs1):
                        v = VD(v)
                        v.wait_ge(sd, bs1)
                        v.tensor_tensor(out=cols[0:1, 48:49], in0=cols[0:1, 46:47],
                                        in1=cols[0:1, 47:48], op=OP.mult)
                        v.tensor_scalar(out=cols[0:1, 48:49], in0=cols[0:1, 48:49],
                                        scalar1=-1.0, scalar2=0.0,
                                        op0=OP.mult, op1=OP.bypass).then_inc(vd, 1)
                    S.add(vd, 1)
                    bv3 = S.get(vd)

                    @block.gpsimd
                    def _(g, bv3=bv3):
                        g.wait_ge(vd, bv3)
                        g.dma_start(out=cols[:, 49:50],
                                    in_=cols[0:1, 48:49].unsqueeze(1)
                                    .broadcast_to([1, 128, 1])).then_inc(gd, 16)
                    S.add(gd, 16)
                    gd_b = S.get(gd)

                    for tq in range(8):
                        nkb = 4 * tq + 4
                        t0 = S.get(td)
                        s0 = S.get(sd)
                        v0 = S.get(vd)

                        @block.tensor
                        def _(t, tq=tq, nkb=nkb, t0=t0, s0=s0, v0=v0,
                              gd_b=gd_b):
                            if tq == 0:
                                t.wait_ge(gd, gd_b)
                            for kb in range(nkb):
                                if kb >= 2:
                                    t.wait_ge(sd, s0 + kb - 1)
                                t.matmul(bank(4 + kb % 2),
                                         sbA[0:96, kb * 128:(kb + 1) * 128],
                                         sbA[0:96, 4096 + tq * 512:
                                             4096 + (tq + 1) * 512],
                                         start=True, stop=True).then_inc(td, 1)
                                if kb >= 1:
                                    kk = kb - 1
                                    if kk >= nkb - 4:
                                        t.wait_ge(vd, v0 + 1 + (kk - (nkb - 4)) + 1)
                                    else:
                                        t.wait_ge(sd, s0 + kk + 1)
                                    t.matmul(psB[0:65, 0:512],
                                             sbC[:, 4096 + kk * 65:
                                                 4096 + (kk + 1) * 65],
                                             sbDb2[kk % 2][:],
                                             start=(kk == 0), stop=False
                                             ).then_inc(td, 1)
                            kk = nkb - 1
                            t.wait_ge(vd, v0 + 1 + 4)
                            t.matmul(psB[0:65, 0:512],
                                     sbC[:, 4096 + kk * 65:4096 + (kk + 1) * 65],
                                     sbDb2[kk % 2][:],
                                     start=False, stop=True).then_inc(td, 1)
                        S.add(td, 2 * nkb)

                        @block.scalar
                        def _(sc, tq=tq, nkb=nkb, t0=t0):
                            sc = SD(sc)
                            for kb in range(nkb):
                                sc.wait_ge(td, t0 + max(2 * kb, 1))
                                sc.activation(out=sbDb2[kb % 2][:],
                                              in_=bank(4 + kb % 2),
                                              func=AF.Exp, scale=0.1020620726,
                                              bias=cols[:, 49:50]).then_inc(sd, 1)
                        S.add(sd, nkb)

                        @block.vector
                        def _(v, tq=tq, nkb=nkb, s0=s0):
                            v = VD(v)
                            v.memset(cols[:, 50:51], 0.0).then_inc(vd, 1)
                            for dd in range(4):
                                kb = nkb - 4 + dd
                                v.wait_ge(sd, s0 + kb + 1)
                                v.tensor_tensor(out=sbDb2[kb % 2][:],
                                                in0=sbDb2[kb % 2][:],
                                                in1=msk[:, dd * 512:(dd + 1) * 512],
                                                op=OP.mult).then_inc(vd, 1)
                        S.add(vd, 5)

                        av_td = S.get(td)

                        @block.scalar
                        def _(sc, av_td=av_td):
                            sc = SD(sc)
                            sc.wait_ge(td, av_td)
                            sc.copy(out=sbE[0:65, 0:512], in_=psB[0:65, 0:512]
                                    ).then_inc(sd, 1)
                        S.add(sd, 1)
                        yv_sd = S.get(sd)

                        for sub in range(4):
                            s_b5 = S.get(sd)
                            v_b5 = S.get(vd)

                            @block.tensor
                            def _(t, sub=sub, yv_sd=yv_sd, v_b5=v_b5):
                                t.wait_ge(sd, yv_sd)
                                if sub >= 2:
                                    t.wait_ge(vd, v_b5)
                                t.transpose(out=pya(sub % 2),
                                            in_=sbE[0:65, 0:512]
                                            [0:65, sub * 128:(sub + 1) * 128],
                                            identity=idf[0:65, 0:65]
                                            ).then_inc(td, 1)
                            S.add(td, 1)
                            tv = S.get(td)

                            @block.vector
                            def _(v, sub=sub, tv=tv, tq=tq, hh=hh, bb=bb):
                                v = VD(v)
                                v.wait_ge(td, tv)
                                v.reciprocal(out=cols[:, 51:52],
                                             in_=pya(sub % 2)[:, 64:65])
                                v.tensor_scalar(out=sbEb2[2][:, sub * 64:
                                                            (sub + 1) * 64],
                                                in0=pya(sub % 2)[:, 0:64],
                                                scalar1=cols[:, 51:52], scalar2=0.0,
                                                op0=OP.mult, op1=OP.bypass
                                                ).then_inc(vd, 1)
                            S.add(vd, 1)
                            yv = S.get(vd)

                            @block.gpsimd
                            def _(g, sub=sub, tq=tq, hh=hh, bb=bb, yv=yv):
                                g.wait_ge(vd, yv)
                                gt = tq * 512 + sub * 128
                                j2 = 2 * bb + (1 if gt >= 2048 else 0)
                                loc = gt % 2048
                                g.dma_start(
                                    out=a2ay_in[j2 * 2048 + loc:
                                                j2 * 2048 + loc + 128,
                                                hh * 64:(hh + 1) * 64],
                                    in_=sbEb2[2][:, sub * 64:(sub + 1) * 64]
                                    ).then_inc(st, 16)
                            S.add(st, 16)

            st_y = S.get(st)

            @block.gpsimd
            def _(g):
                g.wait_ge(st, st_y)
                with allow_ici_all_to_all():
                    g.collective_compute(
                        "AllToAll", OP.bypass,
                        replica_groups=[list(range(NCORES))],
                        ins=[a2ay_in.ap().opt()],
                        outs=[a2ay_out.ap().opt()],
                    ).then_inc(cc, 1)
            cc_y = S.add(cc, 1)

            for tc in range(TC):
                ld_pre = S.get(ld)
                vd_pre = S.get(vd)

                @block.sync
                def _(s, tc=tc, vd_pre=vd_pre):
                    if tc == 0:
                        s.wait_ge(cc, cc_y)
                        s.wait_ge(vd, vd_pre)
                    else:
                        s.wait_ge(vd, vd_pre)
                    for r in range(8):
                        s.dma_start(out=sbB[:, r * 128:(r + 1) * 128],
                                    in_=a2ay_out[r * 2048 + tc * 128:
                                                 r * 2048 + (tc + 1) * 128, :]
                                    ).then_inc(ld, 16)
                    for m in range(2):
                        for f in range(8):
                            s.dma_start(
                                out=wb1[:, (m * 8 + f) * 512:(m * 8 + f + 1) * 512],
                                in_=wo_ab[f * 128:(f + 1) * 128,
                                          m * 512:(m + 1) * 512]).then_inc(ld, 16)
                S.add(ld, 16 * 24)
                ld_all = S.get(ld)

                @block.vector
                def _(v, ld_all=ld_all):
                    v = VD(v)
                    v.wait_ge(ld, ld_all)
                    v.tensor_copy(sbD[:, 0:1024], sbB[:, 0:1024]).then_inc(vd, 1)
                S.add(vd, 1)
                vq5 = S.get(vd)

                emit_transposes(sbD[:, 0:1024],
                                lambda f: sbA[:, f * 128:(f + 1) * 128],
                                8, extra_tensor_wait=(vd, vq5))
                sd_yt = S.get(sd)

                @block.tensor
                def _(t, sd_yt=sd_yt, vd_pre=vd_pre, tc=tc):
                    t.wait_ge(sd, sd_yt)
                    for m in range(2):
                        for f in range(8):
                            mm = t.matmul(bank(m),
                                          sbA[:, f * 128:(f + 1) * 128],
                                          wb1[:, (m * 8 + f) * 512:
                                              (m * 8 + f + 1) * 512],
                                          start=(f == 0), stop=(f == 7))
                    mm.then_inc(td, 1)
                S.add(td, 1)
                td_y = S.get(td)

                @block.vector
                def _(v, td_y=td_y, tc=tc):
                    v = VD(v)
                    v.wait_ge(td, td_y)
                    for m in range(2):
                        xr = x_res[:, tc * D + m * 512:tc * D + (m + 1) * 512]
                        tt = v.tensor_tensor(out=xr, in0=bank(m), in1=xr,
                                             op=OP.add)
                        if m == 1:
                            tt.then_inc(vd, 1)
                S.add(vd, 1)


        if N_SUBLAYERS >= 1 and DO_FFN:
            emit_l0ffn()
        if N_SUBLAYERS >= 2:
            emit_l0attn()
        if N_SUBLAYERS >= 3:
            emit_l0ffn()
        if N_SUBLAYERS >= 4:
            l1_wq(wg1, wgq1, 2816, 1024)
            l1_wq(wu1, wuq1, 2816, 1024)
            l1_wq(wd1, wdq1, 1024, 2816)
            emit_l1ffn()
        if N_SUBLAYERS >= 5:
            emit_l1attn()
        if N_SUBLAYERS >= 6:
            emit_l1ffn()

        if DBG:
            dumps = [("d_agx", agx_in, [1024, 2048]),
                     ("d_sK", scanK, [512, 4096]), ("d_sQ", scanQ, [512, 4096]),
                     ("d_sBK", scanBK, [512, 4096]), ("d_sBV", scanBV, [512, 4096]),
                     ("d_a2a", a2a0_in, [1024, 2048]),
                     ("d_a2o", a2a0_out, [1024, 2048])]
            daps = [(nc.dram_tensor(nm, shp, BF16, kind="ExternalOutput").ap(), t)
                    for nm, t, shp in dumps]
            d_vd = S.get(vd); d_td = S.get(td); d_sd = S.get(sd); d_st0 = S.get(st)

            @block.gpsimd
            def _(g):
                g.wait_ge(vd, d_vd)
                g.wait_ge(td, d_td)
                g.wait_ge(sd, d_sd)
                g.wait_ge(ld, S.get(ld))
                g.wait_ge(st, d_st0)
                for w_ap, t in daps:
                    g.dma_start(out=w_ap, in_=t.ap()).then_inc(st, 16)
                g.wait_ge(st, d_st0 + 16 * len(daps))
            S.add(st, 16 * len(daps))

        fin_vd = S.get(vd)
        fin_td = S.get(td)
        fin_sd = S.get(sd)
        st_b = S.get(st)

        @block.gpsimd
        def _(g):
            g.wait_ge(vd, fin_vd)
            g.wait_ge(td, fin_td)
            g.wait_ge(sd, fin_sd)
            g.wait_ge(ld, S.get(ld))
            for tc in range(TC):
                g.dma_start(out=y_out[tc * 128:(tc + 1) * 128, :],
                            in_=x_res[:, tc * D:(tc + 1) * D]).then_inc(st, 16)
            g.wait_ge(st, st_b + 16 * TC)
        S.add(st, 16 * TC)

    return nc


def kernel(**inputs):
    x = np.ascontiguousarray(np.asarray(inputs["L0_attn_x"], dtype=np.float32))
    xf = x.reshape(NCORES * NT, D)
    ident_f = np.eye(128, dtype=np.float32)
    wg = np.ascontiguousarray(np.asarray(inputs["L0_ffn_w_gate"], np.float32))
    wu = np.ascontiguousarray(np.asarray(inputs["L0_ffn_w_up"], np.float32))
    wd = np.ascontiguousarray(np.asarray(inputs["L0_ffn_w_down"], np.float32))

    Wq = np.asarray(inputs["L0_attn_Wq"], np.float32)
    Wk = np.asarray(inputs["L0_attn_Wk"], np.float32)
    Wv = np.asarray(inputs["L0_attn_Wv"], np.float32)
    Wbeta = np.asarray(inputs["L0_attn_Wbeta"], np.float32)
    Wg_a = np.asarray(inputs["L0_attn_Wg"], np.float32)
    Wout_a = np.asarray(inputs["L0_attn_Wout"], np.float32)
    ln_g = np.asarray(inputs["L0_attn_ln_g"], np.float32)
    ln_b = np.asarray(inputs["L0_attn_ln_b"], np.float32)

    perm = np.empty(1024, np.int64)
    for r in range(8):
        for hh in range(2):
            for dd_ in range(64):
                perm[r * 128 + hh * 64 + dd_] = (r + 8 * hh) * 64 + dd_
    lng_r = np.ascontiguousarray(np.broadcast_to(ln_g[perm][None, :], (128, 1024)),
                                 dtype=np.float32).astype(np.float32)
    lnb_r = np.ascontiguousarray(np.broadcast_to(ln_b[perm][None, :], (128, 1024)),
                                 dtype=np.float32).astype(np.float32)
    l0wg_a = np.ascontiguousarray(Wg_a[:, perm])
    l0wo_a = np.ascontiguousarray(Wout_a[perm, :])
    eselS = np.zeros((128, 4096), dtype=BF)
    for _tau in range(32):
        eselS[2 * _tau, _tau * 128:_tau * 128 + 64] = 1
        eselS[2 * _tau + 1, _tau * 128 + 64:_tau * 128 + 128] = 1
    eselS[64:128] = eselS[0:64]
    esel = np.zeros((128, 128), dtype=BF)
    esel[0:64, 62] = 1
    esel[64:128, 63] = 1

    W_DQ = np.asarray(inputs["L1_attn_W_DQ"], np.float32)
    W_UQ = np.asarray(inputs["L1_attn_W_UQ"], np.float32)
    W_DKV = np.asarray(inputs["L1_attn_W_DKV"], np.float32)
    W_UK = np.asarray(inputs["L1_attn_W_UK"], np.float32)
    W_UV = np.asarray(inputs["L1_attn_W_UV"], np.float32)
    W_QR = np.asarray(inputs["L1_attn_W_QR"], np.float32)
    W_KR = np.asarray(inputs["L1_attn_W_KR"], np.float32)
    W_O = np.asarray(inputs["L1_attn_W_O"], np.float32)
    wkv_cat = np.ascontiguousarray(np.concatenate([W_DKV, W_KR], axis=1))
    wo_perm = np.ascontiguousarray(W_O[perm, :])
    tpos = np.arange(4096, dtype=np.float64)
    inv = 1.0 / 10000.0 ** (np.arange(0, 32, 2, dtype=np.float64) / 32)
    fr = tpos[:, None] * inv[None, :]
    emb = np.concatenate([fr, fr], -1)
    scale_all = (np.cos(emb) + np.sin(emb)).astype(np.float32)
    mask4 = np.zeros((128, 4, 512), dtype=BF)
    jj = np.arange(512)[None, :]
    pp = np.arange(128)[:, None]
    for dmi in range(4):
        mask4[:, dmi, :] = (jj >= pp + 128 * dmi).astype(BF)
    mask4 = np.ascontiguousarray(mask4.reshape(128, 2048))

    nc = build_nc()
    in_maps = []
    for c in range(NCORES):
        h0, h1 = c, c + 8
        pwc = np.zeros((1024, 392), np.float32)
        pwc[:, 0:64] = Wq[:, h0 * 64:(h0 + 1) * 64]
        pwc[:, 64:128] = Wq[:, h1 * 64:(h1 + 1) * 64]
        pwc[:, 128:192] = Wk[:, h0 * 64:(h0 + 1) * 64]
        pwc[:, 192:256] = Wk[:, h1 * 64:(h1 + 1) * 64]
        pwc[:, 256:320] = Wv[:, h0 * 64:(h0 + 1) * 64]
        pwc[:, 320:384] = Wv[:, h1 * 64:(h1 + 1) * 64]
        pwc[:, 384] = Wbeta[:, h0]
        pwc[:, 385] = Wbeta[:, h1]
        in_maps.append({
            "x": np.ascontiguousarray(xf[c * NT:(c + 1) * NT]),
            "ident_f": ident_f,
            "l0wg": wg, "l0wu": wu, "l0wd": wd,
            "l0pw": pwc, "l0wg_a": l0wg_a, "l0wo_a": l0wo_a,
            "lng_r": lng_r, "lnb_r": lnb_r, "esel": esel, "eselS": eselS,
            "l1wg": np.ascontiguousarray(np.asarray(inputs["L1_ffn_w_gate"], np.float32)),
            "l1wu": np.ascontiguousarray(np.asarray(inputs["L1_ffn_w_up"], np.float32)),
            "l1wd": np.ascontiguousarray(np.asarray(inputs["L1_ffn_w_down"], np.float32)),
            "l1wdq": W_DQ, "l1wuq": W_UQ, "l1wqr": W_QR,
            "l1wkv": wkv_cat,
            "l1wukv": np.ascontiguousarray(np.concatenate(
                [W_UK[:, h0 * 64:(h0 + 1) * 64], W_UK[:, h1 * 64:(h1 + 1) * 64],
                 W_UV[:, h0 * 64:(h0 + 1) * 64], W_UV[:, h1 * 64:(h1 + 1) * 64]],
                axis=1)),
            "l1wo": wo_perm,
            "scal_t": np.ascontiguousarray(
                scale_all[2048 * (c % 2):2048 * (c % 2) + 2048].reshape(
                    16, 128, 32).transpose(1, 0, 2).reshape(128, 512)),
            "mask4": mask4,
        })
    res = run_bass_kernel_spmd(nc, in_maps, core_ids=list(range(NCORES)))
    y = np.concatenate([res.results[c]["y"] for c in range(NCORES)], axis=0)
    return y.reshape(4, 4096, D).astype(np.float32)
